# version 04
# baseline (speedup 1.0000x reference)
import sys
import numpy as np
from contextlib import ExitStack

sys.path.insert(0, "/opt/trn_rl_repo")
import concourse.bass as bass
import concourse.mybir as mybir
from concourse.bass_utils import run_bass_kernel_spmd

try:
    import jax
    jax.config.update("jax_compilation_cache_dir", "/tmp/jax_cc_cache")
    jax.config.update("jax_persistent_cache_min_compile_time_secs", 10.0)
except Exception:
    pass

F32 = mybir.dt.float32
BF16 = mybir.dt.bfloat16
AF = mybir.ActivationFunctionType
ALU = mybir.AluOpType

D = 1024
H = 8
CH = 128
SESS_T = 512
NCHUNK = 4
TOK_CORE = 4096
L = 8192
NDC = 8
EPS = 1e-5
BETA_MIN = 0.85
BETA_SPAN = 0.999 - 0.85
NKC = 128


def build_nc(n_sess_main=8, n_sess_pa=8, taps=()):
    nc = bass.Bass("TRN2", target_bir_lowering=False)

    d_in = {}
    def di(name, shape):
        d_in[name] = nc.dram_tensor(name, list(shape), F32, kind="ExternalInput")
        return d_in[name]

    xT_d = di("xT", (D, TOK_CORE))
    xTp_d = di("xTp", (D, TOK_CORE))
    di("wqT", (D, D)); di("wkT", (D, D)); di("wvT", (D, D)); di("woT", (D, D))
    di("wbT", (D, H)); di("waT", (D, H))
    bb_d = di("bb", (H, 1)); ba_d = di("ba", (H, 1))
    bo_d = di("bo_cols", (128, NDC))
    wc_d = di("wcT_tiles", (NKC, 128, D))
    ropec_d = di("ropeC", (L, 64))
    ropes_d = di("ropeS", (L, 128))
    railA_d = di("railA", (128, 64 * 32))
    railB_d = di("railB", (128, 64 * 128))
    maskT_d = di("maskT", (128, 128))
    idf_d = di("idf", (128, 128))
    flag_d = di("flag", (128, 1))
    epsc_d = di("epsc", (128, 1))

    out_d = nc.dram_tensor("outT", [D, TOK_CORE], F32, kind="ExternalOutput")
    wcbf_d = nc.dram_tensor("wc_bf", [NKC, 128, D], BF16, kind="Internal")

    tap_d = {}
    def mktap(name, shape, dtype=F32):
        if name in taps:
            tap_d[name] = nc.dram_tensor("tap_" + name, list(shape), dtype,
                                         kind="ExternalOutput")
    mktap("xn", (128, NDC * SESS_T), BF16)
    mktap("qt", (128, H * 128), BF16)
    mktap("kt", (128, H * 128), BF16)
    mktap("vt", (128, H * 128), BF16)
    mktap("at", (128, H * 128), BF16)
    mktap("scale", (128, 128))
    mktap("s0", (128, H * 128))
    mktap("ot", (128, H * SESS_T), BF16)
    mktap("x1", (128, NDC * SESS_T))
    mktap("x1n", (128, NDC * SESS_T), BF16)
    mktap("hc", (128, 256), BF16)
    mktap("sa", (128, H * 128))

    ctx = ExitStack()
    E = ctx.enter_context
    def S(name, shape, dtype):
        return E(nc.sbuf_tensor(name, list(shape), dtype))

    wq = S("wq", (128, NDC * D), BF16)
    wk = S("wk", (128, NDC * D), BF16)
    wv = S("wv", (128, NDC * D), BF16)
    wo = S("wo", (128, NDC * D), BF16)
    wb = S("wb", (128, NDC * H), BF16)
    wa = S("wa", (128, NDC * H), BF16)
    bb = S("bb_s", (H, 1), F32)
    ba = S("ba_s", (H, 1), F32)
    bo = S("bo_s", (128, NDC), F32)
    maskT = S("maskT_s", (128, 128), F32)
    idf = S("idf_s", (128, 128), F32)
    idb = S("idb", (128, 128), BF16)
    flag = S("flag_s", (128, 1), F32)
    epsc = S("epsc_s", (128, 1), F32)
    ones_bf = S("ones_bf", (128, 128), BF16)
    ones8 = S("ones8", (H, CH), F32)
    zero8 = S("zero8", (H, CH), F32)
    wblk = S("wblk", (128, 4096), BF16)

    xT = S("xT_s", (128, NDC * SESS_T), F32)
    xbf = S("xbf", (128, 2 * SESS_T), BF16)
    sq = S("sq", (128, 2 * SESS_T), BF16)
    mB = S("mB", (128, SESS_T), F32)
    msqB = S("msqB", (128, SESS_T), F32)
    rstdB = S("rstdB", (128, SESS_T), F32)
    tmpB = S("tmpB", (128, SESS_T), F32)
    xnT = S("xnT", (128, NDC * SESS_T), BF16)
    beta = S("beta", (H, SESS_T), F32)
    alpha = S("alpha", (H, SESS_T), F32)
    Pc = S("Pc", (H, SESS_T), F32)
    rPc = S("rPc", (H, SESS_T), F32)
    aoP = S("aoP", (H, SESS_T), F32)
    pendR = S("pendR", (H, SESS_T), F32)
    scaleT = S("scaleT", (128, NCHUNK * 32), F32)
    cbuf = S("cbuf", (128, NCHUNK * 64), F32)
    ssb = S("ssb", (128, NCHUNK * 128), F32)
    qbf = S("qbf", (128, D), BF16)
    kbf = S("kbf", (128, D), BF16)
    qt1 = S("qt1", (128, D), BF16)
    qt2 = S("qt2", (128, D), BF16)
    kt1 = S("kt1", (128, D), BF16)
    kt2 = S("kt2", (128, D), BF16)
    qtok = S("qtok", (128, D), BF16)
    khat = S("khat", (128, D), BF16)
    vt_sb = S("vt_sb", (128, D), BF16)
    qT = S("qT", (128, D), BF16)
    kT = S("kT", (128, D), BF16)
    A_sb = S("A_sb", (128, H * 128), BF16)
    OT = S("OT", (128, H * SESS_T), BF16)
    S_sb = S("S_sb", (128, H * 128), F32)
    Sbf = S("Sbf", (128, H * 128), BF16)
    x1T = S("x1T", (128, NDC * SESS_T), F32)
    x1n = S("x1n", (128, NDC * SESS_T), BF16)
    hbuf = S("hbuf", (128, 8 * 256), BF16)
    wcb = S("wcb", (128, 4 * D), BF16)
    ffn_sb = S("ffn_sb", (128, D), F32)
    out_sb = ffn_sb

    PS = [E(nc.psum_tensor(f"PS{i}", (128, 512), F32)) for i in range(6)]
    PB6 = E(nc.psum_tensor("PB6", (128, 1024), BF16))
    PB7 = E(nc.psum_tensor("PB7", (128, 1024), BF16))

    sem_names = ["LD", "WC", "TE", "VE", "SE", "GP", "ST"]
    SEM = {n: E(nc.semaphore("s" + n)) for n in sem_names}

    blk = E(nc.Block())

    def sched(A):
        cnt = {n: 0 for n in sem_names}
        wmax = {n: 0 for n in sem_names}
        last = [None]
        eng = {"sy": nc.sync, "gp": nc.gpsimd, "te": nc.tensor,
               "ve": nc.vector, "se": nc.scalar}

        def fin(e, semn, k=1):
            cnt[semn] += k
            if A == e:
                assert last[0] is not None, (e, semn)
                last[0].then_inc(SEM[semn], k)

        def wt(e, semn, val=None):
            v = cnt[semn] if val is None else val
            wmax[semn] = max(wmax[semn], v)
            if A == e and v > 0:
                eng[e].wait_ge(SEM[semn], v)

        def on(e):
            return A == e

        def put(i):
            last[0] = i
            return i

        def dma(dst, src, semn="LD"):
            if on("sy"):
                put(nc.sync.dma_start(dst, src))
            fin("sy", semn, 16)

        def gdma(dst, src, semn="LD"):
            if on("gp"):
                put(nc.gpsimd.dma_start(dst, src))
            fin("gp", semn, 16)

        def mm(out, lhsT, rhs, start=True, stop=True, tile_position=None):
            if on("te"):
                kw = {}
                if tile_position is not None:
                    kw["tile_position"] = tile_position
                put(nc.tensor.matmul(out, lhsT, rhs, start=start, stop=stop, **kw))

        def tr(out, in_, ident):
            if on("te"):
                put(nc.tensor.transpose(out, in_, ident))

        def ve(f):
            if on("ve"):
                put(f(nc.vector))

        def gp(f):
            if on("gp"):
                put(f(nc.gpsimd))

        def se(f):
            if on("se"):
                put(f(nc.scalar))

        def ap_c(c):
            return bass.AP(cbuf, c * 64, [[NCHUNK * 64, 128], [0, H], [1, 64], [0, 2]])

        def ap_s(c):
            return bass.AP(ssb, c * 128, [[NCHUNK * 128, 128], [0, H], [1, 128]])

        def ap_swap(t):
            return bass.AP(t, 1, [[1024, 128], [128, H], [2, 64], [-1, 2]])

        for name, dst in (("wqT", wq), ("wkT", wk), ("wvT", wv), ("woT", wo)):
            for d in range(NDC):
                gdma(dst[:, d * D:(d + 1) * D],
                     d_in[name].ap()[d * 128:(d + 1) * 128, :])
        for name, dst in (("wbT", wb), ("waT", wa)):
            for d in range(NDC):
                gdma(dst[:, d * H:(d + 1) * H],
                     d_in[name].ap()[d * 128:(d + 1) * 128, :])
        dma(bb[:], bb_d.ap()); dma(ba[:], ba_d.ap()); dma(bo[:], bo_d.ap())
        dma(maskT[:], maskT_d.ap()); dma(idf[:], idf_d.ap())
        dma(flag[:], flag_d.ap()); dma(epsc[:], epsc_d.ap())
        gdma(idb[:], idf_d.ap())
        rav = xT.ap().bitcast(BF16)
        rbv = x1T.ap().bitcast(BF16)
        gdma(rav[:, 0:2048], railA_d.ap())
        gdma(rbv[:, 0:8192], railB_d.ap())
        for i in range(16):
            gdma(wcbf_d.ap()[i * 8:(i + 1) * 8], wc_d.ap()[i * 8:(i + 1) * 8], "WC")
        if on("gp"):
            put(nc.gpsimd.memset(ones8[:], 1.0))
            put(nc.gpsimd.memset(zero8[:], 0.0))
            put(nc.gpsimd.memset(S_sb[:], 0.0))
            put(nc.gpsimd.memset(Sbf[:], 0.0))
        fin("gp", "GP", 1)
        if on("ve"):
            put(nc.vector.memset(ones_bf[:], 1.0))
        fin("ve", "VE", 1)

        wt("te", "GP")
        wt("te", "LD")
        for e2 in range(128):
            h_, p_ = divmod(e2, 32)
            strip = 32 * (p_ % 4)
            fb = (h_ * 8 + p_ // 4) * 128
            pst = PS[e2 % 2]
            for rl in range(2):
                r3 = h_ * 16 + (p_ // 4) * 2 + rl
                mm(pst[strip:strip + 32, 0:128],
                   rav[strip:strip + 16, r3 * 32:(r3 + 1) * 32],
                   rbv[strip:strip + 16, r3 * 128:(r3 + 1) * 128],
                   start=(rl == 0), stop=(rl == 1),
                   tile_position=(strip, strip))
            fin("te", "TE", 1)
            wt("se", "TE")
            se(lambda s_, strip=strip, fb=fb, pst=pst:
               s_.copy(wblk[strip:strip + 32, fb:fb + 128],
                       pst[strip:strip + 32, 0:128]))
            fin("se", "SE", 1)
            if e2 >= 1:
                wt("te", "SE", cnt["SE"] - 1)

        def ln_pass(src, dst_bf):
            wt("se", "VE")
            wt("ve", "SE")
            wt("gp", "VE")
            wt("gp", "SE")
            for d in range(NDC):
                pp = (d % 2) * SESS_T
                blkc = slice(d * SESS_T, (d + 1) * SESS_T)
                if d >= 2:
                    wt("se", "TE")
                    wt("ve", "TE")
                    wt("gp", "TE")
                if d % 2 == 0:
                    se(lambda s_, pp=pp, blkc=blkc:
                       s_.copy(xbf[:, pp:pp + SESS_T], src[:, blkc]))
                    fin("se", "SE", 1)
                else:
                    gp(lambda g_, pp=pp, blkc=blkc:
                       g_.tensor_copy(xbf[:, pp:pp + SESS_T], src[:, blkc]))
                    fin("gp", "GP", 1)
                ve(lambda v, pp=pp, blkc=blkc:
                   v.tensor_tensor(sq[:, pp:pp + SESS_T], src[:, blkc], src[:, blkc],
                                   ALU.mult))
                fin("ve", "VE", 1)
                wt("te", "SE")
                wt("te", "GP")
                wt("te", "VE")
                mm(PS[0][:], ones_bf[:], xbf[:, pp:pp + SESS_T],
                   start=(d == 0), stop=(d == NDC - 1))
                mm(PS[1][:], ones_bf[:], sq[:, pp:pp + SESS_T],
                   start=(d == 0), stop=(d == NDC - 1))
                fin("te", "TE", 1)
            for _w in range(3):
                mm(PS[2][:], ones_bf[:], xbf[:, 0:512])
            fin("te", "TE", 1)
            wt("ve", "TE")
            ve(lambda v: v.tensor_scalar(mB[:], PS[0][:], 1.0 / D, None, ALU.mult))
            ve(lambda v: v.tensor_mul(msqB[:], mB[:], mB[:]))
            ve(lambda v: v.scalar_tensor_tensor(msqB[:], PS[1][:], 1.0 / D, msqB[:],
                                                ALU.mult, ALU.subtract))
            fin("ve", "VE", 1)
            wt("se", "VE")
            se(lambda s_: s_.activation(tmpB[:], msqB[:], AF.Sqrt, bias=epsc[:]))
            fin("se", "SE", 1)
            wt("ve", "SE")
            wt("gp", "SE")
            ve(lambda v: v.reciprocal(rstdB[:], tmpB[:]))
            fin("ve", "VE", 1)
            wt("gp", "VE")
            for d in range(NDC):
                blkc = slice(d * SESS_T, (d + 1) * SESS_T)
                if d % 2 == 0:
                    ve(lambda v, blkc=blkc: v.scalar_tensor_tensor(
                        dst_bf[:, blkc], src[:, blkc], 1.0, mB[:],
                        ALU.mult, ALU.subtract))
                    ve(lambda v, blkc=blkc: v.tensor_mul(dst_bf[:, blkc],
                                                         dst_bf[:, blkc], rstdB[:]))
                else:
                    gp(lambda g_, blkc=blkc: g_.tensor_sub(
                        dst_bf[:, blkc], src[:, blkc], mB[:]))
                    gp(lambda g_, blkc=blkc: g_.tensor_mul(dst_bf[:, blkc],
                                                           dst_bf[:, blkc], rstdB[:]))
            fin("ve", "VE", 1)
            fin("gp", "GP", 1)
            return (cnt["VE"], cnt["GP"])

        def chunk(phase, s, c, g):
            main = phase == "M"
            sc0 = c * 32
            xoff = c * CH

            def proj(wsb, p0, p1):
                for half2 in range(2):
                    pdst = p0 if half2 == 0 else p1
                    for d in range(NDC):
                        lhs = xnT[:, d * SESS_T + xoff: d * SESS_T + xoff + CH]
                        mm(pdst[:], lhs,
                           wsb[:, d * D + half2 * 512: d * D + half2 * 512 + 512],
                           start=(d == 0), stop=(d == NDC - 1))

            if main:
                wt("te", "VE", g.get("ve_ps01", 0))
                wt("te", "VE", g.get("ve_scaleT", 0))
                proj(wq, PS[0], PS[1])
                fin("te", "TE", 1)
            te_q = cnt["TE"]
            wt("te", "VE", g.get("ve_ps23", 0))
            wt("te", "VE", g.get("ve_scaleT", 0))
            proj(wk, PS[2], PS[3])
            fin("te", "TE", 1)
            te_k = cnt["TE"]
            wt("te", "SE", g.get("se_ps45", 0))
            proj(wv, PS[4], PS[5])
            fin("te", "TE", 1)
            te_v = cnt["TE"]

            if main:
                wt("se", "TE", te_q)
                se(lambda s_: s_.copy(qbf[:, 0:512], PS[0][:]))
                se(lambda s_: s_.copy(qbf[:, 512:1024], PS[1][:]))
                fin("se", "SE", 1)
            se_qbf = cnt["SE"]
            wt("se", "TE", te_k)
            se(lambda s_: s_.copy(kbf[:, 0:512], PS[2][:]))
            se(lambda s_: s_.copy(kbf[:, 512:1024], PS[3][:]))
            fin("se", "SE", 1)
            se_kbf = cnt["SE"]

            gp_q = cnt["GP"]
            if main:
                wt("gp", "SE", se_qbf)
                gp(lambda g_: g_.tensor_mul(qt1[:], qbf[:], ap_c(c)))
                gp(lambda g_: g_.tensor_mul(qt2[:], ap_swap(qbf), ap_s(c)))
                gp(lambda g_: g_.tensor_add(qbf[:], qt1[:], qt2[:]))
                for h in range(H):
                    gp(lambda g_, h=h: g_.tensor_scalar(
                        qtok[:, h * 128:(h + 1) * 128], qbf[:, h * 128:(h + 1) * 128],
                        scaleT[:, sc0 + h: sc0 + h + 1], None, ALU.mult))
                fin("gp", "GP", 1)
                gp_q = cnt["GP"]

            wt("ve", "SE", se_kbf)
            ve(lambda v: v.tensor_mul(kt1[:], kbf[:], ap_c(c)))
            ve(lambda v: v.tensor_mul(kt2[:], ap_swap(kbf), ap_s(c)))
            ve(lambda v: v.tensor_add(kbf[:], kt1[:], kt2[:]))
            fin("ve", "VE", 1)
            for h in range(H):
                ve(lambda v, h=h: v.tensor_scalar(
                    khat[:, h * 128:(h + 1) * 128], kbf[:, h * 128:(h + 1) * 128],
                    scaleT[:, sc0 + 16 + h: sc0 + 16 + h + 1], None, ALU.mult))
            fin("ve", "VE", 1)
            ve_khat = cnt["VE"]

            wt("se", "TE", te_v)
            for h in range(H):
                psrc = PS[4] if h < 4 else PS[5]
                se(lambda s_, h=h, psrc=psrc: s_.activation(
                    vt_sb[:, h * 128:(h + 1) * 128],
                    psrc[:, (h % 4) * 128:(h % 4 + 1) * 128],
                    AF.Copy, scale=scaleT[:, sc0 + 8 + h: sc0 + 8 + h + 1]))
            fin("se", "SE", 1)
            se_vt = cnt["SE"]

            if not main:
                wt("te", "VE", ve_khat)
                wt("te", "SE", se_vt)
                te_u = {}
                for h in range(H):
                    pu = PS[2 + h // 4]
                    qtr = (h % 4) * 128
                    mm(pu[:, qtr:qtr + 128], khat[:, h * 128:(h + 1) * 128],
                       vt_sb[:, h * 128:(h + 1) * 128])
                    fin("te", "TE", 1)
                    te_u[h] = cnt["TE"]
                for h in range(H):
                    pu = PS[2 + h // 4]
                    qtr = (h % 4) * 128
                    wt("ve", "TE", te_u[3 if h < 4 else 7])
                    ve(lambda v, h=h, pu=pu, qtr=qtr: v.scalar_tensor_tensor(
                        S_sb[:, h * 128:(h + 1) * 128], S_sb[:, h * 128:(h + 1) * 128],
                        scaleT[:, sc0 + 16 + h: sc0 + 16 + h + 1],
                        pu[:, qtr:qtr + 128], ALU.mult, ALU.add))
                    fin("ve", "VE", 1)
                return {"ve_ps23": cnt["VE"], "ve_rope": ve_khat, "gp_rope": gp_q,
                        "se_ps45": se_vt, "ve_scaleT": g.get("ve_scaleT", 0),
                        "ve_ps01": 0}

            wt("te", "GP", gp_q)
            wt("te", "VE", ve_khat)
            wt("te", "SE", g.get("se_qkT", 0))
            for h in range(H):
                tr(PB6[:, h * 128:(h + 1) * 128],
                   qtok[:, h * 128:(h + 1) * 128], idb[:])
            for h in range(H):
                tr(PB7[:, h * 128:(h + 1) * 128],
                   kbf[:, h * 128:(h + 1) * 128], idb[:])
            fin("te", "TE", 1)
            wt("se", "TE")
            se(lambda s_: s_.copy(qT[:], PB6[:]))
            se(lambda s_: s_.copy(kT[:], PB7[:]))
            fin("se", "SE", 1)
            se_qkT = cnt["SE"]

            wt("te", "SE", se_qkT)
            wt("te", "SE", se_vt)
            te_sc = {}
            te_u = {}
            te_o = {}
            ve_A = {}
            for h in range(H):
                psc = PS[h // 4]
                qtr = (h % 4) * 128
                mm(psc[:, qtr:qtr + 128], kT[:, h * 128:(h + 1) * 128],
                   qT[:, h * 128:(h + 1) * 128])
                fin("te", "TE", 1)
                te_sc[h] = cnt["TE"]
            for h in range(H):
                pu = PS[2 + h // 4]
                qtr = (h % 4) * 128
                mm(pu[:, qtr:qtr + 128], khat[:, h * 128:(h + 1) * 128],
                   vt_sb[:, h * 128:(h + 1) * 128])
                fin("te", "TE", 1)
                te_u[h] = cnt["TE"]
            for h in range(H):
                psc = PS[h // 4]
                qtr = (h % 4) * 128
                wt("ve", "TE", te_sc[3 if h < 4 else 7])
                ve(lambda v, h=h, psc=psc, qtr=qtr: v.tensor_mul(
                    A_sb[:, h * 128:(h + 1) * 128], psc[:, qtr:qtr + 128], maskT[:]))
                fin("ve", "VE", 1)
                ve_A[h] = cnt["VE"]
            for h in range(H):
                po = PS[4 + h // 4]
                qtr = (h % 4) * 128
                wt("te", "VE", ve_A[h])
                mm(po[:, qtr:qtr + 128], vt_sb[:, h * 128:(h + 1) * 128],
                   A_sb[:, h * 128:(h + 1) * 128], start=True, stop=False)
                mm(po[:, qtr:qtr + 128], Sbf[:, h * 128:(h + 1) * 128],
                   qT[:, h * 128:(h + 1) * 128], start=False, stop=True)
                fin("te", "TE", 1)
                te_o[h] = cnt["TE"]
            for h in range(H):
                pu = PS[2 + h // 4]
                qtr = (h % 4) * 128
                wt("ve", "TE", te_u[3 if h < 4 else 7])
                ve(lambda v, h=h, pu=pu, qtr=qtr: v.scalar_tensor_tensor(
                    S_sb[:, h * 128:(h + 1) * 128], S_sb[:, h * 128:(h + 1) * 128],
                    scaleT[:, sc0 + 16 + h: sc0 + 16 + h + 1],
                    pu[:, qtr:qtr + 128], ALU.mult, ALU.add))
                fin("ve", "VE", 1)
            ve_chain = cnt["VE"]
            for h in range(H):
                po = PS[4 + h // 4]
                qtr = (h % 4) * 128
                wt("se", "TE", te_o[3 if h < 4 else 7])
                se(lambda s_, h=h, po=po, qtr=qtr: s_.copy(
                    OT[:, h * SESS_T + c * CH: h * SESS_T + (c + 1) * CH],
                    po[:, qtr:qtr + 128]))
                fin("se", "SE", 1)
            g2 = {"ve_ps23": ve_chain, "ve_rope": ve_khat, "gp_rope": gp_q,
                  "ve_ps01": ve_A[H - 1], "se_ps45": cnt["SE"], "se_qkT": se_qkT,
                  "ve_scaleT": g.get("ve_scaleT", 0)}
            wt("se", "VE", ve_chain)
            se(lambda s_: s_.copy(Sbf[:], S_sb[:]))
            fin("se", "SE", 1)

            if s == 0 and c == 0:
                for nm, buf in (("qt", qT), ("kt", kT), ("vt", vt_sb), ("at", A_sb)):
                    if nm in tap_d:
                        wt("sy", "SE")
                        wt("sy", "VE")
                        dma(tap_d[nm].ap(), buf[:], "ST")
            return g2


        def ffn(s, ve_x1n, gp_x1n):
            pb6f = PB6.ap().bitcast(F32)
            for half in range(2):
                hoff = half * 256
                wt("te", "VE", ve_x1n)
                wt("te", "GP", gp_x1n)
                se_gelu = {}

                def rails(ci):
                    cgl = ci
                    h_, p_ = divmod(cgl, 32)
                    strip = 32 * (p_ % 4)
                    fb = (h_ * 8 + p_ // 4) * 128
                    rhs = x1n[strip:strip + 32,
                              (p_ // 4) * SESS_T + hoff:
                              (p_ // 4) * SESS_T + hoff + 256]
                    if ci >= 1:
                        wt("te", "SE", se_gelu[ci - 1])
                    mm(pb6f[:, 0:256],
                       wblk[strip:strip + 32, fb:fb + 128], rhs,
                       tile_position=(strip, 0))
                    fin("te", "TE", 1)
                    wt("se", "TE")
                    hs = ci % 8
                    se(lambda s_, hs=hs: s_.activation(
                        hbuf[:, hs * 256:(hs + 1) * 256], pb6f[:, 0:256],
                        AF.Gelu))
                    fin("se", "SE", 1)
                    se_gelu[ci] = cnt["SE"]

                def cproj(ci, wcld, slot, j):
                    wt("te", "SE", se_gelu[ci])
                    wt("te", "WC", wcld)
                    hs = ci % 8
                    for tg in range(2):
                        ph0 = PS[0] if tg == 0 else PS[2]
                        ph1 = PS[1] if tg == 0 else PS[3]
                        lhs = hbuf[:, hs * 256 + tg * 128:
                                   hs * 256 + (tg + 1) * 128]
                        w0 = wcb[:, (slot * 2 + j) * D:
                                 (slot * 2 + j) * D + 512]
                        w1 = wcb[:, (slot * 2 + j) * D + 512:
                                 (slot * 2 + j + 1) * D]
                        mm(ph0[:], lhs, w0, start=(ci == 0), stop=(ci == NKC - 1))
                        mm(ph1[:], lhs, w1, start=(ci == 0), stop=(ci == NKC - 1))
                    fin("te", "TE", 1)

                prev = None
                for duo in range(64):
                    slot = duo % 2
                    wt("sy", "TE")
                    for j in range(2):
                        cgl = duo * 2 + j
                        dma(wcb[:, (slot * 2 + j) * D:(slot * 2 + j + 1) * D],
                            wcbf_d.ap()[cgl], "WC")
                    wcld = cnt["WC"]
                    for j in range(2):
                        ci = duo * 2 + j
                        rails(ci)
                        if prev is not None:
                            cproj(*prev)
                        prev = (ci, wcld, slot, j)
                        if s == 0 and half == 0 and ci == 0 and "hc" in tap_d:
                            wt("sy", "SE")
                            dma(tap_d["hc"].ap(), hbuf[:, 0:256], "ST")
                cproj(*prev)
                for tg in range(2):
                    pe0 = PS[0] if tg == 0 else PS[2]
                    pe1 = PS[1] if tg == 0 else PS[3]
                    wt("ve", "TE")
                    wt("ve", "ST")
                    ve(lambda v, pe0=pe0: v.tensor_copy(ffn_sb[:, 0:512], pe0[:]))
                    ve(lambda v, pe1=pe1: v.tensor_copy(ffn_sb[:, 512:1024], pe1[:]))
                    fin("ve", "VE", 1)
                    wt("te", "VE")
                    wt("te", "SE")
                    for m_ in range(NDC):
                        pslot = PS[4] if m_ < 4 else PS[5]
                        tr(pslot[:, (m_ % 4) * 128:(m_ % 4 + 1) * 128],
                           ffn_sb[:, m_ * 128:(m_ + 1) * 128],
                           idf[:])
                    fin("te", "TE", 1)
                    wt("ve", "TE")
                    for m_ in range(NDC):
                        pslot = PS[4] if m_ < 4 else PS[5]
                        col = slice(m_ * SESS_T + hoff + tg * 128,
                                    m_ * SESS_T + hoff + (tg + 1) * 128)
                        ve(lambda v, m_=m_, pslot=pslot, col=col:
                           v.tensor_add(
                               out_sb[:, m_ * 128:(m_ + 1) * 128],
                               pslot[:, (m_ % 4) * 128:(m_ % 4 + 1) * 128],
                               x1T[:, col]))
                    fin("ve", "VE", 1)
                    wt("te", "VE")
                    wt("sy", "VE")
                    tcol0 = s * SESS_T + hoff + tg * 128
                    dma(bass.AP(out_d, tcol0,
                                [[TOK_CORE, 128], [128 * TOK_CORE, NDC], [1, 128]]),
                        bass.AP(out_sb, 0,
                                [[D, 128], [128, NDC], [1, 128]]), "ST")


        def load_x(phase2, s2):
            xsrc = xT_d if phase2 == "M" else xTp_d
            wt("sy", "VE")
            wt("sy", "SE")
            wt("sy", "GP")
            dma(xT[:], bass.AP(xsrc, s2 * SESS_T,
                               [[TOK_CORE, 128], [128 * TOK_CORE, NDC], [1, SESS_T]]))

        def session(phase, s, nxt):
            main = phase == "M"
            tok0 = s * SESS_T

            wt("sy", "VE")
            wt("sy", "GP")
            dma(cbuf[:], bass.AP(ropec_d, (0 if main else TOK_CORE) * 64 + tok0 * 64,
                                 [[64, 128], [128 * 64, NCHUNK], [1, 64]]))
            dma(ssb[:], bass.AP(ropes_d, (0 if main else TOK_CORE) * 128 + tok0 * 128,
                                [[128, 128], [128 * 128, NCHUNK], [1, 128]]))
            xld = cnt["LD"]
            wt("se", "LD", xld)
            wt("ve", "LD", xld)
            wt("gp", "LD", xld)
            ln_pass(xT, xnT)
            if not main and nxt is not None:
                load_x(*nxt)
                nxt = None

            wt("te", "VE")
            wt("te", "GP")
            for d in range(NDC):
                blkc = slice(d * SESS_T, (d + 1) * SESS_T)
                mm(PS[2][0:H, :], wb[:, d * H:(d + 1) * H], xnT[:, blkc],
                   start=(d == 0), stop=(d == NDC - 1))
            for d in range(NDC):
                blkc = slice(d * SESS_T, (d + 1) * SESS_T)
                mm(PS[3][0:H, :], wa[:, d * H:(d + 1) * H], xnT[:, blkc],
                   start=(d == 0), stop=(d == NDC - 1))
            fin("te", "TE", 1)
            for _w in range(2):
                mm(PS[4][:], ones_bf[:], xbf[:, 0:512])
            fin("te", "TE", 1)
            wt("se", "TE")
            se(lambda s_: s_.activation(beta[:], PS[2][0:H, :], AF.Sigmoid, bias=bb[:]))
            se(lambda s_: s_.activation(alpha[:], PS[3][0:H, :], AF.Sigmoid, bias=ba[:]))
            fin("se", "SE", 1)
            wt("ve", "SE")
            ve(lambda v: v.tensor_scalar(beta[:], beta[:], BETA_SPAN, BETA_MIN,
                                         ALU.mult, ALU.add))
            for c in range(NCHUNK):
                cbk = slice(c * CH, (c + 1) * CH)
                ve(lambda v, cbk=cbk: v.tensor_tensor_scan(
                    Pc[:, cbk], beta[:, cbk], zero8[:], 1.0, ALU.mult, ALU.add))
            ve(lambda v: v.reciprocal(rPc[:], Pc[:]))
            ve(lambda v: v.tensor_mul(aoP[:], alpha[:], rPc[:]))
            for c in range(NCHUNK):
                cbk = slice(c * CH, (c + 1) * CH)
                ve(lambda v, c=c, cbk=cbk: v.tensor_scalar(
                    pendR[:, cbk], ones8[:],
                    Pc[:, c * CH + CH - 1:c * CH + CH], None, ALU.mult))
            fin("ve", "VE", 1)
            wt("te", "VE")
            for c in range(NCHUNK):
                cbk = slice(c * CH, (c + 1) * CH)
                tr(PS[4][:, c * 32 + 0:c * 32 + 8], Pc[0:H, cbk], idf[0:8, 0:8])
                tr(PS[4][:, c * 32 + 8:c * 32 + 16], aoP[0:H, cbk], idf[0:8, 0:8])
                tr(PS[4][:, c * 32 + 16:c * 32 + 24], pendR[0:H, cbk], idf[0:8, 0:8])
            fin("te", "TE", 1)
            wt("ve", "TE")
            ve(lambda v: v.tensor_copy(scaleT[:], PS[4][:, 0:128]))
            fin("ve", "VE", 1)
            ve_scaleT = cnt["VE"]
            if main and s == 0:
                if "scale" in tap_d:
                    wt("sy", "VE")
                    dma(tap_d["scale"].ap(), scaleT[:], "ST")
                if "xn" in tap_d:
                    wt("sy", "VE")
                    dma(tap_d["xn"].ap(), xnT[:], "ST")

            g = {"ve_scaleT": ve_scaleT}
            for c in range(NCHUNK):
                g = chunk(phase, s, c, g)

            if main:
                wt("te", "SE", g["se_ps45"])
                for m in range(NDC):
                    pwo = PS[0] if m % 2 == 0 else PS[1]
                    if m <= 1:
                        wt("te", "VE", g["ve_ps01"])
                    for hd in range(NDC):
                        mm(pwo[:], wo[:, hd * D + m * 128: hd * D + (m + 1) * 128],
                           OT[:, hd * SESS_T:(hd + 1) * SESS_T],
                           start=(hd == 0), stop=(hd == NDC - 1))
                    fin("te", "TE", 1)
                    wt("ve", "TE")
                    ve(lambda v, m=m, pwo=pwo: v.scalar_tensor_tensor(
                        x1T[:, m * SESS_T:(m + 1) * SESS_T], pwo[:],
                        bo[:, m:m + 1], xT[:, m * SESS_T:(m + 1) * SESS_T],
                        ALU.add, ALU.add))
                    fin("ve", "VE", 1)
                    if m >= 1:
                        wt("te", "VE", cnt["VE"] - 1)
                if s == 0 and "x1" in tap_d:
                    wt("sy", "VE")
                    dma(tap_d["x1"].ap(), x1T[:], "ST")
                if nxt is not None:
                    load_x(*nxt)
                    nxt = None
                wt("te", "VE")
                ve_x1n, gp_x1n = ln_pass(x1T, x1n)
                if s == 0 and "x1n" in tap_d:
                    wt("sy", "VE")
                    dma(tap_d["x1n"].ap(), x1n[:], "ST")
                if s == 0 and "ot" in tap_d:
                    wt("sy", "SE")
                    dma(tap_d["ot"].ap(), OT[:], "ST")
                ffn(s, ve_x1n, gp_x1n)
            return g

        seq = [("A", s) for s in range(n_sess_pa)] + \
              [("M", s) for s in range(n_sess_main)]
        if seq:
            load_x(*seq[0])
        snext = {i: (seq[i + 1] if i + 1 < len(seq) else None)
                 for i in range(len(seq))}
        for i, (ph, s) in enumerate(seq):
            if ph == "A":
                session("A", s, snext[i])
            if ph == "A" and (i + 1 == len(seq) or seq[i + 1][0] == "M"):
                if "sa" in tap_d:
                    wt("sy", "VE")
                    dma(tap_d["sa"].ap(), S_sb[:], "ST")
                    wt("ve", "ST")
                ve(lambda v: v.tensor_scalar(S_sb[:], S_sb[:], flag[:], None, ALU.mult))
                fin("ve", "VE", 1)
                wt("se", "VE")
                se(lambda s_: s_.copy(Sbf[:], S_sb[:]))
                fin("se", "SE", 1)
            if ph == "M":
                g = session("M", s, snext[i])
                if s == 0 and "s0" in tap_d:
                    wt("sy", "VE")
                    dma(tap_d["s0"].ap(), S_sb[:], "ST")

        wt("sy", "ST")
        return cnt, wmax

    results = {}
    for a in ("sy", "gp", "te", "ve", "se"):
        dec = {"sy": blk.sync, "gp": blk.gpsimd, "te": blk.tensor,
               "ve": blk.vector, "se": blk.scalar}[a]

        def f(_eng, a=a):
            results[a] = sched(a)
        dec(f)

    base = results["sy"][0]
    for a in ("gp", "te", "ve", "se"):
        assert results[a][0] == base, (a, results[a][0], base)
    for semn in base:
        assert results["sy"][1][semn] <= base[semn], (semn,)

    ctx.close()
    return nc, tap_d



def _rope_tables():
    import jax
    import jax.numpy as jnp
    half = 64
    with jax.default_device(jax.devices("cpu")[0]):
        inv = 1.0 / 10000.0 ** (jnp.arange(half, dtype=jnp.float32) / half)
        freqs = jnp.arange(L, dtype=jnp.float32)[:, None] * inv[None, :]
        cos = np.asarray(jax.device_get(jnp.cos(freqs)), dtype=np.float32)
        sin = np.asarray(jax.device_get(jnp.sin(freqs)), dtype=np.float32)
    ss = np.empty((L, 128), dtype=np.float32)
    ss[:, 0::2] = -sin
    ss[:, 1::2] = sin
    return cos, ss


def make_in_map(inputs, core):
    b, halfsel = core // 2, core % 2
    x = np.asarray(inputs["L0_attn_x"])
    own = np.ascontiguousarray(x[b, halfsel * TOK_CORE:(halfsel + 1) * TOK_CORE].T)
    par = np.ascontiguousarray(x[b, (1 - halfsel) * TOK_CORE:(2 - halfsel) * TOK_CORE].T)
    cos, ss = _rope_tables()
    own_off = halfsel * TOK_CORE
    par_off = (1 - halfsel) * TOK_CORE
    ropeC = np.ascontiguousarray(np.concatenate(
        [cos[own_off:own_off + TOK_CORE], cos[par_off:par_off + TOK_CORE]], axis=0))
    ropeS = np.ascontiguousarray(np.concatenate(
        [ss[own_off:own_off + TOK_CORE], ss[par_off:par_off + TOK_CORE]], axis=0))

    rm = np.asarray(inputs["L0_ffn_rail_maker"])
    rup = np.asarray(inputs["L0_ffn_rail_up_proj"])
    railA = np.zeros((128, 64 * 32), dtype=np.float32)
    railB = np.zeros((128, 64 * 128), dtype=np.float32)
    for h in range(4):
        for p in range(32):
            strip = 32 * (p % 4)
            for rl in range(2):
                r3 = h * 16 + (p // 4) * 2 + rl
                blkA = np.zeros((16, 32), np.float32)
                blkA[:, rl * 16:(rl + 1) * 16] = rm[h, 2 * p + rl].T
                blkB = np.zeros((16, 128), np.float32)
                blkB[:, rl * 64:(rl + 1) * 64] = rup[h, 2 * p + rl].T
                railA[strip:strip + 16, r3 * 32:(r3 + 1) * 32] = blkA
                railB[strip:strip + 16, r3 * 128:(r3 + 1) * 128] = blkB

    maskT = np.triu(np.ones((128, 128), dtype=np.float32))

    return {
        "xT": own, "xTp": par,
        "wqT": np.ascontiguousarray(inputs["L0_attn_Wq"].T),
        "wkT": np.ascontiguousarray(inputs["L0_attn_Wk"].T),
        "wvT": np.ascontiguousarray(inputs["L0_attn_Wv"].T),
        "woT": np.ascontiguousarray(inputs["L0_attn_Wo"].T),
        "wbT": np.ascontiguousarray(inputs["L0_attn_Wbeta"].T),
        "waT": np.ascontiguousarray(inputs["L0_attn_Walpha"].T),
        "bb": np.ascontiguousarray(np.asarray(inputs["L0_attn_bbeta"]).reshape(H, 1)),
        "ba": np.ascontiguousarray(np.asarray(inputs["L0_attn_balpha"]).reshape(H, 1)),
        "bo_cols": np.ascontiguousarray(
            np.asarray(inputs["L0_attn_bo"]).reshape(NDC, 128).T),
        "wcT_tiles": np.ascontiguousarray(
            np.asarray(inputs["L0_ffn_w_cproj"]).T.reshape(NKC, 128, D)),
        "ropeC": ropeC, "ropeS": ropeS,
        "railA": railA, "railB": railB,
        "maskT": maskT,
        "idf": np.eye(128, dtype=np.float32),
        "flag": np.full((128, 1), float(halfsel), dtype=np.float32),
        "epsc": np.full((128, 1), np.float32(EPS), dtype=np.float32),
    }


def kernel(**inputs):
    nc, _ = build_nc(8, 8, taps=())
    in_maps = [make_in_map(inputs, c) for c in range(8)]
    res = run_bass_kernel_spmd(nc, in_maps, core_ids=list(range(8)))
    out = np.empty((4, L, D), dtype=np.float32)
    for c in range(8):
        b, halfsel = c // 2, c % 2
        out[b, halfsel * TOK_CORE:(halfsel + 1) * TOK_CORE] = res.results[c]["outT"].T
    return out


# version 05
# speedup vs baseline: 1.3232x; 1.3232x over previous
import sys
import numpy as np
from contextlib import ExitStack

sys.path.insert(0, "/opt/trn_rl_repo")
import concourse.bass as bass
import concourse.mybir as mybir
from concourse.bass_utils import run_bass_kernel_spmd

try:
    import jax
    jax.config.update("jax_compilation_cache_dir", "/tmp/jax_cc_cache")
    jax.config.update("jax_persistent_cache_min_compile_time_secs", 10.0)
except Exception:
    pass

F32 = mybir.dt.float32
BF16 = mybir.dt.bfloat16
AF = mybir.ActivationFunctionType
ALU = mybir.AluOpType

D = 1024
H = 8
CH = 128
SESS_T = 512
NCHUNK = 4
TOK_CORE = 4096
L = 8192
NDC = 8
EPS = 1e-5
BETA_MIN = 0.85
BETA_SPAN = 0.999 - 0.85
NKC = 128


def build_nc(n_sess_main=8, n_sess_pa=8, taps=()):
    nc = bass.Bass("TRN2", target_bir_lowering=False)

    d_in = {}
    def di(name, shape):
        d_in[name] = nc.dram_tensor(name, list(shape), F32, kind="ExternalInput")
        return d_in[name]

    xT_d = di("xT", (D, TOK_CORE))
    xTp_d = di("xTp", (D, TOK_CORE))
    di("wqT", (D, D)); di("wkT", (D, D)); di("wvT", (D, D)); di("woT", (D, D))
    di("wbT", (D, H)); di("waT", (D, H))
    bb_d = di("bb", (H, 1)); ba_d = di("ba", (H, 1))
    bo_d = di("bo_cols", (128, NDC))
    wc_d = di("wcT_tiles", (NKC, 128, D))
    ropec_d = di("ropeC", (L, 64))
    ropes_d = di("ropeS", (L, 128))
    railA_d = di("railA", (128, 64 * 32))
    railB_d = di("railB", (128, 64 * 128))
    maskT_d = di("maskT", (128, 128))
    idf_d = di("idf", (128, 128))
    flag_d = di("flag", (128, 1))
    epsc_d = di("epsc", (128, 1))

    out_d = nc.dram_tensor("outT", [D, TOK_CORE], F32, kind="ExternalOutput")
    wcbf_d = nc.dram_tensor("wc_bf", [NKC, 128, D], BF16, kind="Internal")

    tap_d = {}
    def mktap(name, shape, dtype=F32):
        if name in taps:
            tap_d[name] = nc.dram_tensor("tap_" + name, list(shape), dtype,
                                         kind="ExternalOutput")
    mktap("xn", (128, NDC * SESS_T), BF16)
    mktap("qt", (128, H * 128), BF16)
    mktap("kt", (128, H * 128), BF16)
    mktap("vt", (128, H * 128), BF16)
    mktap("at", (128, H * 128), BF16)
    mktap("scale", (128, 128))
    mktap("s0", (128, H * 128))
    mktap("ot", (128, H * SESS_T), BF16)
    mktap("x1", (128, NDC * SESS_T))
    mktap("x1n", (128, NDC * SESS_T), BF16)
    mktap("hc", (128, 256), BF16)
    mktap("sa", (128, H * 128))

    ctx = ExitStack()
    E = ctx.enter_context
    def S(name, shape, dtype):
        return E(nc.sbuf_tensor(name, list(shape), dtype))

    wq = S("wq", (128, NDC * D), BF16)
    wk = S("wk", (128, NDC * D), BF16)
    wv = S("wv", (128, NDC * D), BF16)
    wo = S("wo", (128, NDC * D), BF16)
    wb = S("wb", (128, NDC * H), BF16)
    wa = S("wa", (128, NDC * H), BF16)
    bb = S("bb_s", (H, 1), F32)
    ba = S("ba_s", (H, 1), F32)
    bo = S("bo_s", (128, NDC), F32)
    maskT = S("maskT_s", (128, 128), F32)
    idf = S("idf_s", (128, 128), F32)
    idb = S("idb", (128, 128), BF16)
    flag = S("flag_s", (128, 1), F32)
    epsc = S("epsc_s", (128, 1), F32)
    ones_bf = S("ones_bf", (128, 128), BF16)
    ones8 = S("ones8", (H, CH), F32)
    zero8 = S("zero8", (H, CH), F32)
    wblk = S("wblk", (128, 4096), BF16)

    xT = S("xT_s", (128, NDC * SESS_T), F32)
    xbf = S("xbf", (128, 2 * SESS_T), BF16)
    sq = S("sq", (128, 2 * SESS_T), BF16)
    mB = S("mB", (128, SESS_T), F32)
    msqB = S("msqB", (128, SESS_T), F32)
    rstdB = S("rstdB", (128, SESS_T), F32)
    tmpB = S("tmpB", (128, SESS_T), F32)
    xnT = S("xnT", (128, NDC * SESS_T), BF16)
    beta = S("beta", (H, SESS_T), F32)
    alpha = S("alpha", (H, SESS_T), F32)
    Pc = S("Pc", (H, SESS_T), F32)
    rPc = S("rPc", (H, SESS_T), F32)
    aoP = S("aoP", (H, SESS_T), F32)
    pendR = S("pendR", (H, SESS_T), F32)
    scaleT = S("scaleT", (128, NCHUNK * 32), F32)
    cbuf = S("cbuf", (128, NCHUNK * 64), F32)
    ssb = S("ssb", (128, NCHUNK * 128), F32)
    qbf = S("qbf", (128, D), BF16)
    kbf = S("kbf", (128, D), BF16)
    qt1 = S("qt1", (128, D), BF16)
    qt2 = S("qt2", (128, D), BF16)
    kt1 = S("kt1", (128, D), BF16)
    kt2 = S("kt2", (128, D), BF16)
    qtok = S("qtok", (128, D), BF16)
    khat = S("khat", (128, D), BF16)
    vt_sb = S("vt_sb", (128, D), BF16)
    qT = S("qT", (128, D), BF16)
    kT = S("kT", (128, D), BF16)
    A_sb = S("A_sb", (128, H * 128), BF16)
    OT = S("OT", (128, H * SESS_T), BF16)
    S_sb = S("S_sb", (128, H * 128), F32)
    Sbf = S("Sbf", (128, H * 128), BF16)
    x1T = S("x1T", (128, NDC * SESS_T), F32)
    x1n = S("x1n", (128, NDC * SESS_T), BF16)
    hbuf = S("hbuf", (128, 8 * 256), BF16)
    wcb = S("wcb", (128, 4 * D), BF16)
    ffn_sb = S("ffn_sb", (128, D), F32)
    out_sb = ffn_sb

    PS = [E(nc.psum_tensor(f"PS{i}", (128, 512), F32)) for i in range(6)]
    PB6 = E(nc.psum_tensor("PB6", (128, 1024), BF16))
    PB7 = E(nc.psum_tensor("PB7", (128, 1024), BF16))

    sem_names = ["LD", "WC", "TE", "VE", "SE", "GP", "ST"]
    SEM = {n: E(nc.semaphore("s" + n)) for n in sem_names}

    blk = E(nc.Block())

    def sched(A):
        cnt = {n: 0 for n in sem_names}
        wmax = {n: 0 for n in sem_names}
        last = [None]
        eng = {"sy": nc.sync, "gp": nc.gpsimd, "te": nc.tensor,
               "ve": nc.vector, "se": nc.scalar}

        def fin(e, semn, k=1):
            cnt[semn] += k
            if A == e:
                assert last[0] is not None, (e, semn)
                last[0].then_inc(SEM[semn], k)

        def wt(e, semn, val=None):
            v = cnt[semn] if val is None else val
            wmax[semn] = max(wmax[semn], v)
            if A == e and v > 0:
                eng[e].wait_ge(SEM[semn], v)

        def on(e):
            return A == e

        def put(i):
            last[0] = i
            return i

        def dma(dst, src, semn="LD"):
            if on("sy"):
                put(nc.sync.dma_start(dst, src))
            fin("sy", semn, 16)

        def gdma(dst, src, semn="LD"):
            if on("gp"):
                put(nc.gpsimd.dma_start(dst, src))
            fin("gp", semn, 16)

        def mm(out, lhsT, rhs, start=True, stop=True, tile_position=None):
            if on("te"):
                kw = {}
                if tile_position is not None:
                    kw["tile_position"] = tile_position
                put(nc.tensor.matmul(out, lhsT, rhs, start=start, stop=stop, **kw))

        def tr(out, in_, ident):
            if on("te"):
                put(nc.tensor.transpose(out, in_, ident))

        def ve(f):
            if on("ve"):
                put(f(nc.vector))

        def gp(f):
            if on("gp"):
                put(f(nc.gpsimd))

        def se(f):
            if on("se"):
                put(f(nc.scalar))

        def ap_c(c):
            return bass.AP(cbuf, c * 64, [[NCHUNK * 64, 128], [0, H], [1, 64], [0, 2]])

        def ap_s(c):
            return bass.AP(ssb, c * 128, [[NCHUNK * 128, 128], [0, H], [1, 128]])

        def ap_swap(t):
            return bass.AP(t, 1, [[1024, 128], [128, H], [2, 64], [-1, 2]])

        for name, dst in (("wqT", wq), ("wkT", wk), ("wvT", wv), ("woT", wo)):
            for d in range(NDC):
                gdma(dst[:, d * D:(d + 1) * D],
                     d_in[name].ap()[d * 128:(d + 1) * 128, :])
        for name, dst in (("wbT", wb), ("waT", wa)):
            for d in range(NDC):
                gdma(dst[:, d * H:(d + 1) * H],
                     d_in[name].ap()[d * 128:(d + 1) * 128, :])
        dma(bb[:], bb_d.ap()); dma(ba[:], ba_d.ap()); dma(bo[:], bo_d.ap())
        dma(maskT[:], maskT_d.ap()); dma(idf[:], idf_d.ap())
        dma(flag[:], flag_d.ap()); dma(epsc[:], epsc_d.ap())
        gdma(idb[:], idf_d.ap())
        rav = xT.ap().bitcast(BF16)
        rbv = x1T.ap().bitcast(BF16)
        gdma(rav[:, 0:2048], railA_d.ap())
        gdma(rbv[:, 0:8192], railB_d.ap())
        for i in range(16):
            gdma(wcbf_d.ap()[i * 8:(i + 1) * 8], wc_d.ap()[i * 8:(i + 1) * 8], "WC")
        if on("gp"):
            put(nc.gpsimd.memset(ones8[:], 1.0))
            put(nc.gpsimd.memset(zero8[:], 0.0))
            put(nc.gpsimd.memset(S_sb[:], 0.0))
            put(nc.gpsimd.memset(Sbf[:], 0.0))
        fin("gp", "GP", 1)
        if on("ve"):
            put(nc.vector.memset(ones_bf[:], 1.0))
        fin("ve", "VE", 1)

        wt("te", "GP")
        wt("te", "LD")
        for e2 in range(128):
            h_, p_ = divmod(e2, 32)
            strip = 32 * (p_ % 4)
            fb = (h_ * 8 + p_ // 4) * 128
            pst = PS[e2 % 2]
            for rl in range(2):
                r3 = h_ * 16 + (p_ // 4) * 2 + rl
                mm(pst[strip:strip + 32, 0:128],
                   rav[strip:strip + 16, r3 * 32:(r3 + 1) * 32],
                   rbv[strip:strip + 16, r3 * 128:(r3 + 1) * 128],
                   start=(rl == 0), stop=(rl == 1),
                   tile_position=(strip, strip))
            fin("te", "TE", 1)
            wt("se", "TE")
            se(lambda s_, strip=strip, fb=fb, pst=pst:
               s_.copy(wblk[strip:strip + 32, fb:fb + 128],
                       pst[strip:strip + 32, 0:128]))
            fin("se", "SE", 1)
            if e2 >= 1:
                wt("te", "SE", cnt["SE"] - 1)

        def ln_pass(src, dst_bf):
            wt("se", "VE")
            wt("ve", "SE")
            wt("gp", "VE")
            wt("gp", "SE")
            te_ln = {}
            for d in range(NDC):
                pp = (d % 2) * SESS_T
                blkc = slice(d * SESS_T, (d + 1) * SESS_T)
                if d >= 2:
                    wt("se", "TE", te_ln[d - 2])
                    wt("ve", "TE", te_ln[d - 2])
                    wt("gp", "TE", te_ln[d - 2])
                if d % 2 == 0:
                    se(lambda s_, pp=pp, blkc=blkc:
                       s_.copy(xbf[:, pp:pp + SESS_T], src[:, blkc]))
                    fin("se", "SE", 1)
                else:
                    gp(lambda g_, pp=pp, blkc=blkc:
                       g_.tensor_copy(xbf[:, pp:pp + SESS_T], src[:, blkc]))
                    fin("gp", "GP", 1)
                ve(lambda v, pp=pp, blkc=blkc:
                   v.tensor_tensor(sq[:, pp:pp + SESS_T], src[:, blkc], src[:, blkc],
                                   ALU.mult))
                fin("ve", "VE", 1)
                wt("te", "SE")
                wt("te", "GP")
                wt("te", "VE")
                mm(PS[0][:], ones_bf[:], xbf[:, pp:pp + SESS_T],
                   start=(d == 0), stop=(d == NDC - 1))
                mm(PS[1][:], ones_bf[:], sq[:, pp:pp + SESS_T],
                   start=(d == 0), stop=(d == NDC - 1))
                fin("te", "TE", 1)
                te_ln[d] = cnt["TE"]
            for _w in range(3):
                mm(PS[2][:], ones_bf[:], xbf[:, 0:512])
            fin("te", "TE", 1)
            wt("ve", "TE")
            ve(lambda v: v.tensor_scalar(mB[:], PS[0][:], 1.0 / D, None, ALU.mult))
            ve(lambda v: v.tensor_mul(msqB[:], mB[:], mB[:]))
            ve(lambda v: v.scalar_tensor_tensor(msqB[:], PS[1][:], 1.0 / D, msqB[:],
                                                ALU.mult, ALU.subtract))
            fin("ve", "VE", 1)
            wt("se", "VE")
            se(lambda s_: s_.activation(tmpB[:], msqB[:], AF.Sqrt, bias=epsc[:]))
            fin("se", "SE", 1)
            wt("ve", "SE")
            wt("gp", "SE")
            ve(lambda v: v.reciprocal(rstdB[:], tmpB[:]))
            fin("ve", "VE", 1)
            wt("gp", "VE")
            for d in range(NDC):
                blkc = slice(d * SESS_T, (d + 1) * SESS_T)
                if d % 2 == 0:
                    ve(lambda v, blkc=blkc: v.scalar_tensor_tensor(
                        dst_bf[:, blkc], src[:, blkc], 1.0, mB[:],
                        ALU.mult, ALU.subtract))
                    ve(lambda v, blkc=blkc: v.tensor_mul(dst_bf[:, blkc],
                                                         dst_bf[:, blkc], rstdB[:]))
                else:
                    gp(lambda g_, blkc=blkc: g_.tensor_sub(
                        dst_bf[:, blkc], src[:, blkc], mB[:]))
                    gp(lambda g_, blkc=blkc: g_.tensor_mul(dst_bf[:, blkc],
                                                           dst_bf[:, blkc], rstdB[:]))
            fin("ve", "VE", 1)
            fin("gp", "GP", 1)
            return (cnt["VE"], cnt["GP"])

        def chunk(phase, s, c, g):
            main = phase == "M"
            sc0 = c * 32
            xoff = c * CH

            def proj(wsb, p0, p1):
                for half2 in range(2):
                    pdst = p0 if half2 == 0 else p1
                    for d in range(NDC):
                        lhs = xnT[:, d * SESS_T + xoff: d * SESS_T + xoff + CH]
                        mm(pdst[:], lhs,
                           wsb[:, d * D + half2 * 512: d * D + half2 * 512 + 512],
                           start=(d == 0), stop=(d == NDC - 1))

            if main:
                wt("te", "VE", g.get("ve_ps01", 0))
                wt("te", "VE", g.get("ve_scaleT", 0))
                proj(wq, PS[0], PS[1])
                fin("te", "TE", 1)
            te_q = cnt["TE"]
            wt("te", "VE", g.get("ve_ps23", 0))
            wt("te", "VE", g.get("ve_scaleT", 0))
            proj(wk, PS[2], PS[3])
            fin("te", "TE", 1)
            te_k = cnt["TE"]
            wt("te", "SE", g.get("se_ps45", 0))
            wt("te", "VE", g.get("ve_ps45", 0))
            proj(wv, PS[4], PS[5])
            fin("te", "TE", 1)
            te_v = cnt["TE"]

            if main:
                wt("se", "TE", te_q)
                se(lambda s_: s_.copy(qbf[:, 0:512], PS[0][:]))
                se(lambda s_: s_.copy(qbf[:, 512:1024], PS[1][:]))
                fin("se", "SE", 1)
            se_qbf = cnt["SE"]
            wt("se", "TE", te_k)
            se(lambda s_: s_.copy(kbf[:, 0:512], PS[2][:]))
            se(lambda s_: s_.copy(kbf[:, 512:1024], PS[3][:]))
            fin("se", "SE", 1)
            se_kbf = cnt["SE"]

            gp_q = cnt["GP"]
            if main:
                wt("gp", "SE", se_qbf)
                gp(lambda g_: g_.tensor_mul(qt1[:], qbf[:], ap_c(c)))
                gp(lambda g_: g_.tensor_mul(qt2[:], ap_swap(qbf), ap_s(c)))
                gp(lambda g_: g_.tensor_add(qbf[:], qt1[:], qt2[:]))
                for h in range(H):
                    gp(lambda g_, h=h: g_.tensor_scalar(
                        qtok[:, h * 128:(h + 1) * 128], qbf[:, h * 128:(h + 1) * 128],
                        scaleT[:, sc0 + h: sc0 + h + 1], None, ALU.mult))
                fin("gp", "GP", 1)
                gp_q = cnt["GP"]

            wt("ve", "SE", se_kbf)
            ve(lambda v: v.tensor_mul(kt1[:], kbf[:], ap_c(c)))
            ve(lambda v: v.tensor_mul(kt2[:], ap_swap(kbf), ap_s(c)))
            ve(lambda v: v.tensor_add(kbf[:], kt1[:], kt2[:]))
            fin("ve", "VE", 1)
            ve_krope = cnt["VE"]
            wt("gp", "VE", ve_krope)
            for h in range(H):
                gp(lambda g_, h=h: g_.tensor_scalar(
                    khat[:, h * 128:(h + 1) * 128], kbf[:, h * 128:(h + 1) * 128],
                    scaleT[:, sc0 + 16 + h: sc0 + 16 + h + 1], None, ALU.mult))
            fin("gp", "GP", 1)
            gp_khat = cnt["GP"]

            wt("ve", "TE", te_v)
            for h in range(H):
                psrc = PS[4] if h < 4 else PS[5]
                ve(lambda v, h=h, psrc=psrc: v.tensor_scalar(
                    vt_sb[:, h * 128:(h + 1) * 128],
                    psrc[:, (h % 4) * 128:(h % 4 + 1) * 128],
                    scaleT[:, sc0 + 8 + h: sc0 + 8 + h + 1], None, ALU.mult))
            fin("ve", "VE", 1)
            ve_vt = cnt["VE"]

            if not main:
                wt("te", "GP", gp_khat)
                wt("te", "VE", ve_vt)
                te_u = {}
                for h in range(H):
                    pu = PS[2 + h // 4]
                    qtr = (h % 4) * 128
                    mm(pu[:, qtr:qtr + 128], khat[:, h * 128:(h + 1) * 128],
                       vt_sb[:, h * 128:(h + 1) * 128])
                    fin("te", "TE", 1)
                    te_u[h] = cnt["TE"]
                for h in range(H):
                    pu = PS[2 + h // 4]
                    qtr = (h % 4) * 128
                    wt("ve", "TE", te_u[3 if h < 4 else 7])
                    ve(lambda v, h=h, pu=pu, qtr=qtr: v.scalar_tensor_tensor(
                        S_sb[:, h * 128:(h + 1) * 128], S_sb[:, h * 128:(h + 1) * 128],
                        scaleT[:, sc0 + 16 + h: sc0 + 16 + h + 1],
                        pu[:, qtr:qtr + 128], ALU.mult, ALU.add))
                    fin("ve", "VE", 1)
                return {"ve_ps23": cnt["VE"], "ve_ps45": ve_vt,
                        "ve_scaleT": g.get("ve_scaleT", 0), "ve_ps01": 0,
                        "se_ps45": 0}

            wt("te", "GP", gp_q)
            wt("te", "VE", ve_krope)
            wt("te", "SE", g.get("se_qkT", 0))
            wt("te", "VE", g.get("ve_qkT", 0))
            for h in range(H):
                tr(PB6[:, h * 128:(h + 1) * 128],
                   qtok[:, h * 128:(h + 1) * 128], idb[:])
            for h in range(H):
                tr(PB7[:, h * 128:(h + 1) * 128],
                   kbf[:, h * 128:(h + 1) * 128], idb[:])
            fin("te", "TE", 1)
            wt("se", "TE")
            se(lambda s_: s_.copy(qT[:], PB6[:]))
            fin("se", "SE", 1)
            se_qkT = cnt["SE"]
            wt("ve", "TE")
            ve(lambda v: v.tensor_copy(kT[:], PB7[:]))
            fin("ve", "VE", 1)
            ve_qkT = cnt["VE"]

            te_sc = {}
            te_u = {}
            te_o = {}
            ve_A = {}
            wt("te", "GP", gp_khat)
            wt("te", "VE", ve_vt)
            for h in range(H):
                pu = PS[2 + h // 4]
                qtr = (h % 4) * 128
                mm(pu[:, qtr:qtr + 128], khat[:, h * 128:(h + 1) * 128],
                   vt_sb[:, h * 128:(h + 1) * 128])
                fin("te", "TE", 1)
                te_u[h] = cnt["TE"]
            wt("te", "SE", se_qkT)
            wt("te", "VE", ve_qkT)
            for h in range(H):
                psc = PS[h // 4]
                qtr = (h % 4) * 128
                mm(psc[:, qtr:qtr + 128], kT[:, h * 128:(h + 1) * 128],
                   qT[:, h * 128:(h + 1) * 128])
                fin("te", "TE", 1)
                te_sc[h] = cnt["TE"]
            for h in range(H):
                psc = PS[h // 4]
                qtr = (h % 4) * 128
                wt("ve", "TE", te_sc[3 if h < 4 else 7])
                ve(lambda v, h=h, psc=psc, qtr=qtr: v.tensor_mul(
                    A_sb[:, h * 128:(h + 1) * 128], psc[:, qtr:qtr + 128], maskT[:]))
                fin("ve", "VE", 1)
                ve_A[h] = cnt["VE"]
            for h in range(H):
                po = PS[4 + h // 4]
                qtr = (h % 4) * 128
                wt("te", "VE", ve_A[h])
                mm(po[:, qtr:qtr + 128], vt_sb[:, h * 128:(h + 1) * 128],
                   A_sb[:, h * 128:(h + 1) * 128], start=True, stop=False)
                mm(po[:, qtr:qtr + 128], Sbf[:, h * 128:(h + 1) * 128],
                   qT[:, h * 128:(h + 1) * 128], start=False, stop=True)
                fin("te", "TE", 1)
                te_o[h] = cnt["TE"]
            for h in range(H):
                pu = PS[2 + h // 4]
                qtr = (h % 4) * 128
                wt("ve", "TE", te_u[3 if h < 4 else 7])
                ve(lambda v, h=h, pu=pu, qtr=qtr: v.scalar_tensor_tensor(
                    S_sb[:, h * 128:(h + 1) * 128], S_sb[:, h * 128:(h + 1) * 128],
                    scaleT[:, sc0 + 16 + h: sc0 + 16 + h + 1],
                    pu[:, qtr:qtr + 128], ALU.mult, ALU.add))
                fin("ve", "VE", 1)
            ve_chain = cnt["VE"]
            for h in range(H):
                po = PS[4 + h // 4]
                qtr = (h % 4) * 128
                wt("se", "TE", te_o[3 if h < 4 else 7])
                se(lambda s_, h=h, po=po, qtr=qtr: s_.copy(
                    OT[:, h * SESS_T + c * CH: h * SESS_T + (c + 1) * CH],
                    po[:, qtr:qtr + 128]))
                fin("se", "SE", 1)
            g2 = {"ve_ps23": ve_chain, "ve_ps01": ve_A[H - 1],
                  "se_ps45": cnt["SE"], "se_qkT": se_qkT, "ve_qkT": ve_qkT,
                  "ve_scaleT": g.get("ve_scaleT", 0), "ve_ps45": ve_vt}
            wt("se", "VE", ve_chain)
            se(lambda s_: s_.copy(Sbf[:], S_sb[:]))
            fin("se", "SE", 1)

            if s == 0 and c == 0:
                for nm, buf in (("qt", qT), ("kt", kT), ("vt", vt_sb), ("at", A_sb)):
                    if nm in tap_d:
                        wt("sy", "SE")
                        wt("sy", "VE")
                        dma(tap_d[nm].ap(), buf[:], "ST")
            return g2


        def ffn(s, ve_x1n, gp_x1n):
            pb6f = PB6.ap().bitcast(F32)
            for half in range(2):
                hoff = half * 256
                wt("te", "VE", ve_x1n)
                wt("te", "GP", gp_x1n)
                se_gelu = {}

                def rails(ci):
                    cgl = ci
                    h_, p_ = divmod(cgl, 32)
                    strip = 32 * (p_ % 4)
                    fb = (h_ * 8 + p_ // 4) * 128
                    rhs = x1n[strip:strip + 32,
                              (p_ // 4) * SESS_T + hoff:
                              (p_ // 4) * SESS_T + hoff + 256]
                    if ci >= 1:
                        wt("te", "SE", se_gelu[ci - 1])
                    mm(pb6f[:, 0:256],
                       wblk[strip:strip + 32, fb:fb + 128], rhs,
                       tile_position=(strip, 0))
                    fin("te", "TE", 1)
                    wt("se", "TE")
                    hs = ci % 8
                    se(lambda s_, hs=hs: s_.activation(
                        hbuf[:, hs * 256:(hs + 1) * 256], pb6f[:, 0:256],
                        AF.Gelu))
                    fin("se", "SE", 1)
                    se_gelu[ci] = cnt["SE"]

                def cproj(ci, wcld, slot, j):
                    wt("te", "SE", se_gelu[ci])
                    wt("te", "WC", wcld)
                    hs = ci % 8
                    for tg in range(2):
                        ph0 = PS[0] if tg == 0 else PS[2]
                        ph1 = PS[1] if tg == 0 else PS[3]
                        lhs = hbuf[:, hs * 256 + tg * 128:
                                   hs * 256 + (tg + 1) * 128]
                        w0 = wcb[:, (slot * 2 + j) * D:
                                 (slot * 2 + j) * D + 512]
                        w1 = wcb[:, (slot * 2 + j) * D + 512:
                                 (slot * 2 + j + 1) * D]
                        mm(ph0[:], lhs, w0, start=(ci == 0), stop=(ci == NKC - 1))
                        mm(ph1[:], lhs, w1, start=(ci == 0), stop=(ci == NKC - 1))
                    fin("te", "TE", 1)

                prev = None
                wcl = {}
                te_duo = {}

                def wc_load(duo):
                    slot = duo % 2
                    if duo >= 2:
                        wt("sy", "TE", te_duo[duo - 2])
                    for j in range(2):
                        cgl = duo * 2 + j
                        dma(wcb[:, (slot * 2 + j) * D:(slot * 2 + j + 1) * D],
                            wcbf_d.ap()[cgl], "WC")
                    wcl[duo] = cnt["WC"]

                wc_load(0)
                wc_load(1)
                for duo in range(64):
                    slot = duo % 2
                    for j in range(2):
                        ci = duo * 2 + j
                        rails(ci)
                        if prev is not None:
                            cproj(*prev)
                        prev = (ci, wcl[duo], slot, j)
                        if s == 0 and half == 0 and ci == 0 and "hc" in tap_d:
                            wt("sy", "SE")
                            dma(tap_d["hc"].ap(), hbuf[:, 0:256], "ST")
                    te_duo[duo] = cnt["TE"]
                    if duo + 2 < 64:
                        wc_load(duo + 2)
                cproj(*prev)
                for tg in range(2):
                    pe0 = PS[0] if tg == 0 else PS[2]
                    pe1 = PS[1] if tg == 0 else PS[3]
                    wt("ve", "TE")
                    wt("ve", "ST")
                    ve(lambda v, pe0=pe0: v.tensor_copy(ffn_sb[:, 0:512], pe0[:]))
                    ve(lambda v, pe1=pe1: v.tensor_copy(ffn_sb[:, 512:1024], pe1[:]))
                    fin("ve", "VE", 1)
                    wt("te", "VE")
                    wt("te", "SE")
                    for m_ in range(NDC):
                        pslot = PS[4] if m_ < 4 else PS[5]
                        tr(pslot[:, (m_ % 4) * 128:(m_ % 4 + 1) * 128],
                           ffn_sb[:, m_ * 128:(m_ + 1) * 128],
                           idf[:])
                    fin("te", "TE", 1)
                    wt("ve", "TE")
                    for m_ in range(NDC):
                        pslot = PS[4] if m_ < 4 else PS[5]
                        col = slice(m_ * SESS_T + hoff + tg * 128,
                                    m_ * SESS_T + hoff + (tg + 1) * 128)
                        ve(lambda v, m_=m_, pslot=pslot, col=col:
                           v.tensor_add(
                               out_sb[:, m_ * 128:(m_ + 1) * 128],
                               pslot[:, (m_ % 4) * 128:(m_ % 4 + 1) * 128],
                               x1T[:, col]))
                    fin("ve", "VE", 1)
                    wt("te", "VE")
                    wt("sy", "VE")
                    tcol0 = s * SESS_T + hoff + tg * 128
                    dma(bass.AP(out_d, tcol0,
                                [[TOK_CORE, 128], [128 * TOK_CORE, NDC], [1, 128]]),
                        bass.AP(out_sb, 0,
                                [[D, 128], [128, NDC], [1, 128]]), "ST")


        def load_x(phase2, s2):
            xsrc = xT_d if phase2 == "M" else xTp_d
            wt("sy", "VE")
            wt("sy", "SE")
            wt("sy", "GP")
            dma(xT[:], bass.AP(xsrc, s2 * SESS_T,
                               [[TOK_CORE, 128], [128 * TOK_CORE, NDC], [1, SESS_T]]))

        def session(phase, s, nxt):
            main = phase == "M"
            tok0 = s * SESS_T

            wt("sy", "VE")
            wt("sy", "GP")
            dma(cbuf[:], bass.AP(ropec_d, (0 if main else TOK_CORE) * 64 + tok0 * 64,
                                 [[64, 128], [128 * 64, NCHUNK], [1, 64]]))
            dma(ssb[:], bass.AP(ropes_d, (0 if main else TOK_CORE) * 128 + tok0 * 128,
                                [[128, 128], [128 * 128, NCHUNK], [1, 128]]))
            xld = cnt["LD"]
            wt("se", "LD", xld)
            wt("ve", "LD", xld)
            wt("gp", "LD", xld)
            ln_pass(xT, xnT)
            if not main and nxt is not None:
                load_x(*nxt)
                nxt = None

            wt("te", "VE")
            wt("te", "GP")
            for d in range(NDC):
                blkc = slice(d * SESS_T, (d + 1) * SESS_T)
                mm(PS[2][0:H, :], wb[:, d * H:(d + 1) * H], xnT[:, blkc],
                   start=(d == 0), stop=(d == NDC - 1))
            for d in range(NDC):
                blkc = slice(d * SESS_T, (d + 1) * SESS_T)
                mm(PS[3][0:H, :], wa[:, d * H:(d + 1) * H], xnT[:, blkc],
                   start=(d == 0), stop=(d == NDC - 1))
            fin("te", "TE", 1)
            for _w in range(2):
                mm(PS[4][:], ones_bf[:], xbf[:, 0:512])
            fin("te", "TE", 1)
            wt("se", "TE")
            se(lambda s_: s_.activation(beta[:], PS[2][0:H, :], AF.Sigmoid, bias=bb[:]))
            se(lambda s_: s_.activation(alpha[:], PS[3][0:H, :], AF.Sigmoid, bias=ba[:]))
            fin("se", "SE", 1)
            wt("ve", "SE")
            ve(lambda v: v.tensor_scalar(beta[:], beta[:], BETA_SPAN, BETA_MIN,
                                         ALU.mult, ALU.add))
            for c in range(NCHUNK):
                cbk = slice(c * CH, (c + 1) * CH)
                ve(lambda v, cbk=cbk: v.tensor_tensor_scan(
                    Pc[:, cbk], beta[:, cbk], zero8[:], 1.0, ALU.mult, ALU.add))
            ve(lambda v: v.reciprocal(rPc[:], Pc[:]))
            ve(lambda v: v.tensor_mul(aoP[:], alpha[:], rPc[:]))
            for c in range(NCHUNK):
                cbk = slice(c * CH, (c + 1) * CH)
                ve(lambda v, c=c, cbk=cbk: v.tensor_scalar(
                    pendR[:, cbk], ones8[:],
                    Pc[:, c * CH + CH - 1:c * CH + CH], None, ALU.mult))
            fin("ve", "VE", 1)
            wt("te", "VE")
            for c in range(NCHUNK):
                cbk = slice(c * CH, (c + 1) * CH)
                tr(PS[4][:, c * 32 + 0:c * 32 + 8], Pc[0:H, cbk], idf[0:8, 0:8])
                tr(PS[4][:, c * 32 + 8:c * 32 + 16], aoP[0:H, cbk], idf[0:8, 0:8])
                tr(PS[4][:, c * 32 + 16:c * 32 + 24], pendR[0:H, cbk], idf[0:8, 0:8])
            fin("te", "TE", 1)
            wt("ve", "TE")
            ve(lambda v: v.tensor_copy(scaleT[:], PS[4][:, 0:128]))
            fin("ve", "VE", 1)
            ve_scaleT = cnt["VE"]
            if main and s == 0:
                if "scale" in tap_d:
                    wt("sy", "VE")
                    dma(tap_d["scale"].ap(), scaleT[:], "ST")
                if "xn" in tap_d:
                    wt("sy", "VE")
                    dma(tap_d["xn"].ap(), xnT[:], "ST")

            g = {"ve_scaleT": ve_scaleT}
            for c in range(NCHUNK):
                g = chunk(phase, s, c, g)

            if main:
                wt("te", "SE", g["se_ps45"])
                for m in range(NDC):
                    pwo = PS[0] if m % 2 == 0 else PS[1]
                    if m <= 1:
                        wt("te", "VE", g["ve_ps01"])
                    for hd in range(NDC):
                        mm(pwo[:], wo[:, hd * D + m * 128: hd * D + (m + 1) * 128],
                           OT[:, hd * SESS_T:(hd + 1) * SESS_T],
                           start=(hd == 0), stop=(hd == NDC - 1))
                    fin("te", "TE", 1)
                    wt("ve", "TE")
                    ve(lambda v, m=m, pwo=pwo: v.scalar_tensor_tensor(
                        x1T[:, m * SESS_T:(m + 1) * SESS_T], pwo[:],
                        bo[:, m:m + 1], xT[:, m * SESS_T:(m + 1) * SESS_T],
                        ALU.add, ALU.add))
                    fin("ve", "VE", 1)
                    if m >= 1:
                        wt("te", "VE", cnt["VE"] - 1)
                if s == 0 and "x1" in tap_d:
                    wt("sy", "VE")
                    dma(tap_d["x1"].ap(), x1T[:], "ST")
                if nxt is not None:
                    load_x(*nxt)
                    nxt = None
                wt("te", "VE")
                ve_x1n, gp_x1n = ln_pass(x1T, x1n)
                if s == 0 and "x1n" in tap_d:
                    wt("sy", "VE")
                    dma(tap_d["x1n"].ap(), x1n[:], "ST")
                if s == 0 and "ot" in tap_d:
                    wt("sy", "SE")
                    dma(tap_d["ot"].ap(), OT[:], "ST")
                ffn(s, ve_x1n, gp_x1n)
            return g

        seq = [("A", s) for s in range(n_sess_pa)] + \
              [("M", s) for s in range(n_sess_main)]
        if seq:
            load_x(*seq[0])
        snext = {i: (seq[i + 1] if i + 1 < len(seq) else None)
                 for i in range(len(seq))}
        for i, (ph, s) in enumerate(seq):
            if ph == "A":
                session("A", s, snext[i])
            if ph == "A" and (i + 1 == len(seq) or seq[i + 1][0] == "M"):
                if "sa" in tap_d:
                    wt("sy", "VE")
                    dma(tap_d["sa"].ap(), S_sb[:], "ST")
                    wt("ve", "ST")
                ve(lambda v: v.tensor_scalar(S_sb[:], S_sb[:], flag[:], None, ALU.mult))
                fin("ve", "VE", 1)
                wt("se", "VE")
                se(lambda s_: s_.copy(Sbf[:], S_sb[:]))
                fin("se", "SE", 1)
            if ph == "M":
                g = session("M", s, snext[i])
                if s == 0 and "s0" in tap_d:
                    wt("sy", "VE")
                    dma(tap_d["s0"].ap(), S_sb[:], "ST")

        wt("sy", "ST")
        return cnt, wmax

    results = {}
    for a in ("sy", "gp", "te", "ve", "se"):
        dec = {"sy": blk.sync, "gp": blk.gpsimd, "te": blk.tensor,
               "ve": blk.vector, "se": blk.scalar}[a]

        def f(_eng, a=a):
            results[a] = sched(a)
        dec(f)

    base = results["sy"][0]
    for a in ("gp", "te", "ve", "se"):
        assert results[a][0] == base, (a, results[a][0], base)
    for semn in base:
        assert results["sy"][1][semn] <= base[semn], (semn,)

    ctx.close()
    return nc, tap_d



def _rope_tables():
    import jax
    import jax.numpy as jnp
    half = 64
    with jax.default_device(jax.devices("cpu")[0]):
        inv = 1.0 / 10000.0 ** (jnp.arange(half, dtype=jnp.float32) / half)
        freqs = jnp.arange(L, dtype=jnp.float32)[:, None] * inv[None, :]
        cos = np.asarray(jax.device_get(jnp.cos(freqs)), dtype=np.float32)
        sin = np.asarray(jax.device_get(jnp.sin(freqs)), dtype=np.float32)
    ss = np.empty((L, 128), dtype=np.float32)
    ss[:, 0::2] = -sin
    ss[:, 1::2] = sin
    return cos, ss


def make_in_map(inputs, core):
    b, halfsel = core // 2, core % 2
    x = np.asarray(inputs["L0_attn_x"])
    own = np.ascontiguousarray(x[b, halfsel * TOK_CORE:(halfsel + 1) * TOK_CORE].T)
    par = np.ascontiguousarray(x[b, (1 - halfsel) * TOK_CORE:(2 - halfsel) * TOK_CORE].T)
    cos, ss = _rope_tables()
    own_off = halfsel * TOK_CORE
    par_off = (1 - halfsel) * TOK_CORE
    ropeC = np.ascontiguousarray(np.concatenate(
        [cos[own_off:own_off + TOK_CORE], cos[par_off:par_off + TOK_CORE]], axis=0))
    ropeS = np.ascontiguousarray(np.concatenate(
        [ss[own_off:own_off + TOK_CORE], ss[par_off:par_off + TOK_CORE]], axis=0))

    rm = np.asarray(inputs["L0_ffn_rail_maker"])
    rup = np.asarray(inputs["L0_ffn_rail_up_proj"])
    railA = np.zeros((128, 64 * 32), dtype=np.float32)
    railB = np.zeros((128, 64 * 128), dtype=np.float32)
    for h in range(4):
        for p in range(32):
            strip = 32 * (p % 4)
            for rl in range(2):
                r3 = h * 16 + (p // 4) * 2 + rl
                blkA = np.zeros((16, 32), np.float32)
                blkA[:, rl * 16:(rl + 1) * 16] = rm[h, 2 * p + rl].T
                blkB = np.zeros((16, 128), np.float32)
                blkB[:, rl * 64:(rl + 1) * 64] = rup[h, 2 * p + rl].T
                railA[strip:strip + 16, r3 * 32:(r3 + 1) * 32] = blkA
                railB[strip:strip + 16, r3 * 128:(r3 + 1) * 128] = blkB

    maskT = np.triu(np.ones((128, 128), dtype=np.float32))

    return {
        "xT": own, "xTp": par,
        "wqT": np.ascontiguousarray(inputs["L0_attn_Wq"].T),
        "wkT": np.ascontiguousarray(inputs["L0_attn_Wk"].T),
        "wvT": np.ascontiguousarray(inputs["L0_attn_Wv"].T),
        "woT": np.ascontiguousarray(inputs["L0_attn_Wo"].T),
        "wbT": np.ascontiguousarray(inputs["L0_attn_Wbeta"].T),
        "waT": np.ascontiguousarray(inputs["L0_attn_Walpha"].T),
        "bb": np.ascontiguousarray(np.asarray(inputs["L0_attn_bbeta"]).reshape(H, 1)),
        "ba": np.ascontiguousarray(np.asarray(inputs["L0_attn_balpha"]).reshape(H, 1)),
        "bo_cols": np.ascontiguousarray(
            np.asarray(inputs["L0_attn_bo"]).reshape(NDC, 128).T),
        "wcT_tiles": np.ascontiguousarray(
            np.asarray(inputs["L0_ffn_w_cproj"]).T.reshape(NKC, 128, D)),
        "ropeC": ropeC, "ropeS": ropeS,
        "railA": railA, "railB": railB,
        "maskT": maskT,
        "idf": np.eye(128, dtype=np.float32),
        "flag": np.full((128, 1), float(halfsel), dtype=np.float32),
        "epsc": np.full((128, 1), np.float32(EPS), dtype=np.float32),
    }


def kernel(**inputs):
    nc, _ = build_nc(8, 8, taps=())
    in_maps = [make_in_map(inputs, c) for c in range(8)]
    res = run_bass_kernel_spmd(nc, in_maps, core_ids=list(range(8)))
    out = np.empty((4, L, D), dtype=np.float32)
    for c in range(8):
        b, halfsel = c // 2, c % 2
        out[b, halfsel * TOK_CORE:(halfsel + 1) * TOK_CORE] = res.results[c]["outT"].T
    return out


# version 06
# speedup vs baseline: 1.7271x; 1.3052x over previous
import sys
import numpy as np
from contextlib import ExitStack

sys.path.insert(0, "/opt/trn_rl_repo")
import concourse.bass as bass
import concourse.mybir as mybir
from concourse.bass_utils import run_bass_kernel_spmd

try:
    import jax
    jax.config.update("jax_compilation_cache_dir", "/tmp/jax_cc_cache")
    jax.config.update("jax_persistent_cache_min_compile_time_secs", 10.0)
except Exception:
    pass

F32 = mybir.dt.float32
BF16 = mybir.dt.bfloat16
AF = mybir.ActivationFunctionType
ALU = mybir.AluOpType

D = 1024
H = 8
CH = 128
SESS_T = 512
NCHUNK = 4
TOK_CORE = 4096
L = 8192
NDC = 8
EPS = 1e-5
BETA_MIN = 0.85
BETA_SPAN = 0.999 - 0.85
NKC = 128


def build_nc(n_sess_main=8, n_sess_pa=8, taps=()):
    nc = bass.Bass("TRN2", target_bir_lowering=False)

    d_in = {}
    def di(name, shape):
        d_in[name] = nc.dram_tensor(name, list(shape), F32, kind="ExternalInput")
        return d_in[name]

    xT_d = di("xT", (D, TOK_CORE))
    xTp_d = di("xTp", (D, TOK_CORE))
    di("wqT", (D, D)); di("wkT", (D, D)); di("wvT", (D, D)); di("woT", (D, D))
    di("wbT", (D, H)); di("waT", (D, H))
    bb_d = di("bb", (H, 1)); ba_d = di("ba", (H, 1))
    bo_d = di("bo_cols", (128, NDC))
    wc_d = di("wcT_tiles", (NKC, 128, D))
    ropec_d = di("ropeC", (L, 64))
    ropes_d = di("ropeS", (L, 128))
    railA_d = di("railA", (128, 64 * 32))
    railB_d = di("railB", (128, 64 * 128))
    maskT_d = di("maskT", (128, 128))
    idf_d = di("idf", (128, 128))
    flag_d = di("flag", (128, 1))
    epsc_d = di("epsc", (128, 1))

    out_d = nc.dram_tensor("outT", [D, TOK_CORE], F32, kind="ExternalOutput")
    wcbf_d = nc.dram_tensor("wc_bf", [NKC, 128, D], BF16, kind="Internal")

    tap_d = {}
    def mktap(name, shape, dtype=F32):
        if name in taps:
            tap_d[name] = nc.dram_tensor("tap_" + name, list(shape), dtype,
                                         kind="ExternalOutput")
    mktap("xn", (128, NDC * SESS_T), BF16)
    mktap("qt", (128, H * 128), BF16)
    mktap("kt", (128, H * 128), BF16)
    mktap("vt", (128, H * 128), BF16)
    mktap("at", (128, H * 128), BF16)
    mktap("scale", (128, 128))
    mktap("s0", (128, H * 128))
    mktap("ot", (128, H * SESS_T), BF16)
    mktap("x1", (128, NDC * SESS_T))
    mktap("x1n", (128, NDC * SESS_T), BF16)
    mktap("hc", (128, 256), BF16)
    mktap("sa", (128, H * 128))

    ctx = ExitStack()
    E = ctx.enter_context
    def S(name, shape, dtype):
        return E(nc.sbuf_tensor(name, list(shape), dtype))

    wq = S("wq", (128, NDC * D), BF16)
    wk = S("wk", (128, NDC * D), BF16)
    wv = S("wv", (128, NDC * D), BF16)
    wo = S("wo", (128, NDC * D), BF16)
    wb = S("wb", (128, NDC * H), BF16)
    wa = S("wa", (128, NDC * H), BF16)
    bb = S("bb_s", (H, 1), F32)
    ba = S("ba_s", (H, 1), F32)
    bo = S("bo_s", (128, NDC), F32)
    maskT = S("maskT_s", (128, 128), F32)
    idf = S("idf_s", (128, 128), F32)
    idb = S("idb", (128, 128), BF16)
    flag = S("flag_s", (128, 1), F32)
    epsc = S("epsc_s", (128, 1), F32)
    ones_bf = S("ones_bf", (128, 128), BF16)
    ones8 = S("ones8", (H, CH), F32)
    zero8 = S("zero8", (H, CH), F32)
    wblk = S("wblk", (128, 4096), BF16)

    xT = S("xT_s", (128, NDC * SESS_T), F32)
    xbf = S("xbf", (128, 2 * SESS_T), BF16)
    sq = S("sq", (128, 2 * SESS_T), BF16)
    mB = S("mB", (128, SESS_T), F32)
    msqB = S("msqB", (128, SESS_T), F32)
    rstdB = S("rstdB", (128, SESS_T), F32)
    tmpB = S("tmpB", (128, SESS_T), F32)
    xnT = S("xnT", (128, NDC * SESS_T), BF16)
    beta = S("beta", (H, SESS_T), F32)
    alpha = S("alpha", (H, SESS_T), F32)
    Pc = S("Pc", (H, SESS_T), F32)
    rPc = S("rPc", (H, SESS_T), F32)
    aoP = S("aoP", (H, SESS_T), F32)
    pendR = S("pendR", (H, SESS_T), F32)
    scaleT = S("scaleT", (128, NCHUNK * 32), F32)
    cbuf = S("cbuf", (128, NCHUNK * 64), F32)
    ssb = S("ssb", (128, NCHUNK * 128), F32)
    qbf = S("qbf", (128, D), BF16)
    kbf = S("kbf", (128, D), BF16)
    qt1 = S("qt1", (128, D), BF16)
    qt2 = S("qt2", (128, D), BF16)
    kt1 = S("kt1", (128, D), BF16)
    kt2 = S("kt2", (128, D), BF16)
    qtok = S("qtok", (128, D), BF16)
    khat = S("khat", (128, D), BF16)
    vt_sb = S("vt_sb", (128, D), BF16)
    qT = S("qT", (128, D), BF16)
    kT = S("kT", (128, D), BF16)
    A_sb = S("A_sb", (128, H * 128), BF16)
    OT = S("OT", (128, H * SESS_T), BF16)
    S_sb = S("S_sb", (128, H * 128), F32)
    Sbf = S("Sbf", (128, H * 128), BF16)
    x1T = S("x1T", (128, NDC * SESS_T), F32)
    x1n = S("x1n", (128, NDC * SESS_T), BF16)
    hbuf = S("hbuf", (128, 8 * 256), BF16)
    wcb = S("wcb", (128, 4 * D), BF16)
    ffn_sb = S("ffn_sb", (128, D), F32)
    out_sb = ffn_sb

    PS = [E(nc.psum_tensor(f"PS{i}", (128, 512), F32)) for i in range(6)]
    PB6 = E(nc.psum_tensor("PB6", (128, 1024), BF16))
    PB7 = E(nc.psum_tensor("PB7", (128, 1024), BF16))

    sem_names = ["LD", "WC", "TE", "VE", "SE", "GP", "ST"]
    SEM = {n: E(nc.semaphore("s" + n)) for n in sem_names}

    blk = E(nc.Block())

    def sched(A):
        cnt = {n: 0 for n in sem_names}
        wmax = {n: 0 for n in sem_names}
        last = [None]
        eng = {"sy": nc.sync, "gp": nc.gpsimd, "te": nc.tensor,
               "ve": nc.vector, "se": nc.scalar}

        def fin(e, semn, k=1):
            cnt[semn] += k
            if A == e:
                assert last[0] is not None, (e, semn)
                last[0].then_inc(SEM[semn], k)

        def wt(e, semn, val=None):
            v = cnt[semn] if val is None else val
            wmax[semn] = max(wmax[semn], v)
            if A == e and v > 0:
                eng[e].wait_ge(SEM[semn], v)

        def on(e):
            return A == e

        def put(i):
            last[0] = i
            return i

        def dma(dst, src, semn="LD"):
            if on("sy"):
                put(nc.sync.dma_start(dst, src))
            fin("sy", semn, 16)

        def gdma(dst, src, semn="LD"):
            if on("gp"):
                put(nc.gpsimd.dma_start(dst, src))
            fin("gp", semn, 16)

        def mm(out, lhsT, rhs, start=True, stop=True, tile_position=None):
            if on("te"):
                kw = {}
                if tile_position is not None:
                    kw["tile_position"] = tile_position
                put(nc.tensor.matmul(out, lhsT, rhs, start=start, stop=stop, **kw))

        def tr(out, in_, ident):
            if on("te"):
                put(nc.tensor.transpose(out, in_, ident))

        def ve(f):
            if on("ve"):
                put(f(nc.vector))

        def gp(f):
            if on("gp"):
                put(f(nc.gpsimd))

        def se(f):
            if on("se"):
                put(f(nc.scalar))

        def ap_c(c):
            return bass.AP(cbuf, c * 64, [[NCHUNK * 64, 128], [0, H], [1, 64], [0, 2]])

        def ap_s(c):
            return bass.AP(ssb, c * 128, [[NCHUNK * 128, 128], [0, H], [1, 128]])

        def ap_swap(t):
            return bass.AP(t, 1, [[1024, 128], [128, H], [2, 64], [-1, 2]])

        for name, dst in (("wqT", wq), ("wkT", wk), ("wvT", wv), ("woT", wo)):
            for d in range(NDC):
                gdma(dst[:, d * D:(d + 1) * D],
                     d_in[name].ap()[d * 128:(d + 1) * 128, :])
        for name, dst in (("wbT", wb), ("waT", wa)):
            for d in range(NDC):
                gdma(dst[:, d * H:(d + 1) * H],
                     d_in[name].ap()[d * 128:(d + 1) * 128, :])
        dma(bb[:], bb_d.ap()); dma(ba[:], ba_d.ap()); dma(bo[:], bo_d.ap())
        dma(maskT[:], maskT_d.ap()); dma(idf[:], idf_d.ap())
        dma(flag[:], flag_d.ap()); dma(epsc[:], epsc_d.ap())
        gdma(idb[:], idf_d.ap())
        rav = xT.ap().bitcast(BF16)
        rbv = x1T.ap().bitcast(BF16)
        gdma(rav[:, 0:2048], railA_d.ap())
        gdma(rbv[:, 0:8192], railB_d.ap())
        for i in range(16):
            gdma(wcbf_d.ap()[i * 8:(i + 1) * 8], wc_d.ap()[i * 8:(i + 1) * 8], "WC")
        if on("gp"):
            put(nc.gpsimd.memset(ones8[:], 1.0))
            put(nc.gpsimd.memset(zero8[:], 0.0))
            put(nc.gpsimd.memset(S_sb[:], 0.0))
            put(nc.gpsimd.memset(Sbf[:], 0.0))
        fin("gp", "GP", 1)
        if on("ve"):
            put(nc.vector.memset(ones_bf[:], 1.0))
        fin("ve", "VE", 1)

        wt("te", "GP")
        wt("te", "LD")
        for e2 in range(128):
            h_, p_ = divmod(e2, 32)
            strip = 32 * (p_ % 4)
            fb = (h_ * 8 + p_ // 4) * 128
            pst = PS[e2 % 2]
            for rl in range(2):
                r3 = h_ * 16 + (p_ // 4) * 2 + rl
                mm(pst[strip:strip + 32, 0:128],
                   rav[strip:strip + 16, r3 * 32:(r3 + 1) * 32],
                   rbv[strip:strip + 16, r3 * 128:(r3 + 1) * 128],
                   start=(rl == 0), stop=(rl == 1),
                   tile_position=(strip, strip))
            fin("te", "TE", 1)
            wt("se", "TE")
            se(lambda s_, strip=strip, fb=fb, pst=pst:
               s_.copy(wblk[strip:strip + 32, fb:fb + 128],
                       pst[strip:strip + 32, 0:128]))
            fin("se", "SE", 1)
            if e2 >= 1:
                wt("te", "SE", cnt["SE"] - 1)

        def ln_pass(src, dst_bf):
            wt("se", "VE")
            wt("ve", "SE")
            wt("gp", "VE")
            wt("gp", "SE")
            te_ln = {}
            for d in range(NDC):
                pp = (d % 2) * SESS_T
                blkc = slice(d * SESS_T, (d + 1) * SESS_T)
                if d >= 2:
                    wt("se", "TE", te_ln[d - 2])
                    wt("ve", "TE", te_ln[d - 2])
                    wt("gp", "TE", te_ln[d - 2])
                if d % 2 == 0:
                    se(lambda s_, pp=pp, blkc=blkc:
                       s_.copy(xbf[:, pp:pp + SESS_T], src[:, blkc]))
                    fin("se", "SE", 1)
                else:
                    gp(lambda g_, pp=pp, blkc=blkc:
                       g_.tensor_copy(xbf[:, pp:pp + SESS_T], src[:, blkc]))
                    fin("gp", "GP", 1)
                ve(lambda v, pp=pp, blkc=blkc:
                   v.tensor_tensor(sq[:, pp:pp + SESS_T], src[:, blkc], src[:, blkc],
                                   ALU.mult))
                fin("ve", "VE", 1)
                wt("te", "SE")
                wt("te", "GP")
                wt("te", "VE")
                mm(PS[0][:], ones_bf[:], xbf[:, pp:pp + SESS_T],
                   start=(d == 0), stop=(d == NDC - 1))
                mm(PS[1][:], ones_bf[:], sq[:, pp:pp + SESS_T],
                   start=(d == 0), stop=(d == NDC - 1))
                fin("te", "TE", 1)
                te_ln[d] = cnt["TE"]
            for _w in range(3):
                mm(PS[2][:], ones_bf[:], xbf[:, 0:512])
            fin("te", "TE", 1)
            wt("ve", "TE")
            ve(lambda v: v.tensor_scalar(mB[:], PS[0][:], 1.0 / D, None, ALU.mult))
            ve(lambda v: v.tensor_mul(msqB[:], mB[:], mB[:]))
            ve(lambda v: v.scalar_tensor_tensor(msqB[:], PS[1][:], 1.0 / D, msqB[:],
                                                ALU.mult, ALU.subtract))
            fin("ve", "VE", 1)
            wt("se", "VE")
            se(lambda s_: s_.activation(tmpB[:], msqB[:], AF.Sqrt, bias=epsc[:]))
            fin("se", "SE", 1)
            wt("ve", "SE")
            ve(lambda v: v.reciprocal(rstdB[:], tmpB[:]))
            fin("ve", "VE", 1)
            ve_app = {}
            for d in range(NDC):
                blkc = slice(d * SESS_T, (d + 1) * SESS_T)
                ve(lambda v, blkc=blkc: v.scalar_tensor_tensor(
                    dst_bf[:, blkc], src[:, blkc], 1.0, mB[:],
                    ALU.mult, ALU.subtract))
                ve(lambda v, blkc=blkc: v.tensor_mul(dst_bf[:, blkc],
                                                     dst_bf[:, blkc], rstdB[:]))
                fin("ve", "VE", 1)
                ve_app[d] = cnt["VE"]
            return ve_app

        def chunk(phase, s, c, g):
            main = phase == "M"
            sc0 = c * 32
            xoff = c * CH

            def proj(wsb, p0, p1):
                for half2 in range(2):
                    pdst = p0 if half2 == 0 else p1
                    for d in range(NDC):
                        lhs = xnT[:, d * SESS_T + xoff: d * SESS_T + xoff + CH]
                        mm(pdst[:], lhs,
                           wsb[:, d * D + half2 * 512: d * D + half2 * 512 + 512],
                           start=(d == 0), stop=(d == NDC - 1))

            if main:
                wt("te", "VE", g.get("ve_ps01", 0))
                wt("te", "VE", g.get("ve_scaleT", 0))
                proj(wq, PS[0], PS[1])
                fin("te", "TE", 1)
            te_q = cnt["TE"]
            wt("te", "VE", g.get("ve_ps23", 0))
            wt("te", "VE", g.get("ve_scaleT", 0))
            proj(wk, PS[2], PS[3])
            fin("te", "TE", 1)
            te_k = cnt["TE"]
            wt("te", "SE", g.get("se_ps45", 0))
            proj(wv, PS[4], PS[5])
            fin("te", "TE", 1)
            te_v = cnt["TE"]

            if main:
                wt("se", "TE", te_q)
                se(lambda s_: s_.copy(qbf[:, 0:512], PS[0][:]))
                se(lambda s_: s_.copy(qbf[:, 512:1024], PS[1][:]))
                fin("se", "SE", 1)
            se_qbf = cnt["SE"]
            wt("se", "TE", te_k)
            se(lambda s_: s_.copy(kbf[:, 0:512], PS[2][:]))
            se(lambda s_: s_.copy(kbf[:, 512:1024], PS[3][:]))
            fin("se", "SE", 1)
            se_kbf = cnt["SE"]

            ve_qrope = cnt["VE"]
            if main:
                wt("ve", "SE", se_qbf)
                ve(lambda v: v.tensor_mul(qt1[:], qbf[:], ap_c(c)))
                ve(lambda v: v.tensor_mul(qt2[:], ap_swap(qbf), ap_s(c)))
                ve(lambda v: v.tensor_add(qbf[:], qt1[:], qt2[:]))
                fin("ve", "VE", 1)
                ve_qrope = cnt["VE"]
            wt("ve", "SE", se_kbf)
            ve(lambda v: v.tensor_mul(kt1[:], kbf[:], ap_c(c)))
            ve(lambda v: v.tensor_mul(kt2[:], ap_swap(kbf), ap_s(c)))
            ve(lambda v: v.tensor_add(kbf[:], kt1[:], kt2[:]))
            fin("ve", "VE", 1)
            ve_krope = cnt["VE"]
            for h in range(H):
                ve(lambda v, h=h: v.tensor_scalar(
                    khat[:, h * 128:(h + 1) * 128], kbf[:, h * 128:(h + 1) * 128],
                    scaleT[:, sc0 + 16 + h: sc0 + 16 + h + 1], None, ALU.mult))
            fin("ve", "VE", 1)
            ve_khat = cnt["VE"]

            se_qtok = cnt["SE"]
            if main:
                wt("se", "VE", ve_qrope)
                for h in range(H):
                    se(lambda s_, h=h: s_.activation(
                        qtok[:, h * 128:(h + 1) * 128], qbf[:, h * 128:(h + 1) * 128],
                        AF.Copy, scale=scaleT[:, sc0 + h: sc0 + h + 1]))
                fin("se", "SE", 1)
                se_qtok = cnt["SE"]

            wt("se", "TE", te_v)
            for h in range(H):
                psrc = PS[4] if h < 4 else PS[5]
                se(lambda s_, h=h, psrc=psrc: s_.activation(
                    vt_sb[:, h * 128:(h + 1) * 128],
                    psrc[:, (h % 4) * 128:(h % 4 + 1) * 128],
                    AF.Copy, scale=scaleT[:, sc0 + 8 + h: sc0 + 8 + h + 1]))
            fin("se", "SE", 1)
            se_vt = cnt["SE"]

            if not main:
                wt("te", "VE", ve_khat)
                wt("te", "SE", se_vt)
                te_u = {}
                for h in range(H):
                    pu = PS[2 + h // 4]
                    qtr = (h % 4) * 128
                    mm(pu[:, qtr:qtr + 128], khat[:, h * 128:(h + 1) * 128],
                       vt_sb[:, h * 128:(h + 1) * 128])
                    fin("te", "TE", 1)
                    te_u[h] = cnt["TE"]
                for h in range(H):
                    pu = PS[2 + h // 4]
                    qtr = (h % 4) * 128
                    wt("ve", "TE", te_u[3 if h < 4 else 7])
                    ve(lambda v, h=h, pu=pu, qtr=qtr: v.scalar_tensor_tensor(
                        S_sb[:, h * 128:(h + 1) * 128], S_sb[:, h * 128:(h + 1) * 128],
                        scaleT[:, sc0 + 16 + h: sc0 + 16 + h + 1],
                        pu[:, qtr:qtr + 128], ALU.mult, ALU.add))
                    fin("ve", "VE", 1)
                return {"ve_ps23": cnt["VE"], "se_ps45": se_vt,
                        "ve_scaleT": g.get("ve_scaleT", 0), "ve_ps01": 0}

            wt("te", "SE", se_qtok)
            wt("te", "VE", ve_krope)
            wt("te", "SE", g.get("se_qkT", 0))
            wt("te", "VE", g.get("ve_qkT", 0))
            for h in range(H):
                tr(PB6[:, h * 128:(h + 1) * 128],
                   qtok[:, h * 128:(h + 1) * 128], idb[:])
            for h in range(H):
                tr(PB7[:, h * 128:(h + 1) * 128],
                   kbf[:, h * 128:(h + 1) * 128], idb[:])
            fin("te", "TE", 1)
            wt("se", "TE")
            se(lambda s_: s_.copy(qT[:], PB6[:]))
            fin("se", "SE", 1)
            se_qkT = cnt["SE"]
            wt("ve", "TE")
            ve(lambda v: v.tensor_copy(kT[:], PB7[:]))
            fin("ve", "VE", 1)
            ve_qkT = cnt["VE"]

            te_sc = {}
            te_u = {}
            te_o = {}
            ve_A = {}
            wt("te", "VE", ve_khat)
            wt("te", "SE", se_vt)
            for h in range(H):
                pu = PS[2 + h // 4]
                qtr = (h % 4) * 128
                mm(pu[:, qtr:qtr + 128], khat[:, h * 128:(h + 1) * 128],
                   vt_sb[:, h * 128:(h + 1) * 128])
                fin("te", "TE", 1)
                te_u[h] = cnt["TE"]
            wt("te", "SE", se_qkT)
            wt("te", "VE", ve_qkT)
            for h in range(H):
                psc = PS[h // 4]
                qtr = (h % 4) * 128
                mm(psc[:, qtr:qtr + 128], kT[:, h * 128:(h + 1) * 128],
                   qT[:, h * 128:(h + 1) * 128])
                fin("te", "TE", 1)
                te_sc[h] = cnt["TE"]
            for h in range(H):
                psc = PS[h // 4]
                qtr = (h % 4) * 128
                wt("ve", "TE", te_sc[3 if h < 4 else 7])
                ve(lambda v, h=h, psc=psc, qtr=qtr: v.tensor_mul(
                    A_sb[:, h * 128:(h + 1) * 128], psc[:, qtr:qtr + 128], maskT[:]))
                fin("ve", "VE", 1)
                ve_A[h] = cnt["VE"]
            for h in range(H):
                po = PS[4 + h // 4]
                qtr = (h % 4) * 128
                wt("te", "VE", ve_A[h])
                mm(po[:, qtr:qtr + 128], vt_sb[:, h * 128:(h + 1) * 128],
                   A_sb[:, h * 128:(h + 1) * 128], start=True, stop=False)
                mm(po[:, qtr:qtr + 128], Sbf[:, h * 128:(h + 1) * 128],
                   qT[:, h * 128:(h + 1) * 128], start=False, stop=True)
                fin("te", "TE", 1)
                te_o[h] = cnt["TE"]
            for h in range(H):
                pu = PS[2 + h // 4]
                qtr = (h % 4) * 128
                wt("ve", "TE", te_u[3 if h < 4 else 7])
                ve(lambda v, h=h, pu=pu, qtr=qtr: v.scalar_tensor_tensor(
                    S_sb[:, h * 128:(h + 1) * 128], S_sb[:, h * 128:(h + 1) * 128],
                    scaleT[:, sc0 + 16 + h: sc0 + 16 + h + 1],
                    pu[:, qtr:qtr + 128], ALU.mult, ALU.add))
                fin("ve", "VE", 1)
            ve_chain = cnt["VE"]
            for h in range(H):
                po = PS[4 + h // 4]
                qtr = (h % 4) * 128
                wt("se", "TE", te_o[3 if h < 4 else 7])
                se(lambda s_, h=h, po=po, qtr=qtr: s_.copy(
                    OT[:, h * SESS_T + c * CH: h * SESS_T + (c + 1) * CH],
                    po[:, qtr:qtr + 128]))
                fin("se", "SE", 1)
            g2 = {"ve_ps23": ve_chain, "ve_ps01": ve_A[H - 1],
                  "se_ps45": cnt["SE"], "se_qkT": se_qkT, "ve_qkT": ve_qkT,
                  "ve_scaleT": g.get("ve_scaleT", 0)}
            wt("se", "VE", ve_chain)
            se(lambda s_: s_.copy(Sbf[:], S_sb[:]))
            fin("se", "SE", 1)

            if s == 0 and c == 0:
                for nm, buf in (("qt", qT), ("kt", kT), ("vt", vt_sb), ("at", A_sb)):
                    if nm in tap_d:
                        wt("sy", "SE")
                        wt("sy", "VE")
                        dma(tap_d[nm].ap(), buf[:], "ST")
            return g2


        def ffn(s, ve_app2):
            pb6f = PB6.ap().bitcast(F32)
            for half in range(2):
                hoff = half * 256
                se_gelu = {}

                def rails(ci):
                    cgl = ci
                    h_, p_ = divmod(cgl, 32)
                    strip = 32 * (p_ % 4)
                    fb = (h_ * 8 + p_ // 4) * 128
                    rhs = x1n[strip:strip + 32,
                              (p_ // 4) * SESS_T + hoff:
                              (p_ // 4) * SESS_T + hoff + 256]
                    wt("te", "VE", ve_app2[p_ // 4])
                    if ci >= 1:
                        wt("te", "SE", se_gelu[ci - 1])
                    mm(pb6f[:, 0:256],
                       wblk[strip:strip + 32, fb:fb + 128], rhs,
                       tile_position=(strip, 0))
                    fin("te", "TE", 1)
                    wt("se", "TE")
                    hs = ci % 8
                    se(lambda s_, hs=hs: s_.activation(
                        hbuf[:, hs * 256:(hs + 1) * 256], pb6f[:, 0:256],
                        AF.Gelu))
                    fin("se", "SE", 1)
                    se_gelu[ci] = cnt["SE"]

                def cproj(ci, wcld, slot, j):
                    wt("te", "SE", se_gelu[ci])
                    wt("te", "WC", wcld)
                    hs = ci % 8
                    for tg in range(2):
                        ph0 = PS[0] if tg == 0 else PS[2]
                        ph1 = PS[1] if tg == 0 else PS[3]
                        lhs = hbuf[:, hs * 256 + tg * 128:
                                   hs * 256 + (tg + 1) * 128]
                        w0 = wcb[:, (slot * 2 + j) * D:
                                 (slot * 2 + j) * D + 512]
                        w1 = wcb[:, (slot * 2 + j) * D + 512:
                                 (slot * 2 + j + 1) * D]
                        mm(ph0[:], lhs, w0, start=(ci == 0), stop=(ci == NKC - 1))
                        mm(ph1[:], lhs, w1, start=(ci == 0), stop=(ci == NKC - 1))
                    fin("te", "TE", 1)

                prev = None
                wcl = {}
                te_duo = {}

                def wc_load(duo):
                    slot = duo % 2
                    if duo >= 2:
                        wt("sy", "TE", te_duo[duo - 2])
                    for j in range(2):
                        cgl = duo * 2 + j
                        dma(wcb[:, (slot * 2 + j) * D:(slot * 2 + j + 1) * D],
                            wcbf_d.ap()[cgl], "WC")
                    wcl[duo] = cnt["WC"]

                wc_load(0)
                wc_load(1)
                for duo in range(64):
                    slot = duo % 2
                    for j in range(2):
                        ci = duo * 2 + j
                        rails(ci)
                        if prev is not None:
                            cproj(*prev)
                        prev = (ci, wcl[duo], slot, j)
                        if s == 0 and half == 0 and ci == 0 and "hc" in tap_d:
                            wt("sy", "SE")
                            dma(tap_d["hc"].ap(), hbuf[:, 0:256], "ST")
                    te_duo[duo] = cnt["TE"]
                    if duo + 2 < 64:
                        wc_load(duo + 2)
                cproj(*prev)
                for tg in range(2):
                    pe0 = PS[0] if tg == 0 else PS[2]
                    pe1 = PS[1] if tg == 0 else PS[3]
                    wt("ve", "TE")
                    wt("ve", "ST")
                    ve(lambda v, pe0=pe0: v.tensor_copy(ffn_sb[:, 0:512], pe0[:]))
                    ve(lambda v, pe1=pe1: v.tensor_copy(ffn_sb[:, 512:1024], pe1[:]))
                    fin("ve", "VE", 1)
                    wt("te", "VE")
                    wt("te", "SE")
                    for m_ in range(NDC):
                        pslot = PS[4] if m_ < 4 else PS[5]
                        tr(pslot[:, (m_ % 4) * 128:(m_ % 4 + 1) * 128],
                           ffn_sb[:, m_ * 128:(m_ + 1) * 128],
                           idf[:])
                    fin("te", "TE", 1)
                    wt("ve", "TE")
                    for m_ in range(NDC):
                        pslot = PS[4] if m_ < 4 else PS[5]
                        col = slice(m_ * SESS_T + hoff + tg * 128,
                                    m_ * SESS_T + hoff + (tg + 1) * 128)
                        ve(lambda v, m_=m_, pslot=pslot, col=col:
                           v.tensor_add(
                               out_sb[:, m_ * 128:(m_ + 1) * 128],
                               pslot[:, (m_ % 4) * 128:(m_ % 4 + 1) * 128],
                               x1T[:, col]))
                    fin("ve", "VE", 1)
                    wt("te", "VE")
                    wt("sy", "VE")
                    tcol0 = s * SESS_T + hoff + tg * 128
                    dma(bass.AP(out_d, tcol0,
                                [[TOK_CORE, 128], [128 * TOK_CORE, NDC], [1, 128]]),
                        bass.AP(out_sb, 0,
                                [[D, 128], [128, NDC], [1, 128]]), "ST")


        def load_x(phase2, s2):
            xsrc = xT_d if phase2 == "M" else xTp_d
            wt("sy", "VE")
            wt("sy", "SE")
            wt("sy", "GP")
            dma(xT[:], bass.AP(xsrc, s2 * SESS_T,
                               [[TOK_CORE, 128], [128 * TOK_CORE, NDC], [1, SESS_T]]))

        def session(phase, s, nxt):
            main = phase == "M"
            tok0 = s * SESS_T

            wt("sy", "VE")
            wt("sy", "GP")
            dma(cbuf[:], bass.AP(ropec_d, (0 if main else TOK_CORE) * 64 + tok0 * 64,
                                 [[64, 128], [128 * 64, NCHUNK], [1, 64]]))
            dma(ssb[:], bass.AP(ropes_d, (0 if main else TOK_CORE) * 128 + tok0 * 128,
                                [[128, 128], [128 * 128, NCHUNK], [1, 128]]))
            xld = cnt["LD"]
            wt("se", "LD", xld)
            wt("ve", "LD", xld)
            wt("gp", "LD", xld)
            ve_app = ln_pass(xT, xnT)
            if not main and nxt is not None:
                load_x(*nxt)
                nxt = None

            for d in range(NDC):
                blkc = slice(d * SESS_T, (d + 1) * SESS_T)
                wt("te", "VE", ve_app[d])
                mm(PS[2][0:H, :], wb[:, d * H:(d + 1) * H], xnT[:, blkc],
                   start=(d == 0), stop=(d == NDC - 1))
                mm(PS[3][0:H, :], wa[:, d * H:(d + 1) * H], xnT[:, blkc],
                   start=(d == 0), stop=(d == NDC - 1))
            fin("te", "TE", 1)
            for _w in range(2):
                mm(PS[4][:], ones_bf[:], xbf[:, 0:512])
            fin("te", "TE", 1)
            wt("se", "TE")
            se(lambda s_: s_.activation(beta[:], PS[2][0:H, :], AF.Sigmoid, bias=bb[:]))
            se(lambda s_: s_.activation(alpha[:], PS[3][0:H, :], AF.Sigmoid, bias=ba[:]))
            fin("se", "SE", 1)
            wt("ve", "SE")
            ve(lambda v: v.tensor_scalar(beta[:], beta[:], BETA_SPAN, BETA_MIN,
                                         ALU.mult, ALU.add))
            for c in range(NCHUNK):
                cbk = slice(c * CH, (c + 1) * CH)
                ve(lambda v, cbk=cbk: v.tensor_tensor_scan(
                    Pc[:, cbk], beta[:, cbk], zero8[:], 1.0, ALU.mult, ALU.add))
            ve(lambda v: v.reciprocal(rPc[:], Pc[:]))
            ve(lambda v: v.tensor_mul(aoP[:], alpha[:], rPc[:]))
            for c in range(NCHUNK):
                cbk = slice(c * CH, (c + 1) * CH)
                ve(lambda v, c=c, cbk=cbk: v.tensor_scalar(
                    pendR[:, cbk], ones8[:],
                    Pc[:, c * CH + CH - 1:c * CH + CH], None, ALU.mult))
            fin("ve", "VE", 1)
            wt("te", "VE")
            for c in range(NCHUNK):
                cbk = slice(c * CH, (c + 1) * CH)
                tr(PS[4][:, c * 32 + 0:c * 32 + 8], Pc[0:H, cbk], idf[0:8, 0:8])
                tr(PS[4][:, c * 32 + 8:c * 32 + 16], aoP[0:H, cbk], idf[0:8, 0:8])
                tr(PS[4][:, c * 32 + 16:c * 32 + 24], pendR[0:H, cbk], idf[0:8, 0:8])
            fin("te", "TE", 1)
            wt("ve", "TE")
            ve(lambda v: v.tensor_copy(scaleT[:], PS[4][:, 0:128]))
            fin("ve", "VE", 1)
            ve_scaleT = cnt["VE"]
            if main and s == 0:
                if "scale" in tap_d:
                    wt("sy", "VE")
                    dma(tap_d["scale"].ap(), scaleT[:], "ST")
                if "xn" in tap_d:
                    wt("sy", "VE")
                    dma(tap_d["xn"].ap(), xnT[:], "ST")

            g = {"ve_scaleT": ve_scaleT}
            for c in range(NCHUNK):
                g = chunk(phase, s, c, g)

            if main:
                wt("te", "SE", g["se_ps45"])
                for m in range(NDC):
                    pwo = PS[0] if m % 2 == 0 else PS[1]
                    if m <= 1:
                        wt("te", "VE", g["ve_ps01"])
                    for hd in range(NDC):
                        mm(pwo[:], wo[:, hd * D + m * 128: hd * D + (m + 1) * 128],
                           OT[:, hd * SESS_T:(hd + 1) * SESS_T],
                           start=(hd == 0), stop=(hd == NDC - 1))
                    fin("te", "TE", 1)
                    wt("ve", "TE")
                    ve(lambda v, m=m, pwo=pwo: v.scalar_tensor_tensor(
                        x1T[:, m * SESS_T:(m + 1) * SESS_T], pwo[:],
                        bo[:, m:m + 1], xT[:, m * SESS_T:(m + 1) * SESS_T],
                        ALU.add, ALU.add))
                    fin("ve", "VE", 1)
                    if m >= 1:
                        wt("te", "VE", cnt["VE"] - 1)
                if s == 0 and "x1" in tap_d:
                    wt("sy", "VE")
                    dma(tap_d["x1"].ap(), x1T[:], "ST")
                if nxt is not None:
                    load_x(*nxt)
                    nxt = None
                wt("te", "VE")
                ve_app2 = ln_pass(x1T, x1n)
                if s == 0 and "x1n" in tap_d:
                    wt("sy", "VE")
                    dma(tap_d["x1n"].ap(), x1n[:], "ST")
                if s == 0 and "ot" in tap_d:
                    wt("sy", "SE")
                    dma(tap_d["ot"].ap(), OT[:], "ST")
                ffn(s, ve_app2)
            return g

        seq = [("A", s) for s in range(n_sess_pa)] + \
              [("M", s) for s in range(n_sess_main)]
        if seq:
            load_x(*seq[0])
        snext = {i: (seq[i + 1] if i + 1 < len(seq) else None)
                 for i in range(len(seq))}
        for i, (ph, s) in enumerate(seq):
            if ph == "A":
                session("A", s, snext[i])
            if ph == "A" and (i + 1 == len(seq) or seq[i + 1][0] == "M"):
                if "sa" in tap_d:
                    wt("sy", "VE")
                    dma(tap_d["sa"].ap(), S_sb[:], "ST")
                    wt("ve", "ST")
                ve(lambda v: v.tensor_scalar(S_sb[:], S_sb[:], flag[:], None, ALU.mult))
                fin("ve", "VE", 1)
                wt("se", "VE")
                se(lambda s_: s_.copy(Sbf[:], S_sb[:]))
                fin("se", "SE", 1)
            if ph == "M":
                g = session("M", s, snext[i])
                if s == 0 and "s0" in tap_d:
                    wt("sy", "VE")
                    dma(tap_d["s0"].ap(), S_sb[:], "ST")

        wt("sy", "ST")
        return cnt, wmax

    results = {}
    for a in ("sy", "gp", "te", "ve", "se"):
        dec = {"sy": blk.sync, "gp": blk.gpsimd, "te": blk.tensor,
               "ve": blk.vector, "se": blk.scalar}[a]

        def f(_eng, a=a):
            results[a] = sched(a)
        dec(f)

    base = results["sy"][0]
    for a in ("gp", "te", "ve", "se"):
        assert results[a][0] == base, (a, results[a][0], base)
    for semn in base:
        assert results["sy"][1][semn] <= base[semn], (semn,)

    ctx.close()
    return nc, tap_d



def _rope_tables():
    import jax
    import jax.numpy as jnp
    half = 64
    with jax.default_device(jax.devices("cpu")[0]):
        inv = 1.0 / 10000.0 ** (jnp.arange(half, dtype=jnp.float32) / half)
        freqs = jnp.arange(L, dtype=jnp.float32)[:, None] * inv[None, :]
        cos = np.asarray(jax.device_get(jnp.cos(freqs)), dtype=np.float32)
        sin = np.asarray(jax.device_get(jnp.sin(freqs)), dtype=np.float32)
    ss = np.empty((L, 128), dtype=np.float32)
    ss[:, 0::2] = -sin
    ss[:, 1::2] = sin
    return cos, ss


def make_in_map(inputs, core):
    b, halfsel = core // 2, core % 2
    x = np.asarray(inputs["L0_attn_x"])
    own = np.ascontiguousarray(x[b, halfsel * TOK_CORE:(halfsel + 1) * TOK_CORE].T)
    par = np.ascontiguousarray(x[b, (1 - halfsel) * TOK_CORE:(2 - halfsel) * TOK_CORE].T)
    cos, ss = _rope_tables()
    own_off = halfsel * TOK_CORE
    par_off = (1 - halfsel) * TOK_CORE
    ropeC = np.ascontiguousarray(np.concatenate(
        [cos[own_off:own_off + TOK_CORE], cos[par_off:par_off + TOK_CORE]], axis=0))
    ropeS = np.ascontiguousarray(np.concatenate(
        [ss[own_off:own_off + TOK_CORE], ss[par_off:par_off + TOK_CORE]], axis=0))

    rm = np.asarray(inputs["L0_ffn_rail_maker"])
    rup = np.asarray(inputs["L0_ffn_rail_up_proj"])
    railA = np.zeros((128, 64 * 32), dtype=np.float32)
    railB = np.zeros((128, 64 * 128), dtype=np.float32)
    for h in range(4):
        for p in range(32):
            strip = 32 * (p % 4)
            for rl in range(2):
                r3 = h * 16 + (p // 4) * 2 + rl
                blkA = np.zeros((16, 32), np.float32)
                blkA[:, rl * 16:(rl + 1) * 16] = rm[h, 2 * p + rl].T
                blkB = np.zeros((16, 128), np.float32)
                blkB[:, rl * 64:(rl + 1) * 64] = rup[h, 2 * p + rl].T
                railA[strip:strip + 16, r3 * 32:(r3 + 1) * 32] = blkA
                railB[strip:strip + 16, r3 * 128:(r3 + 1) * 128] = blkB

    maskT = np.triu(np.ones((128, 128), dtype=np.float32))

    return {
        "xT": own, "xTp": par,
        "wqT": np.ascontiguousarray(inputs["L0_attn_Wq"].T),
        "wkT": np.ascontiguousarray(inputs["L0_attn_Wk"].T),
        "wvT": np.ascontiguousarray(inputs["L0_attn_Wv"].T),
        "woT": np.ascontiguousarray(inputs["L0_attn_Wo"].T),
        "wbT": np.ascontiguousarray(inputs["L0_attn_Wbeta"].T),
        "waT": np.ascontiguousarray(inputs["L0_attn_Walpha"].T),
        "bb": np.ascontiguousarray(np.asarray(inputs["L0_attn_bbeta"]).reshape(H, 1)),
        "ba": np.ascontiguousarray(np.asarray(inputs["L0_attn_balpha"]).reshape(H, 1)),
        "bo_cols": np.ascontiguousarray(
            np.asarray(inputs["L0_attn_bo"]).reshape(NDC, 128).T),
        "wcT_tiles": np.ascontiguousarray(
            np.asarray(inputs["L0_ffn_w_cproj"]).T.reshape(NKC, 128, D)),
        "ropeC": ropeC, "ropeS": ropeS,
        "railA": railA, "railB": railB,
        "maskT": maskT,
        "idf": np.eye(128, dtype=np.float32),
        "flag": np.full((128, 1), float(halfsel), dtype=np.float32),
        "epsc": np.full((128, 1), np.float32(EPS), dtype=np.float32),
    }


def kernel(**inputs):
    nc, _ = build_nc(8, 8, taps=())
    in_maps = [make_in_map(inputs, c) for c in range(8)]
    res = run_bass_kernel_spmd(nc, in_maps, core_ids=list(range(8)))
    out = np.empty((4, L, D), dtype=np.float32)
    for c in range(8):
        b, halfsel = c // 2, c % 2
        out[b, halfsel * TOK_CORE:(halfsel + 1) * TOK_CORE] = res.results[c]["outT"].T
    return out


# version 07
# speedup vs baseline: 1.7956x; 1.0397x over previous
import sys
import numpy as np
from contextlib import ExitStack

sys.path.insert(0, "/opt/trn_rl_repo")
import concourse.bass as bass
import concourse.mybir as mybir
from concourse.bass_utils import run_bass_kernel_spmd

try:
    import jax
    jax.config.update("jax_compilation_cache_dir", "/tmp/jax_cc_cache")
    jax.config.update("jax_persistent_cache_min_compile_time_secs", 10.0)
except Exception:
    pass

F32 = mybir.dt.float32
BF16 = mybir.dt.bfloat16
AF = mybir.ActivationFunctionType
ALU = mybir.AluOpType

D = 1024
H = 8
CH = 128
SESS_T = 512
NCHUNK = 4
TOK_CORE = 4096
L = 8192
NDC = 8
EPS = 1e-5
BETA_MIN = 0.85
BETA_SPAN = 0.999 - 0.85
NKC = 128


def build_nc(n_sess_main=8, n_sess_pa=8, taps=()):
    nc = bass.Bass("TRN2", target_bir_lowering=False)

    d_in = {}
    def di(name, shape):
        d_in[name] = nc.dram_tensor(name, list(shape), F32, kind="ExternalInput")
        return d_in[name]

    xT_d = di("xT", (D, TOK_CORE))
    xTp_d = di("xTp", (D, TOK_CORE))
    di("wqT", (D, D)); di("wkT", (D, D)); di("wvT", (D, D)); di("woT", (D, D))
    di("wbT", (D, H)); di("waT", (D, H))
    bb_d = di("bb", (H, 1)); ba_d = di("ba", (H, 1))
    bo_d = di("bo_cols", (128, NDC))
    wc_d = di("wcT_tiles", (NKC, 128, D))
    ropec_d = di("ropeC", (L, 64))
    ropes_d = di("ropeS", (L, 128))
    railA_d = di("railA", (128, 64 * 32))
    railB_d = di("railB", (128, 64 * 128))
    maskT_d = di("maskT", (128, 128))
    idf_d = di("idf", (128, 128))
    flag_d = di("flag", (128, 1))
    epsc_d = di("epsc", (128, 1))

    out_d = nc.dram_tensor("outT", [D, TOK_CORE], F32, kind="ExternalOutput")
    wcbf_d = nc.dram_tensor("wc_bf", [NKC, 128, D], BF16, kind="Internal")

    tap_d = {}
    def mktap(name, shape, dtype=F32):
        if name in taps:
            tap_d[name] = nc.dram_tensor("tap_" + name, list(shape), dtype,
                                         kind="ExternalOutput")
    mktap("xn", (128, NDC * SESS_T), BF16)
    mktap("qt", (128, H * 128), BF16)
    mktap("kt", (128, H * 128), BF16)
    mktap("vt", (128, H * 128), BF16)
    mktap("at", (128, H * 128), BF16)
    mktap("scale", (128, 128))
    mktap("s0", (128, H * 128))
    mktap("ot", (128, H * SESS_T), BF16)
    mktap("x1", (128, NDC * SESS_T))
    mktap("x1n", (128, NDC * SESS_T), BF16)
    mktap("hc", (128, 256), BF16)
    mktap("sa", (128, H * 128))

    ctx = ExitStack()
    E = ctx.enter_context
    def S(name, shape, dtype):
        return E(nc.sbuf_tensor(name, list(shape), dtype))

    wq = S("wq", (128, NDC * D), BF16)
    wk = S("wk", (128, NDC * D), BF16)
    wv = S("wv", (128, NDC * D), BF16)
    wo = S("wo", (128, NDC * D), BF16)
    wb = S("wb", (128, NDC * H), BF16)
    wa = S("wa", (128, NDC * H), BF16)
    bb = S("bb_s", (H, 1), F32)
    ba = S("ba_s", (H, 1), F32)
    bo = S("bo_s", (128, NDC), F32)
    maskT = S("maskT_s", (128, 128), F32)
    idf = S("idf_s", (128, 128), F32)
    idb = S("idb", (128, 128), BF16)
    flag = S("flag_s", (128, 1), F32)
    epsc = S("epsc_s", (128, 1), F32)
    ones_bf = S("ones_bf", (128, 128), BF16)
    ones8 = S("ones8", (H, CH), F32)
    zero8 = S("zero8", (H, CH), F32)
    wblk = S("wblk", (128, 4096), BF16)

    xT = S("xT_s", (128, NDC * SESS_T), F32)
    xbf = S("xbf", (128, 2 * SESS_T), BF16)
    sq = S("sq", (128, 2 * SESS_T), BF16)
    mB = S("mB", (128, SESS_T), F32)
    msqB = S("msqB", (128, SESS_T), F32)
    rstdB = S("rstdB", (128, SESS_T), F32)
    tmpB = S("tmpB", (128, SESS_T), F32)
    xnT = S("xnT", (128, NDC * SESS_T), BF16)
    beta = S("beta", (H, SESS_T), F32)
    alpha = S("alpha", (H, SESS_T), F32)
    Pc = S("Pc", (H, SESS_T), F32)
    pendR = S("pendR", (H, SESS_T), F32)
    scaleT = S("scaleT", (128, NCHUNK * 32), F32)
    aoPT = S("aoPT", (128, NCHUNK * 8), F32)
    rcpT = S("rcpT", (128, NCHUNK * 8), F32)
    cbuf = S("cbuf", (128, NCHUNK * 64), F32)
    ssb = S("ssb", (128, NCHUNK * 128), F32)
    qbf = S("qbf", (128, D), BF16)
    kbf = S("kbf", (128, D), BF16)
    qt1 = S("qt1", (128, D), BF16)
    qt2 = S("qt2", (128, D), BF16)
    kt1 = S("kt1", (128, D), BF16)
    kt2 = S("kt2", (128, D), BF16)
    qtok = S("qtok", (128, D), BF16)
    khat = S("khat", (128, D), BF16)
    vt_sb = S("vt_sb", (128, D), BF16)
    qT = S("qT", (128, D), BF16)
    kT = S("kT", (128, D), BF16)
    A_sb = S("A_sb", (128, H * 128), BF16)
    OT = S("OT", (128, H * SESS_T), BF16)
    S_sb = S("S_sb", (128, H * 128), F32)
    Sbf = S("Sbf", (128, H * 128), BF16)
    x1T = S("x1T", (128, NDC * SESS_T), F32)
    x1n = S("x1n", (128, NDC * SESS_T), BF16)
    hbuf = S("hbuf", (128, 8 * 256), BF16)
    wcb = S("wcb", (128, 4 * D), BF16)
    ffn_sb = S("ffn_sb", (128, D), F32)
    out_sb = ffn_sb

    PS = [E(nc.psum_tensor(f"PS{i}", (128, 512), F32)) for i in range(6)]
    PB6 = E(nc.psum_tensor("PB6", (128, 1024), BF16))
    PB7 = E(nc.psum_tensor("PB7", (128, 1024), BF16))

    sem_names = ["LD", "WC", "TE", "VE", "SE", "GP", "ST"]
    SEM = {n: E(nc.semaphore("s" + n)) for n in sem_names}

    blk = E(nc.Block())

    def sched(A):
        cnt = {n: 0 for n in sem_names}
        wmax = {n: 0 for n in sem_names}
        last = [None]
        eng = {"sy": nc.sync, "gp": nc.gpsimd, "te": nc.tensor,
               "ve": nc.vector, "se": nc.scalar}

        def fin(e, semn, k=1):
            cnt[semn] += k
            if A == e:
                assert last[0] is not None, (e, semn)
                last[0].then_inc(SEM[semn], k)

        def wt(e, semn, val=None):
            v = cnt[semn] if val is None else val
            wmax[semn] = max(wmax[semn], v)
            if A == e and v > 0:
                eng[e].wait_ge(SEM[semn], v)

        def on(e):
            return A == e

        def put(i):
            last[0] = i
            return i

        def dma(dst, src, semn="LD"):
            if on("sy"):
                put(nc.sync.dma_start(dst, src))
            fin("sy", semn, 16)

        def gdma(dst, src, semn="LD"):
            if on("gp"):
                put(nc.gpsimd.dma_start(dst, src))
            fin("gp", semn, 16)

        def mm(out, lhsT, rhs, start=True, stop=True, tile_position=None):
            if on("te"):
                kw = {}
                if tile_position is not None:
                    kw["tile_position"] = tile_position
                put(nc.tensor.matmul(out, lhsT, rhs, start=start, stop=stop, **kw))

        def tr(out, in_, ident):
            if on("te"):
                put(nc.tensor.transpose(out, in_, ident))

        def ve(f):
            if on("ve"):
                put(f(nc.vector))

        def gp(f):
            if on("gp"):
                put(f(nc.gpsimd))

        def se(f):
            if on("se"):
                put(f(nc.scalar))

        def ap_c(c):
            return bass.AP(cbuf, c * 64, [[NCHUNK * 64, 128], [0, H], [1, 64], [0, 2]])

        def ap_s(c):
            return bass.AP(ssb, c * 128, [[NCHUNK * 128, 128], [0, H], [1, 128]])

        def ap_swap(t):
            return bass.AP(t, 1, [[1024, 128], [128, H], [2, 64], [-1, 2]])

        for name, dst in (("wqT", wq), ("wkT", wk), ("wvT", wv), ("woT", wo)):
            for d in range(NDC):
                gdma(dst[:, d * D:(d + 1) * D],
                     d_in[name].ap()[d * 128:(d + 1) * 128, :])
        for name, dst in (("wbT", wb), ("waT", wa)):
            for d in range(NDC):
                gdma(dst[:, d * H:(d + 1) * H],
                     d_in[name].ap()[d * 128:(d + 1) * 128, :])
        dma(bb[:], bb_d.ap()); dma(ba[:], ba_d.ap()); dma(bo[:], bo_d.ap())
        dma(maskT[:], maskT_d.ap()); dma(idf[:], idf_d.ap())
        dma(flag[:], flag_d.ap()); dma(epsc[:], epsc_d.ap())
        gdma(idb[:], idf_d.ap())
        rav = xT.ap().bitcast(BF16)
        rbv = x1T.ap().bitcast(BF16)
        gdma(rav[:, 0:2048], railA_d.ap())
        gdma(rbv[:, 0:8192], railB_d.ap())
        for i in range(16):
            gdma(wcbf_d.ap()[i * 8:(i + 1) * 8], wc_d.ap()[i * 8:(i + 1) * 8], "WC")
        if on("gp"):
            put(nc.gpsimd.memset(ones8[:], 1.0))
            put(nc.gpsimd.memset(zero8[:], 0.0))
            put(nc.gpsimd.memset(S_sb[:], 0.0))
            put(nc.gpsimd.memset(Sbf[:], 0.0))
        fin("gp", "GP", 1)
        if on("ve"):
            put(nc.vector.memset(ones_bf[:], 1.0))
        fin("ve", "VE", 1)

        wt("te", "GP")
        wt("te", "LD")
        for e2 in range(128):
            h_, p_ = divmod(e2, 32)
            strip = 32 * (p_ % 4)
            fb = (h_ * 8 + p_ // 4) * 128
            pst = PS[e2 % 2]
            for rl in range(2):
                r3 = h_ * 16 + (p_ // 4) * 2 + rl
                mm(pst[strip:strip + 32, 0:128],
                   rav[strip:strip + 16, r3 * 32:(r3 + 1) * 32],
                   rbv[strip:strip + 16, r3 * 128:(r3 + 1) * 128],
                   start=(rl == 0), stop=(rl == 1),
                   tile_position=(strip, strip))
            fin("te", "TE", 1)
            wt("se", "TE")
            se(lambda s_, strip=strip, fb=fb, pst=pst:
               s_.copy(wblk[strip:strip + 32, fb:fb + 128],
                       pst[strip:strip + 32, 0:128]))
            fin("se", "SE", 1)
            if e2 >= 1:
                wt("te", "SE", cnt["SE"] - 1)

        def ln_pass(src, dst_bf):
            wt("se", "VE")
            wt("ve", "SE")
            wt("gp", "VE")
            wt("gp", "SE")
            te_ln = {}
            for d in range(NDC):
                pp = (d % 2) * SESS_T
                blkc = slice(d * SESS_T, (d + 1) * SESS_T)
                if d >= 2:
                    wt("se", "TE", te_ln[d - 2])
                    wt("ve", "TE", te_ln[d - 2])
                    wt("gp", "TE", te_ln[d - 2])
                if d % 2 == 0:
                    se(lambda s_, pp=pp, blkc=blkc:
                       s_.copy(xbf[:, pp:pp + SESS_T], src[:, blkc]))
                    fin("se", "SE", 1)
                else:
                    gp(lambda g_, pp=pp, blkc=blkc:
                       g_.tensor_copy(xbf[:, pp:pp + SESS_T], src[:, blkc]))
                    fin("gp", "GP", 1)
                ve(lambda v, pp=pp, blkc=blkc:
                   v.tensor_tensor(sq[:, pp:pp + SESS_T], src[:, blkc], src[:, blkc],
                                   ALU.mult))
                fin("ve", "VE", 1)
                wt("te", "SE")
                wt("te", "GP")
                wt("te", "VE")
                mm(PS[0][:], ones_bf[:], xbf[:, pp:pp + SESS_T],
                   start=(d == 0), stop=(d == NDC - 1))
                mm(PS[1][:], ones_bf[:], sq[:, pp:pp + SESS_T],
                   start=(d == 0), stop=(d == NDC - 1))
                fin("te", "TE", 1)
                te_ln[d] = cnt["TE"]
            for _w in range(3):
                mm(PS[2][:], ones_bf[:], xbf[:, 0:512])
            fin("te", "TE", 1)
            wt("ve", "TE")
            ve(lambda v: v.tensor_scalar(mB[:], PS[0][:], 1.0 / D, None, ALU.mult))
            ve(lambda v: v.tensor_mul(msqB[:], mB[:], mB[:]))
            ve(lambda v: v.scalar_tensor_tensor(msqB[:], PS[1][:], 1.0 / D, msqB[:],
                                                ALU.mult, ALU.subtract))
            fin("ve", "VE", 1)
            wt("se", "VE")
            se(lambda s_: s_.activation(tmpB[:], msqB[:], AF.Sqrt, bias=epsc[:]))
            fin("se", "SE", 1)
            wt("ve", "SE")
            ve(lambda v: v.reciprocal(rstdB[:], tmpB[:]))
            fin("ve", "VE", 1)
            ve_app = {}
            for d in range(NDC):
                blkc = slice(d * SESS_T, (d + 1) * SESS_T)
                ve(lambda v, blkc=blkc: v.scalar_tensor_tensor(
                    dst_bf[:, blkc], src[:, blkc], 1.0, mB[:],
                    ALU.mult, ALU.subtract))
                ve(lambda v, blkc=blkc: v.tensor_mul(dst_bf[:, blkc],
                                                     dst_bf[:, blkc], rstdB[:]))
                fin("ve", "VE", 1)
                ve_app[d] = cnt["VE"]
            return ve_app

        def chunk(phase, s, c, g):
            main = phase == "M"
            sc0 = c * 32
            xoff = c * CH

            def proj(wsb, p0, p1):
                for half2 in range(2):
                    pdst = p0 if half2 == 0 else p1
                    for d in range(NDC):
                        lhs = xnT[:, d * SESS_T + xoff: d * SESS_T + xoff + CH]
                        mm(pdst[:], lhs,
                           wsb[:, d * D + half2 * 512: d * D + half2 * 512 + 512],
                           start=(d == 0), stop=(d == NDC - 1))

            if main:
                wt("te", "VE", g.get("ve_ps01", 0))
                wt("te", "VE", g.get("ve_scaleT", 0))
                proj(wq, PS[0], PS[1])
                fin("te", "TE", 1)
            te_q = cnt["TE"]
            wt("te", "VE", g.get("ve_ps23", 0))
            wt("te", "VE", g.get("ve_scaleT", 0))
            proj(wk, PS[2], PS[3])
            fin("te", "TE", 1)
            te_k = cnt["TE"]
            wt("te", "SE", g.get("se_ps45", 0))
            proj(wv, PS[4], PS[5])
            fin("te", "TE", 1)
            te_v = cnt["TE"]

            if main:
                wt("se", "TE", te_q)
                se(lambda s_: s_.copy(qbf[:, 0:512], PS[0][:]))
                se(lambda s_: s_.copy(qbf[:, 512:1024], PS[1][:]))
                fin("se", "SE", 1)
            se_qbf = cnt["SE"]
            wt("se", "TE", te_k)
            se(lambda s_: s_.copy(kbf[:, 0:512], PS[2][:]))
            se(lambda s_: s_.copy(kbf[:, 512:1024], PS[3][:]))
            fin("se", "SE", 1)
            se_kbf = cnt["SE"]

            ve_qrope = cnt["VE"]
            if main:
                wt("ve", "SE", se_qbf)
                ve(lambda v: v.tensor_mul(qt1[:], qbf[:], ap_c(c)))
                ve(lambda v: v.tensor_mul(qt2[:], ap_swap(qbf), ap_s(c)))
                ve(lambda v: v.tensor_add(qbf[:], qt1[:], qt2[:]))
                fin("ve", "VE", 1)
                ve_qrope = cnt["VE"]
            wt("ve", "SE", se_kbf)
            ve(lambda v: v.tensor_mul(kt1[:], kbf[:], ap_c(c)))
            ve(lambda v: v.tensor_mul(kt2[:], ap_swap(kbf), ap_s(c)))
            ve(lambda v: v.tensor_add(kbf[:], kt1[:], kt2[:]))
            fin("ve", "VE", 1)
            ve_krope = cnt["VE"]
            for h in range(H):
                ve(lambda v, h=h: v.tensor_scalar(
                    khat[:, h * 128:(h + 1) * 128], kbf[:, h * 128:(h + 1) * 128],
                    scaleT[:, sc0 + 16 + h: sc0 + 16 + h + 1], None, ALU.mult))
            fin("ve", "VE", 1)
            ve_khat = cnt["VE"]

            se_qtok = cnt["SE"]
            if main:
                wt("se", "VE", ve_qrope)
                for h in range(H):
                    se(lambda s_, h=h: s_.activation(
                        qtok[:, h * 128:(h + 1) * 128], qbf[:, h * 128:(h + 1) * 128],
                        AF.Copy, scale=scaleT[:, sc0 + h: sc0 + h + 1]))
                fin("se", "SE", 1)
                se_qtok = cnt["SE"]

            wt("se", "VE", g.get("ve_scaleT2", 0))
            wt("se", "TE", te_v)
            for h in range(H):
                psrc = PS[4] if h < 4 else PS[5]
                se(lambda s_, h=h, psrc=psrc: s_.activation(
                    vt_sb[:, h * 128:(h + 1) * 128],
                    psrc[:, (h % 4) * 128:(h % 4 + 1) * 128],
                    AF.Copy, scale=aoPT[:, c * 8 + h: c * 8 + h + 1]))
            fin("se", "SE", 1)
            se_vt = cnt["SE"]

            if not main:
                wt("te", "VE", ve_khat)
                wt("te", "SE", se_vt)
                te_u = {}
                for h in range(H):
                    pu = PS[2 + h // 4]
                    qtr = (h % 4) * 128
                    mm(pu[:, qtr:qtr + 128], khat[:, h * 128:(h + 1) * 128],
                       vt_sb[:, h * 128:(h + 1) * 128])
                    fin("te", "TE", 1)
                    te_u[h] = cnt["TE"]
                for h in range(H):
                    pu = PS[2 + h // 4]
                    qtr = (h % 4) * 128
                    wt("ve", "TE", te_u[3 if h < 4 else 7])
                    ve(lambda v, h=h, pu=pu, qtr=qtr: v.scalar_tensor_tensor(
                        S_sb[:, h * 128:(h + 1) * 128], S_sb[:, h * 128:(h + 1) * 128],
                        scaleT[:, sc0 + 16 + h: sc0 + 16 + h + 1],
                        pu[:, qtr:qtr + 128], ALU.mult, ALU.add))
                    fin("ve", "VE", 1)
                return {"ve_ps23": cnt["VE"], "se_ps45": se_vt,
                        "ve_scaleT": g.get("ve_scaleT", 0),
                        "ve_scaleT2": g.get("ve_scaleT2", 0), "ve_ps01": 0}

            wt("te", "SE", se_qtok)
            wt("te", "VE", ve_krope)
            wt("te", "SE", g.get("se_qkT", 0))
            wt("te", "VE", g.get("ve_qkT", 0))
            for h in range(H):
                tr(PB6[:, h * 128:(h + 1) * 128],
                   qtok[:, h * 128:(h + 1) * 128], idb[:])
            for h in range(H):
                tr(PB7[:, h * 128:(h + 1) * 128],
                   kbf[:, h * 128:(h + 1) * 128], idb[:])
            fin("te", "TE", 1)
            wt("se", "TE")
            se(lambda s_: s_.copy(qT[:], PB6[:]))
            fin("se", "SE", 1)
            se_qkT = cnt["SE"]
            wt("ve", "TE")
            ve(lambda v: v.tensor_copy(kT[:], PB7[:]))
            fin("ve", "VE", 1)
            ve_qkT = cnt["VE"]

            te_sc = {}
            te_u = {}
            te_o = {}
            ve_A = {}
            wt("te", "VE", ve_khat)
            wt("te", "SE", se_vt)
            for h in range(H):
                pu = PS[2 + h // 4]
                qtr = (h % 4) * 128
                mm(pu[:, qtr:qtr + 128], khat[:, h * 128:(h + 1) * 128],
                   vt_sb[:, h * 128:(h + 1) * 128])
                fin("te", "TE", 1)
                te_u[h] = cnt["TE"]
            wt("te", "SE", se_qkT)
            wt("te", "VE", ve_qkT)
            for h in range(H):
                psc = PS[h // 4]
                qtr = (h % 4) * 128
                mm(psc[:, qtr:qtr + 128], kT[:, h * 128:(h + 1) * 128],
                   qT[:, h * 128:(h + 1) * 128])
                fin("te", "TE", 1)
                te_sc[h] = cnt["TE"]
            for h in range(H):
                psc = PS[h // 4]
                qtr = (h % 4) * 128
                wt("ve", "TE", te_sc[3 if h < 4 else 7])
                ve(lambda v, h=h, psc=psc, qtr=qtr: v.tensor_mul(
                    A_sb[:, h * 128:(h + 1) * 128], psc[:, qtr:qtr + 128], maskT[:]))
                fin("ve", "VE", 1)
                ve_A[h] = cnt["VE"]
            for h in range(H):
                po = PS[4 + h // 4]
                qtr = (h % 4) * 128
                wt("te", "VE", ve_A[h])
                mm(po[:, qtr:qtr + 128], vt_sb[:, h * 128:(h + 1) * 128],
                   A_sb[:, h * 128:(h + 1) * 128], start=True, stop=False)
                mm(po[:, qtr:qtr + 128], Sbf[:, h * 128:(h + 1) * 128],
                   qT[:, h * 128:(h + 1) * 128], start=False, stop=True)
                fin("te", "TE", 1)
                te_o[h] = cnt["TE"]
            for h in range(H):
                pu = PS[2 + h // 4]
                qtr = (h % 4) * 128
                wt("ve", "TE", te_u[3 if h < 4 else 7])
                ve(lambda v, h=h, pu=pu, qtr=qtr: v.scalar_tensor_tensor(
                    S_sb[:, h * 128:(h + 1) * 128], S_sb[:, h * 128:(h + 1) * 128],
                    scaleT[:, sc0 + 16 + h: sc0 + 16 + h + 1],
                    pu[:, qtr:qtr + 128], ALU.mult, ALU.add))
                fin("ve", "VE", 1)
            ve_chain = cnt["VE"]
            for h in range(H):
                po = PS[4 + h // 4]
                qtr = (h % 4) * 128
                wt("se", "TE", te_o[3 if h < 4 else 7])
                se(lambda s_, h=h, po=po, qtr=qtr: s_.copy(
                    OT[:, h * SESS_T + c * CH: h * SESS_T + (c + 1) * CH],
                    po[:, qtr:qtr + 128]))
                fin("se", "SE", 1)
            g2 = {"ve_ps23": ve_chain, "ve_ps01": ve_A[H - 1],
                  "se_ps45": cnt["SE"], "se_qkT": se_qkT, "ve_qkT": ve_qkT,
                  "ve_scaleT": g.get("ve_scaleT", 0),
                  "ve_scaleT2": g.get("ve_scaleT2", 0)}
            wt("se", "VE", ve_chain)
            se(lambda s_: s_.copy(Sbf[:], S_sb[:]))
            fin("se", "SE", 1)

            if s == 0 and c == 0:
                for nm, buf in (("qt", qT), ("kt", kT), ("vt", vt_sb), ("at", A_sb)):
                    if nm in tap_d:
                        wt("sy", "SE")
                        wt("sy", "VE")
                        dma(tap_d[nm].ap(), buf[:], "ST")
            return g2


        def ffn(s, ve_app2):
            pb6f = PB6.ap().bitcast(F32)
            for half in range(2):
                hoff = half * 256
                se_gelu = {}

                def rails(ci):
                    cgl = ci
                    h_, p_ = divmod(cgl, 32)
                    strip = 32 * (p_ % 4)
                    fb = (h_ * 8 + p_ // 4) * 128
                    rhs = x1n[strip:strip + 32,
                              (p_ // 4) * SESS_T + hoff:
                              (p_ // 4) * SESS_T + hoff + 256]
                    wt("te", "VE", ve_app2[p_ // 4])
                    if ci >= 1:
                        wt("te", "SE", se_gelu[ci - 1])
                    mm(pb6f[:, 0:256],
                       wblk[strip:strip + 32, fb:fb + 128], rhs,
                       tile_position=(strip, 0))
                    fin("te", "TE", 1)
                    wt("se", "TE")
                    hs = ci % 8
                    se(lambda s_, hs=hs: s_.activation(
                        hbuf[:, hs * 256:(hs + 1) * 256], pb6f[:, 0:256],
                        AF.Gelu))
                    fin("se", "SE", 1)
                    se_gelu[ci] = cnt["SE"]

                def cproj(ci, wcld, slot, j):
                    wt("te", "SE", se_gelu[ci])
                    wt("te", "WC", wcld)
                    hs = ci % 8
                    for tg in range(2):
                        ph0 = PS[0] if tg == 0 else PS[2]
                        ph1 = PS[1] if tg == 0 else PS[3]
                        lhs = hbuf[:, hs * 256 + tg * 128:
                                   hs * 256 + (tg + 1) * 128]
                        w0 = wcb[:, (slot * 2 + j) * D:
                                 (slot * 2 + j) * D + 512]
                        w1 = wcb[:, (slot * 2 + j) * D + 512:
                                 (slot * 2 + j + 1) * D]
                        mm(ph0[:], lhs, w0, start=(ci == 0), stop=(ci == NKC - 1))
                        mm(ph1[:], lhs, w1, start=(ci == 0), stop=(ci == NKC - 1))
                    fin("te", "TE", 1)

                pipe = []
                wcl = {}
                te_duo = {}

                def wc_load(duo):
                    slot = duo % 2
                    if duo >= 2:
                        wt("sy", "TE", te_duo[duo - 2])
                    for j in range(2):
                        cgl = duo * 2 + j
                        dma(wcb[:, (slot * 2 + j) * D:(slot * 2 + j + 1) * D],
                            wcbf_d.ap()[cgl], "WC")
                    wcl[duo] = cnt["WC"]

                wc_load(0)
                wc_load(1)
                for duo in range(64):
                    slot = duo % 2
                    for j in range(2):
                        ci = duo * 2 + j
                        rails(ci)
                        if len(pipe) >= 2:
                            cproj(*pipe.pop(0))
                        pipe.append((ci, wcl[duo], slot, j))
                        if s == 0 and half == 0 and ci == 0 and "hc" in tap_d:
                            wt("sy", "SE")
                            dma(tap_d["hc"].ap(), hbuf[:, 0:256], "ST")
                    te_duo[duo] = cnt["TE"]
                    if duo + 2 < 64:
                        wc_load(duo + 2)
                while pipe:
                    cproj(*pipe.pop(0))
                for tg in range(2):
                    pe0 = PS[0] if tg == 0 else PS[2]
                    pe1 = PS[1] if tg == 0 else PS[3]
                    wt("ve", "TE")
                    wt("ve", "ST")
                    ve(lambda v, pe0=pe0: v.tensor_copy(ffn_sb[:, 0:512], pe0[:]))
                    ve(lambda v, pe1=pe1: v.tensor_copy(ffn_sb[:, 512:1024], pe1[:]))
                    fin("ve", "VE", 1)
                    wt("te", "VE")
                    wt("te", "SE")
                    for m_ in range(NDC):
                        pslot = PS[4] if m_ < 4 else PS[5]
                        tr(pslot[:, (m_ % 4) * 128:(m_ % 4 + 1) * 128],
                           ffn_sb[:, m_ * 128:(m_ + 1) * 128],
                           idf[:])
                    fin("te", "TE", 1)
                    wt("ve", "TE")
                    for m_ in range(NDC):
                        pslot = PS[4] if m_ < 4 else PS[5]
                        col = slice(m_ * SESS_T + hoff + tg * 128,
                                    m_ * SESS_T + hoff + (tg + 1) * 128)
                        ve(lambda v, m_=m_, pslot=pslot, col=col:
                           v.tensor_add(
                               out_sb[:, m_ * 128:(m_ + 1) * 128],
                               pslot[:, (m_ % 4) * 128:(m_ % 4 + 1) * 128],
                               x1T[:, col]))
                    fin("ve", "VE", 1)
                    wt("te", "VE")
                    wt("sy", "VE")
                    tcol0 = s * SESS_T + hoff + tg * 128
                    dma(bass.AP(out_d, tcol0,
                                [[TOK_CORE, 128], [128 * TOK_CORE, NDC], [1, 128]]),
                        bass.AP(out_sb, 0,
                                [[D, 128], [128, NDC], [1, 128]]), "ST")


        def load_x(phase2, s2):
            xsrc = xT_d if phase2 == "M" else xTp_d
            wt("sy", "VE")
            wt("sy", "SE")
            wt("sy", "GP")
            dma(xT[:], bass.AP(xsrc, s2 * SESS_T,
                               [[TOK_CORE, 128], [128 * TOK_CORE, NDC], [1, SESS_T]]))

        def session(phase, s, nxt):
            main = phase == "M"
            tok0 = s * SESS_T

            wt("sy", "VE")
            wt("sy", "GP")
            dma(cbuf[:], bass.AP(ropec_d, (0 if main else TOK_CORE) * 64 + tok0 * 64,
                                 [[64, 128], [128 * 64, NCHUNK], [1, 64]]))
            dma(ssb[:], bass.AP(ropes_d, (0 if main else TOK_CORE) * 128 + tok0 * 128,
                                [[128, 128], [128 * 128, NCHUNK], [1, 128]]))
            xld = cnt["LD"]
            wt("se", "LD", xld)
            wt("ve", "LD", xld)
            wt("gp", "LD", xld)
            ve_app = ln_pass(xT, xnT)
            if not main and nxt is not None:
                load_x(*nxt)
                nxt = None

            for d in range(NDC):
                blkc = slice(d * SESS_T, (d + 1) * SESS_T)
                wt("te", "VE", ve_app[d])
                mm(PS[2][0:H, :], wb[:, d * H:(d + 1) * H], xnT[:, blkc],
                   start=(d == 0), stop=(d == NDC - 1))
                mm(PS[3][0:H, :], wa[:, d * H:(d + 1) * H], xnT[:, blkc],
                   start=(d == 0), stop=(d == NDC - 1))
            fin("te", "TE", 1)
            for _w in range(2):
                mm(PS[4][:], ones_bf[:], xbf[:, 0:512])
            fin("te", "TE", 1)
            wt("se", "TE")
            se(lambda s_: s_.activation(beta[:], PS[2][0:H, :], AF.Sigmoid, bias=bb[:]))
            se(lambda s_: s_.activation(alpha[:], PS[3][0:H, :], AF.Sigmoid, bias=ba[:]))
            fin("se", "SE", 1)
            wt("ve", "SE")
            ve(lambda v: v.tensor_scalar(beta[:], beta[:], BETA_SPAN, BETA_MIN,
                                         ALU.mult, ALU.add))
            for c in range(NCHUNK):
                cbk = slice(c * CH, (c + 1) * CH)
                ve(lambda v, cbk=cbk: v.tensor_tensor_scan(
                    Pc[:, cbk], beta[:, cbk], zero8[:], 1.0, ALU.mult, ALU.add))
            for c in range(NCHUNK):
                cbk = slice(c * CH, (c + 1) * CH)
                ve(lambda v, c=c, cbk=cbk: v.tensor_scalar(
                    pendR[:, cbk], ones8[:],
                    Pc[:, c * CH + CH - 1:c * CH + CH], None, ALU.mult))
            fin("ve", "VE", 1)
            wt("te", "VE")
            for c in range(NCHUNK):
                cbk = slice(c * CH, (c + 1) * CH)
                tr(PS[4][:, c * 32 + 0:c * 32 + 8], Pc[0:H, cbk], idf[0:8, 0:8])
                tr(PS[4][:, c * 32 + 8:c * 32 + 16], alpha[0:H, cbk], idf[0:8, 0:8])
                tr(PS[4][:, c * 32 + 16:c * 32 + 24], pendR[0:H, cbk], idf[0:8, 0:8])
            fin("te", "TE", 1)
            wt("ve", "TE")
            ve(lambda v: v.tensor_copy(scaleT[:], PS[4][:, 0:128]))
            fin("ve", "VE", 1)
            ve_scaleT = cnt["VE"]
            ve(lambda v: v.reciprocal(
                rcpT[:], bass.AP(scaleT, 0, [[NCHUNK * 32, 128], [32, NCHUNK], [1, 8]])))
            for c in range(NCHUNK):
                b0 = c * 32
                ve(lambda v, c=c, b0=b0: v.tensor_mul(
                    aoPT[:, c * 8:(c + 1) * 8], scaleT[:, b0 + 8:b0 + 16],
                    rcpT[:, c * 8:(c + 1) * 8]))
            fin("ve", "VE", 1)
            ve_scaleT2 = cnt["VE"]
            if main and s == 0:
                if "scale" in tap_d:
                    wt("sy", "VE")
                    dma(tap_d["scale"].ap(), scaleT[:], "ST")
                if "xn" in tap_d:
                    wt("sy", "VE")
                    dma(tap_d["xn"].ap(), xnT[:], "ST")

            g = {"ve_scaleT": ve_scaleT, "ve_scaleT2": ve_scaleT2}
            for c in range(NCHUNK):
                g = chunk(phase, s, c, g)

            if main:
                wt("te", "SE", g["se_ps45"])
                for m in range(NDC):
                    pwo = PS[0] if m % 2 == 0 else PS[1]
                    if m <= 1:
                        wt("te", "VE", g["ve_ps01"])
                    for hd in range(NDC):
                        mm(pwo[:], wo[:, hd * D + m * 128: hd * D + (m + 1) * 128],
                           OT[:, hd * SESS_T:(hd + 1) * SESS_T],
                           start=(hd == 0), stop=(hd == NDC - 1))
                    fin("te", "TE", 1)
                    wt("ve", "TE")
                    ve(lambda v, m=m, pwo=pwo: v.scalar_tensor_tensor(
                        x1T[:, m * SESS_T:(m + 1) * SESS_T], pwo[:],
                        bo[:, m:m + 1], xT[:, m * SESS_T:(m + 1) * SESS_T],
                        ALU.add, ALU.add))
                    fin("ve", "VE", 1)
                    if m >= 1:
                        wt("te", "VE", cnt["VE"] - 1)
                if s == 0 and "x1" in tap_d:
                    wt("sy", "VE")
                    dma(tap_d["x1"].ap(), x1T[:], "ST")
                if nxt is not None:
                    load_x(*nxt)
                    nxt = None
                wt("te", "VE")
                ve_app2 = ln_pass(x1T, x1n)
                if s == 0 and "x1n" in tap_d:
                    wt("sy", "VE")
                    dma(tap_d["x1n"].ap(), x1n[:], "ST")
                if s == 0 and "ot" in tap_d:
                    wt("sy", "SE")
                    dma(tap_d["ot"].ap(), OT[:], "ST")
                ffn(s, ve_app2)
            return g

        seq = [("A", s) for s in range(n_sess_pa)] + \
              [("M", s) for s in range(n_sess_main)]
        if seq:
            load_x(*seq[0])
        snext = {i: (seq[i + 1] if i + 1 < len(seq) else None)
                 for i in range(len(seq))}
        for i, (ph, s) in enumerate(seq):
            if ph == "A":
                session("A", s, snext[i])
            if ph == "A" and (i + 1 == len(seq) or seq[i + 1][0] == "M"):
                if "sa" in tap_d:
                    wt("sy", "VE")
                    dma(tap_d["sa"].ap(), S_sb[:], "ST")
                    wt("ve", "ST")
                ve(lambda v: v.tensor_scalar(S_sb[:], S_sb[:], flag[:], None, ALU.mult))
                fin("ve", "VE", 1)
                wt("se", "VE")
                se(lambda s_: s_.copy(Sbf[:], S_sb[:]))
                fin("se", "SE", 1)
            if ph == "M":
                g = session("M", s, snext[i])
                if s == 0 and "s0" in tap_d:
                    wt("sy", "VE")
                    dma(tap_d["s0"].ap(), S_sb[:], "ST")

        wt("sy", "ST")
        return cnt, wmax

    results = {}
    for a in ("sy", "gp", "te", "ve", "se"):
        dec = {"sy": blk.sync, "gp": blk.gpsimd, "te": blk.tensor,
               "ve": blk.vector, "se": blk.scalar}[a]

        def f(_eng, a=a):
            results[a] = sched(a)
        dec(f)

    base = results["sy"][0]
    for a in ("gp", "te", "ve", "se"):
        assert results[a][0] == base, (a, results[a][0], base)
    for semn in base:
        assert results["sy"][1][semn] <= base[semn], (semn,)

    ctx.close()
    return nc, tap_d



def _rope_tables():
    import jax
    import jax.numpy as jnp
    half = 64
    with jax.default_device(jax.devices("cpu")[0]):
        inv = 1.0 / 10000.0 ** (jnp.arange(half, dtype=jnp.float32) / half)
        freqs = jnp.arange(L, dtype=jnp.float32)[:, None] * inv[None, :]
        cos = np.asarray(jax.device_get(jnp.cos(freqs)), dtype=np.float32)
        sin = np.asarray(jax.device_get(jnp.sin(freqs)), dtype=np.float32)
    ss = np.empty((L, 128), dtype=np.float32)
    ss[:, 0::2] = -sin
    ss[:, 1::2] = sin
    return cos, ss


def make_in_map(inputs, core):
    b, halfsel = core // 2, core % 2
    x = np.asarray(inputs["L0_attn_x"])
    own = np.ascontiguousarray(x[b, halfsel * TOK_CORE:(halfsel + 1) * TOK_CORE].T)
    par = np.ascontiguousarray(x[b, (1 - halfsel) * TOK_CORE:(2 - halfsel) * TOK_CORE].T)
    cos, ss = _rope_tables()
    own_off = halfsel * TOK_CORE
    par_off = (1 - halfsel) * TOK_CORE
    ropeC = np.ascontiguousarray(np.concatenate(
        [cos[own_off:own_off + TOK_CORE], cos[par_off:par_off + TOK_CORE]], axis=0))
    ropeS = np.ascontiguousarray(np.concatenate(
        [ss[own_off:own_off + TOK_CORE], ss[par_off:par_off + TOK_CORE]], axis=0))

    rm = np.asarray(inputs["L0_ffn_rail_maker"])
    rup = np.asarray(inputs["L0_ffn_rail_up_proj"])
    railA = np.zeros((128, 64 * 32), dtype=np.float32)
    railB = np.zeros((128, 64 * 128), dtype=np.float32)
    for h in range(4):
        for p in range(32):
            strip = 32 * (p % 4)
            for rl in range(2):
                r3 = h * 16 + (p // 4) * 2 + rl
                blkA = np.zeros((16, 32), np.float32)
                blkA[:, rl * 16:(rl + 1) * 16] = rm[h, 2 * p + rl].T
                blkB = np.zeros((16, 128), np.float32)
                blkB[:, rl * 64:(rl + 1) * 64] = rup[h, 2 * p + rl].T
                railA[strip:strip + 16, r3 * 32:(r3 + 1) * 32] = blkA
                railB[strip:strip + 16, r3 * 128:(r3 + 1) * 128] = blkB

    maskT = np.triu(np.ones((128, 128), dtype=np.float32))

    return {
        "xT": own, "xTp": par,
        "wqT": np.ascontiguousarray(inputs["L0_attn_Wq"].T),
        "wkT": np.ascontiguousarray(inputs["L0_attn_Wk"].T),
        "wvT": np.ascontiguousarray(inputs["L0_attn_Wv"].T),
        "woT": np.ascontiguousarray(inputs["L0_attn_Wo"].T),
        "wbT": np.ascontiguousarray(inputs["L0_attn_Wbeta"].T),
        "waT": np.ascontiguousarray(inputs["L0_attn_Walpha"].T),
        "bb": np.ascontiguousarray(np.asarray(inputs["L0_attn_bbeta"]).reshape(H, 1)),
        "ba": np.ascontiguousarray(np.asarray(inputs["L0_attn_balpha"]).reshape(H, 1)),
        "bo_cols": np.ascontiguousarray(
            np.asarray(inputs["L0_attn_bo"]).reshape(NDC, 128).T),
        "wcT_tiles": np.ascontiguousarray(
            np.asarray(inputs["L0_ffn_w_cproj"]).T.reshape(NKC, 128, D)),
        "ropeC": ropeC, "ropeS": ropeS,
        "railA": railA, "railB": railB,
        "maskT": maskT,
        "idf": np.eye(128, dtype=np.float32),
        "flag": np.full((128, 1), float(halfsel), dtype=np.float32),
        "epsc": np.full((128, 1), np.float32(EPS), dtype=np.float32),
    }


def kernel(**inputs):
    nc, _ = build_nc(8, 8, taps=())
    in_maps = [make_in_map(inputs, c) for c in range(8)]
    res = run_bass_kernel_spmd(nc, in_maps, core_ids=list(range(8)))
    out = np.empty((4, L, D), dtype=np.float32)
    for c in range(8):
        b, halfsel = c // 2, c % 2
        out[b, halfsel * TOK_CORE:(halfsel + 1) * TOK_CORE] = res.results[c]["outT"].T
    return out


# version 09
# speedup vs baseline: 1.8627x; 1.0374x over previous
import sys
import numpy as np
from contextlib import ExitStack

sys.path.insert(0, "/opt/trn_rl_repo")
import concourse.bass as bass
import concourse.mybir as mybir
from concourse.bass_utils import run_bass_kernel_spmd

try:
    import jax
    jax.config.update("jax_compilation_cache_dir", "/tmp/jax_cc_cache")
    jax.config.update("jax_persistent_cache_min_compile_time_secs", 10.0)
except Exception:
    pass

F32 = mybir.dt.float32
BF16 = mybir.dt.bfloat16
AF = mybir.ActivationFunctionType
ALU = mybir.AluOpType

D = 1024
H = 8
CH = 128
SESS_T = 512
NCHUNK = 4
TOK_CORE = 4096
L = 8192
NDC = 8
EPS = 1e-5
BETA_MIN = 0.85
BETA_SPAN = 0.999 - 0.85
NKC = 128


def build_nc(n_sess_main=8, n_sess_pa=8, taps=()):
    nc = bass.Bass("TRN2", target_bir_lowering=False)

    d_in = {}
    def di(name, shape):
        d_in[name] = nc.dram_tensor(name, list(shape), F32, kind="ExternalInput")
        return d_in[name]

    xT_d = di("xT", (D, TOK_CORE))
    xTp_d = di("xTp", (D, TOK_CORE))
    di("wqT", (D, D)); di("wkT", (D, D)); di("wvT", (D, D)); di("woT", (D, D))
    di("wbT", (D, H)); di("waT", (D, H))
    bb_d = di("bb", (H, 1)); ba_d = di("ba", (H, 1))
    bo_d = di("bo_cols", (128, NDC))
    wc_d = di("wcT_tiles", (NKC, 128, D))
    ropec_d = di("ropeC", (L, 64))
    ropes_d = di("ropeS", (L, 128))
    railA_d = di("railA", (128, 64 * 32))
    railB_d = di("railB", (128, 64 * 128))
    maskT_d = di("maskT", (128, 128))
    idf_d = di("idf", (128, 128))
    flag_d = di("flag", (128, 1))
    epsc_d = di("epsc", (128, 1))

    out_d = nc.dram_tensor("outT", [D, TOK_CORE], F32, kind="ExternalOutput")
    wcbf_d = nc.dram_tensor("wc_bf", [NKC, 128, D], BF16, kind="Internal")

    tap_d = {}
    def mktap(name, shape, dtype=F32):
        if name in taps:
            tap_d[name] = nc.dram_tensor("tap_" + name, list(shape), dtype,
                                         kind="ExternalOutput")
    mktap("xn", (128, NDC * SESS_T), BF16)
    mktap("qt", (128, H * 128), BF16)
    mktap("kt", (128, H * 128), BF16)
    mktap("vt", (128, H * 128), BF16)
    mktap("at", (128, H * 128), BF16)
    mktap("scale", (128, 128))
    mktap("s0", (128, H * 128))
    mktap("ot", (128, H * SESS_T), BF16)
    mktap("x1", (128, NDC * SESS_T))
    mktap("x1n", (128, NDC * SESS_T), BF16)
    mktap("hc", (128, 256), BF16)
    mktap("sa", (128, H * 128))

    ctx = ExitStack()
    E = ctx.enter_context
    def S(name, shape, dtype):
        return E(nc.sbuf_tensor(name, list(shape), dtype))

    wq = S("wq", (128, NDC * D), BF16)
    wk = S("wk", (128, NDC * D), BF16)
    wv = S("wv", (128, NDC * D), BF16)
    wo = S("wo", (128, NDC * D), BF16)
    wb = S("wb", (128, NDC * H), BF16)
    wa = S("wa", (128, NDC * H), BF16)
    bb = S("bb_s", (H, 1), F32)
    ba = S("ba_s", (H, 1), F32)
    bo = S("bo_s", (128, NDC), F32)
    maskT = S("maskT_s", (128, 128), F32)
    idf = S("idf_s", (128, 128), F32)
    idb = S("idb", (128, 128), BF16)
    flag = S("flag_s", (128, 1), F32)
    epsc = S("epsc_s", (128, 1), F32)
    ones_bf = S("ones_bf", (128, 128), BF16)
    ones8 = S("ones8", (H, CH), F32)
    zero8 = S("zero8", (H, CH), F32)
    wblk = S("wblk", (128, 4096), BF16)

    xT = S("xT_s", (128, NDC * SESS_T), F32)
    xbf = S("xbf", (128, NDC * SESS_T), BF16)
    sq = S("sq", (128, 2 * SESS_T), BF16)
    mB = S("mB", (128, SESS_T), F32)
    msqB = S("msqB", (128, SESS_T), F32)
    rstdB = S("rstdB", (128, SESS_T), F32)
    rstdb16 = S("rstdb16", (128, SESS_T), BF16)
    tmpB = S("tmpB", (128, SESS_T), F32)
    xnT = S("xnT", (128, NDC * SESS_T), BF16)
    beta = S("beta", (H, SESS_T), F32)
    alpha = S("alpha", (H, SESS_T), F32)
    Pc = S("Pc", (H, SESS_T), F32)
    pendR = S("pendR", (H, SESS_T), F32)
    scaleT = S("scaleT", (128, NCHUNK * 32), F32)
    aoPT = S("aoPT", (128, NCHUNK * 8), F32)
    rcpT = S("rcpT", (128, NCHUNK * 8), F32)
    cbuf = S("cbuf", (128, NCHUNK * 64), F32)
    ssb = S("ssb", (128, NCHUNK * 128), F32)
    qbf = S("qbf", (128, D), BF16)
    kbf = S("kbf", (128, D), BF16)
    kt1 = S("kt1", (128, D), BF16)
    kt2 = S("kt2", (128, D), BF16)
    qt1 = kt1
    qt2 = kt2
    qtok = S("qtok", (128, D), BF16)
    khat = S("khat", (128, D), BF16)
    vt_sb = S("vt_sb", (128, D), BF16)
    qT = S("qT", (128, D), BF16)
    kT = S("kT", (128, D), BF16)
    A_sb = S("A_sb", (128, H * 128), BF16)
    OT = S("OT", (128, H * SESS_T), BF16)
    S_sb = S("S_sb", (128, H * 128), F32)
    Sbf = S("Sbf", (128, H * 128), BF16)
    x1T = S("x1T", (128, NDC * SESS_T), F32)
    x1n = S("x1n", (128, NDC * SESS_T), BF16)
    hbuf = S("hbuf", (128, 8 * 256), BF16)
    wcb = S("wcb", (128, 4 * D), BF16)
    ffn_sb = S("ffn_sb", (128, D), F32)
    out_sb = ffn_sb

    PS = [E(nc.psum_tensor(f"PS{i}", (128, 512), F32)) for i in range(6)]
    PB6 = E(nc.psum_tensor("PB6", (128, 1024), BF16))
    PB7 = E(nc.psum_tensor("PB7", (128, 1024), BF16))

    sem_names = ["LD", "WC", "TE", "VE", "SE", "GP", "ST"]
    SEM = {n: E(nc.semaphore("s" + n)) for n in sem_names}

    blk = E(nc.Block())

    def sched(A):
        cnt = {n: 0 for n in sem_names}
        wmax = {n: 0 for n in sem_names}
        last = [None]
        eng = {"sy": nc.sync, "gp": nc.gpsimd, "te": nc.tensor,
               "ve": nc.vector, "se": nc.scalar}

        def fin(e, semn, k=1):
            cnt[semn] += k
            if A == e:
                assert last[0] is not None, (e, semn)
                last[0].then_inc(SEM[semn], k)

        lastw = {}

        def wt(e, semn, val=None):
            v = cnt[semn] if val is None else val
            wmax[semn] = max(wmax[semn], v)
            if A == e and v > 0 and v > lastw.get(semn, 0):
                eng[e].wait_ge(SEM[semn], v)
                lastw[semn] = v

        def on(e):
            return A == e

        def put(i):
            last[0] = i
            return i

        def dma(dst, src, semn="LD"):
            if on("sy"):
                put(nc.sync.dma_start(dst, src))
            fin("sy", semn, 16)

        def gdma(dst, src, semn="LD"):
            if on("gp"):
                put(nc.gpsimd.dma_start(dst, src))
            fin("gp", semn, 16)

        def mm(out, lhsT, rhs, start=True, stop=True, tile_position=None):
            if on("te"):
                kw = {}
                if tile_position is not None:
                    kw["tile_position"] = tile_position
                put(nc.tensor.matmul(out, lhsT, rhs, start=start, stop=stop, **kw))

        def tr(out, in_, ident):
            if on("te"):
                put(nc.tensor.transpose(out, in_, ident))

        def ve(f):
            if on("ve"):
                put(f(nc.vector))

        def gp(f):
            if on("gp"):
                put(f(nc.gpsimd))

        def se(f):
            if on("se"):
                put(f(nc.scalar))

        def ap_c(c):
            return bass.AP(cbuf, c * 64, [[NCHUNK * 64, 128], [0, H], [1, 64], [0, 2]])

        def ap_s(c):
            return bass.AP(ssb, c * 128, [[NCHUNK * 128, 128], [0, H], [1, 128]])

        def ap_swap(t):
            return bass.AP(t, 1, [[1024, 128], [128, H], [2, 64], [-1, 2]])

        for name, dst in (("wqT", wq), ("wkT", wk), ("wvT", wv), ("woT", wo)):
            for d in range(NDC):
                gdma(dst[:, d * D:(d + 1) * D],
                     d_in[name].ap()[d * 128:(d + 1) * 128, :])
        for name, dst in (("wbT", wb), ("waT", wa)):
            for d in range(NDC):
                gdma(dst[:, d * H:(d + 1) * H],
                     d_in[name].ap()[d * 128:(d + 1) * 128, :])
        dma(bb[:], bb_d.ap()); dma(ba[:], ba_d.ap()); dma(bo[:], bo_d.ap())
        dma(maskT[:], maskT_d.ap()); dma(idf[:], idf_d.ap())
        dma(flag[:], flag_d.ap()); dma(epsc[:], epsc_d.ap())
        gdma(idb[:], idf_d.ap())
        rav = xT.ap().bitcast(BF16)
        rbv = x1T.ap().bitcast(BF16)
        gdma(rav[:, 0:2048], railA_d.ap())
        gdma(rbv[:, 0:8192], railB_d.ap())
        for i in range(16):
            gdma(wcbf_d.ap()[i * 8:(i + 1) * 8], wc_d.ap()[i * 8:(i + 1) * 8], "WC")
        if on("gp"):
            put(nc.gpsimd.memset(ones8[:], 1.0))
            put(nc.gpsimd.memset(zero8[:], 0.0))
            put(nc.gpsimd.memset(S_sb[:], 0.0))
            put(nc.gpsimd.memset(Sbf[:], 0.0))
        fin("gp", "GP", 1)
        if on("ve"):
            put(nc.vector.memset(ones_bf[:], 1.0))
        fin("ve", "VE", 1)

        wt("te", "GP")
        wt("te", "LD")
        for e2 in range(128):
            h_, p_ = divmod(e2, 32)
            strip = 32 * (p_ % 4)
            fb = (h_ * 8 + p_ // 4) * 128
            pst = PS[e2 % 2]
            for rl in range(2):
                r3 = h_ * 16 + (p_ // 4) * 2 + rl
                mm(pst[strip:strip + 32, 0:128],
                   rav[strip:strip + 16, r3 * 32:(r3 + 1) * 32],
                   rbv[strip:strip + 16, r3 * 128:(r3 + 1) * 128],
                   start=(rl == 0), stop=(rl == 1),
                   tile_position=(strip, strip))
            fin("te", "TE", 1)
            wt("se", "TE")
            se(lambda s_, strip=strip, fb=fb, pst=pst:
               s_.copy(wblk[strip:strip + 32, fb:fb + 128],
                       pst[strip:strip + 32, 0:128]))
            fin("se", "SE", 1)
            if e2 >= 1:
                wt("te", "SE", cnt["SE"] - 1)

        def ln_pass(src, dst_bf, src_bf=None):
            wt("se", "VE")
            wt("ve", "SE")
            wt("gp", "VE")
            wt("gp", "SE")
            te_ln = {}
            for d in range(NDC):
                pp = (d % 2) * SESS_T
                blkc = slice(d * SESS_T, (d + 1) * SESS_T)
                if d >= 2:
                    wt("ve", "TE", te_ln[d - 2])
                if src_bf is None:
                    if d >= 2:
                        wt("se", "TE", te_ln[d - 2])
                        wt("gp", "TE", te_ln[d - 2])
                    if d % 2 == 0:
                        se(lambda s_, pp=pp, blkc=blkc:
                           s_.copy(xbf[:, pp:pp + SESS_T], src[:, blkc]))
                        fin("se", "SE", 1)
                    else:
                        gp(lambda g_, pp=pp, blkc=blkc:
                           g_.tensor_copy(xbf[:, pp:pp + SESS_T], src[:, blkc]))
                        fin("gp", "GP", 1)
                    ve(lambda v, pp=pp, blkc=blkc:
                       v.tensor_tensor(sq[:, pp:pp + SESS_T], src[:, blkc],
                                       src[:, blkc], ALU.mult))
                    fin("ve", "VE", 1)
                    xsrc_mm = xbf[:, pp:pp + SESS_T]
                else:
                    ve(lambda v, pp=pp, blkc=blkc:
                       v.tensor_tensor(sq[:, pp:pp + SESS_T], src_bf[:, blkc],
                                       src_bf[:, blkc], ALU.mult))
                    fin("ve", "VE", 1)
                    xsrc_mm = src_bf[:, blkc]
                wt("te", "SE")
                wt("te", "GP")
                wt("te", "VE")
                mm(PS[0][:], ones_bf[:], xsrc_mm,
                   start=(d == 0), stop=(d == NDC - 1))
                mm(PS[1][:], ones_bf[:], sq[:, pp:pp + SESS_T],
                   start=(d == 0), stop=(d == NDC - 1))
                fin("te", "TE", 1)
                te_ln[d] = cnt["TE"]
            for _w in range(3):
                mm(PS[2][:, 0:128], ones_bf[:], ones_bf[:, 0:128])
            fin("te", "TE", 1)
            wt("ve", "TE")
            ve(lambda v: v.tensor_scalar(mB[:], PS[0][:], 1.0 / D, None, ALU.mult))
            ve(lambda v: v.tensor_mul(msqB[:], mB[:], mB[:]))
            ve(lambda v: v.scalar_tensor_tensor(msqB[:], PS[1][:], 1.0 / D, msqB[:],
                                                ALU.mult, ALU.subtract))
            fin("ve", "VE", 1)
            wt("se", "VE")
            se(lambda s_: s_.activation(tmpB[:], msqB[:], AF.Sqrt, bias=epsc[:]))
            fin("se", "SE", 1)
            wt("ve", "SE")
            ve(lambda v: v.reciprocal(rstdB[:], tmpB[:]))
            fin("ve", "VE", 1)
            wt("se", "VE")
            se(lambda s_: s_.copy(rstdb16[:], rstdB[:]))
            fin("se", "SE", 1)
            wt("ve", "SE")
            ve_app = {}
            for d in range(NDC):
                blkc = slice(d * SESS_T, (d + 1) * SESS_T)
                ve(lambda v, blkc=blkc: v.scalar_tensor_tensor(
                    dst_bf[:, blkc], src[:, blkc], 1.0, mB[:],
                    ALU.mult, ALU.subtract))
                ve(lambda v, blkc=blkc: v.tensor_mul(dst_bf[:, blkc],
                                                     dst_bf[:, blkc], rstdb16[:]))
                fin("ve", "VE", 1)
                ve_app[d] = cnt["VE"]
            return ve_app

        def chunk(phase, s, c, g):
            main = phase == "M"
            sc0 = c * 32
            xoff = c * CH

            def proj(wsb, p0, p1):
                for half2 in range(2):
                    pdst = p0 if half2 == 0 else p1
                    for d in range(NDC):
                        lhs = xnT[:, d * SESS_T + xoff: d * SESS_T + xoff + CH]
                        mm(pdst[:], lhs,
                           wsb[:, d * D + half2 * 512: d * D + half2 * 512 + 512],
                           start=(d == 0), stop=(d == NDC - 1))

            if main:
                wt("te", "VE", g.get("ve_ps01", 0))
                wt("te", "VE", g.get("ve_scaleT", 0))
                proj(wq, PS[0], PS[1])
                fin("te", "TE", 1)
            te_q = cnt["TE"]
            wt("te", "VE", g.get("ve_ps23", 0))
            wt("te", "VE", g.get("ve_scaleT", 0))
            proj(wk, PS[2], PS[3])
            fin("te", "TE", 1)
            te_k = cnt["TE"]
            wt("te", "SE", g.get("se_ps45", 0))
            proj(wv, PS[4], PS[5])
            fin("te", "TE", 1)
            te_v = cnt["TE"]

            if main:
                wt("se", "TE", te_q)
                se(lambda s_: s_.copy(qbf[:, 0:512], PS[0][:]))
                se(lambda s_: s_.copy(qbf[:, 512:1024], PS[1][:]))
                fin("se", "SE", 1)
            se_qbf = cnt["SE"]
            wt("se", "TE", te_k)
            se(lambda s_: s_.copy(kbf[:, 0:512], PS[2][:]))
            se(lambda s_: s_.copy(kbf[:, 512:1024], PS[3][:]))
            fin("se", "SE", 1)
            se_kbf = cnt["SE"]

            ve_qrope = cnt["VE"]
            if main:
                wt("ve", "SE", se_qbf)
                ve(lambda v: v.tensor_mul(qt1[:], qbf[:], ap_c(c)))
                ve(lambda v: v.tensor_mul(qt2[:], ap_swap(qbf), ap_s(c)))
                ve(lambda v: v.tensor_add(qbf[:], qt1[:], qt2[:]))
                fin("ve", "VE", 1)
                ve_qrope = cnt["VE"]
            wt("ve", "SE", se_kbf)
            ve(lambda v: v.tensor_mul(kt1[:], kbf[:], ap_c(c)))
            ve(lambda v: v.tensor_mul(kt2[:], ap_swap(kbf), ap_s(c)))
            ve(lambda v: v.tensor_add(kbf[:], kt1[:], kt2[:]))
            fin("ve", "VE", 1)
            ve_krope = cnt["VE"]
            for h in range(H):
                ve(lambda v, h=h: v.tensor_scalar(
                    khat[:, h * 128:(h + 1) * 128], kbf[:, h * 128:(h + 1) * 128],
                    scaleT[:, sc0 + 16 + h: sc0 + 16 + h + 1], None, ALU.mult))
            fin("ve", "VE", 1)
            ve_khat = cnt["VE"]

            se_qtok = cnt["SE"]
            if main:
                wt("se", "VE", ve_qrope)
                for h in range(H):
                    se(lambda s_, h=h: s_.activation(
                        qtok[:, h * 128:(h + 1) * 128], qbf[:, h * 128:(h + 1) * 128],
                        AF.Copy, scale=scaleT[:, sc0 + h: sc0 + h + 1]))
                fin("se", "SE", 1)
                se_qtok = cnt["SE"]

            wt("se", "VE", g.get("ve_scaleT2", 0))
            wt("se", "TE", te_v)
            for h in range(H):
                psrc = PS[4] if h < 4 else PS[5]
                se(lambda s_, h=h, psrc=psrc: s_.activation(
                    vt_sb[:, h * 128:(h + 1) * 128],
                    psrc[:, (h % 4) * 128:(h % 4 + 1) * 128],
                    AF.Copy, scale=aoPT[:, c * 8 + h: c * 8 + h + 1]))
            fin("se", "SE", 1)
            se_vt = cnt["SE"]

            if not main:
                wt("te", "VE", ve_khat)
                wt("te", "SE", se_vt)
                te_u = {}
                for h in range(H):
                    pu = PS[2 + h // 4]
                    qtr = (h % 4) * 128
                    mm(pu[:, qtr:qtr + 128], khat[:, h * 128:(h + 1) * 128],
                       vt_sb[:, h * 128:(h + 1) * 128])
                    fin("te", "TE", 1)
                    te_u[h] = cnt["TE"]
                for h in range(H):
                    pu = PS[2 + h // 4]
                    qtr = (h % 4) * 128
                    wt("ve", "TE", te_u[3 if h < 4 else 7])
                    ve(lambda v, h=h, pu=pu, qtr=qtr: v.scalar_tensor_tensor(
                        S_sb[:, h * 128:(h + 1) * 128], S_sb[:, h * 128:(h + 1) * 128],
                        scaleT[:, sc0 + 16 + h: sc0 + 16 + h + 1],
                        pu[:, qtr:qtr + 128], ALU.mult, ALU.add))
                    fin("ve", "VE", 1)
                return {"ve_ps23": cnt["VE"], "se_ps45": se_vt,
                        "ve_scaleT": g.get("ve_scaleT", 0),
                        "ve_scaleT2": g.get("ve_scaleT2", 0), "ve_ps01": 0}

            wt("te", "SE", se_qtok)
            wt("te", "VE", ve_krope)
            wt("te", "SE", g.get("se_qkT", 0))
            wt("te", "VE", g.get("ve_qkT", 0))
            for h in range(H):
                tr(PB6[:, h * 128:(h + 1) * 128],
                   qtok[:, h * 128:(h + 1) * 128], idb[:])
            for h in range(H):
                tr(PB7[:, h * 128:(h + 1) * 128],
                   kbf[:, h * 128:(h + 1) * 128], idb[:])
            fin("te", "TE", 1)
            wt("se", "TE")
            se(lambda s_: s_.copy(qT[:], PB6[:]))
            fin("se", "SE", 1)
            se_qkT = cnt["SE"]
            wt("ve", "TE")
            ve(lambda v: v.tensor_copy(kT[:], PB7[:]))
            fin("ve", "VE", 1)
            ve_qkT = cnt["VE"]

            te_sc = {}
            te_u = {}
            te_o = {}
            ve_A = {}
            wt("te", "VE", ve_khat)
            wt("te", "SE", se_vt)
            for h in range(H):
                pu = PS[2 + h // 4]
                qtr = (h % 4) * 128
                mm(pu[:, qtr:qtr + 128], khat[:, h * 128:(h + 1) * 128],
                   vt_sb[:, h * 128:(h + 1) * 128])
                fin("te", "TE", 1)
                te_u[h] = cnt["TE"]
            wt("te", "SE", se_qkT)
            wt("te", "VE", ve_qkT)
            for h in range(H):
                psc = PS[h // 4]
                qtr = (h % 4) * 128
                mm(psc[:, qtr:qtr + 128], kT[:, h * 128:(h + 1) * 128],
                   qT[:, h * 128:(h + 1) * 128])
                fin("te", "TE", 1)
                te_sc[h] = cnt["TE"]
            for h in range(H):
                psc = PS[h // 4]
                qtr = (h % 4) * 128
                wt("ve", "TE", te_sc[3 if h < 4 else 7])
                ve(lambda v, h=h, psc=psc, qtr=qtr: v.tensor_mul(
                    A_sb[:, h * 128:(h + 1) * 128], psc[:, qtr:qtr + 128], maskT[:]))
                fin("ve", "VE", 1)
                ve_A[h] = cnt["VE"]
            for h in range(H):
                po = PS[4 + h // 4]
                qtr = (h % 4) * 128
                wt("te", "VE", ve_A[h])
                mm(po[:, qtr:qtr + 128], vt_sb[:, h * 128:(h + 1) * 128],
                   A_sb[:, h * 128:(h + 1) * 128], start=True, stop=False)
                mm(po[:, qtr:qtr + 128], Sbf[:, h * 128:(h + 1) * 128],
                   qT[:, h * 128:(h + 1) * 128], start=False, stop=True)
                fin("te", "TE", 1)
                te_o[h] = cnt["TE"]
            for h in range(H):
                pu = PS[2 + h // 4]
                qtr = (h % 4) * 128
                wt("ve", "TE", te_u[3 if h < 4 else 7])
                ve(lambda v, h=h, pu=pu, qtr=qtr: v.scalar_tensor_tensor(
                    S_sb[:, h * 128:(h + 1) * 128], S_sb[:, h * 128:(h + 1) * 128],
                    scaleT[:, sc0 + 16 + h: sc0 + 16 + h + 1],
                    pu[:, qtr:qtr + 128], ALU.mult, ALU.add))
                fin("ve", "VE", 1)
            ve_chain = cnt["VE"]
            for h in range(H):
                po = PS[4 + h // 4]
                qtr = (h % 4) * 128
                wt("se", "TE", te_o[3 if h < 4 else 7])
                se(lambda s_, h=h, po=po, qtr=qtr: s_.copy(
                    OT[:, h * SESS_T + c * CH: h * SESS_T + (c + 1) * CH],
                    po[:, qtr:qtr + 128]))
                fin("se", "SE", 1)
            g2 = {"ve_ps23": ve_chain, "ve_ps01": ve_A[H - 1],
                  "se_ps45": cnt["SE"], "se_qkT": se_qkT, "ve_qkT": ve_qkT,
                  "ve_scaleT": g.get("ve_scaleT", 0),
                  "ve_scaleT2": g.get("ve_scaleT2", 0)}
            wt("se", "VE", ve_chain)
            se(lambda s_: s_.copy(Sbf[:], S_sb[:]))
            fin("se", "SE", 1)

            if s == 0 and c == 0:
                for nm, buf in (("qt", qT), ("kt", kT), ("vt", vt_sb), ("at", A_sb)):
                    if nm in tap_d:
                        wt("sy", "SE")
                        wt("sy", "VE")
                        dma(tap_d[nm].ap(), buf[:], "ST")
            return g2


        def ffn(s, ve_app2):
            pb6f = PB6.ap().bitcast(F32)
            for half in range(2):
                hoff = half * 256
                se_gelu = {}

                def rails(ci):
                    cgl = ci
                    h_, p_ = divmod(cgl, 32)
                    strip = 32 * (p_ % 4)
                    fb = (h_ * 8 + p_ // 4) * 128
                    rhs = x1n[strip:strip + 32,
                              (p_ // 4) * SESS_T + hoff:
                              (p_ // 4) * SESS_T + hoff + 256]
                    wt("te", "VE", ve_app2[p_ // 4])
                    if ci >= 1:
                        wt("te", "SE", se_gelu[ci - 1])
                    mm(pb6f[:, 0:256],
                       wblk[strip:strip + 32, fb:fb + 128], rhs,
                       tile_position=(strip, 0))
                    fin("te", "TE", 1)
                    wt("se", "TE")
                    hs = ci % 8
                    se(lambda s_, hs=hs: s_.activation(
                        hbuf[:, hs * 256:(hs + 1) * 256], pb6f[:, 0:256],
                        AF.Gelu))
                    fin("se", "SE", 1)
                    se_gelu[ci] = cnt["SE"]

                def cproj(ci, wcld, slot, j):
                    wt("te", "SE", se_gelu[ci])
                    wt("te", "WC", wcld)
                    hs = ci % 8
                    for tg in range(2):
                        ph0 = PS[0] if tg == 0 else PS[2]
                        ph1 = PS[1] if tg == 0 else PS[3]
                        lhs = hbuf[:, hs * 256 + tg * 128:
                                   hs * 256 + (tg + 1) * 128]
                        w0 = wcb[:, (slot * 2 + j) * D:
                                 (slot * 2 + j) * D + 512]
                        w1 = wcb[:, (slot * 2 + j) * D + 512:
                                 (slot * 2 + j + 1) * D]
                        mm(ph0[:], lhs, w0, start=(ci == 0), stop=(ci == NKC - 1))
                        mm(ph1[:], lhs, w1, start=(ci == 0), stop=(ci == NKC - 1))
                    fin("te", "TE", 1)

                pipe = []
                wcl = {}
                te_duo = {}

                def wc_load(duo):
                    slot = duo % 2
                    wt("sy", "WC", 256)
                    if duo >= 2:
                        wt("sy", "TE", te_duo[duo - 2])
                    for j in range(2):
                        cgl = duo * 2 + j
                        dma(wcb[:, (slot * 2 + j) * D:(slot * 2 + j + 1) * D],
                            wcbf_d.ap()[cgl], "WC")
                    wcl[duo] = cnt["WC"]

                wc_load(0)
                wc_load(1)
                for duo in range(64):
                    slot = duo % 2
                    for j in range(2):
                        ci = duo * 2 + j
                        rails(ci)
                        if len(pipe) >= 2:
                            cproj(*pipe.pop(0))
                        pipe.append((ci, wcl[duo], slot, j))
                        if s == 0 and half == 0 and ci == 0 and "hc" in tap_d:
                            wt("sy", "SE")
                            dma(tap_d["hc"].ap(), hbuf[:, 0:256], "ST")
                    te_duo[duo] = cnt["TE"]
                    if duo + 2 < 64:
                        wc_load(duo + 2)
                while pipe:
                    cproj(*pipe.pop(0))
                for tg in range(2):
                    pe0 = PS[0] if tg == 0 else PS[2]
                    pe1 = PS[1] if tg == 0 else PS[3]
                    wt("ve", "TE")
                    wt("ve", "ST")
                    ve(lambda v, pe0=pe0: v.tensor_copy(ffn_sb[:, 0:512], pe0[:]))
                    ve(lambda v, pe1=pe1: v.tensor_copy(ffn_sb[:, 512:1024], pe1[:]))
                    fin("ve", "VE", 1)
                    wt("te", "VE")
                    wt("te", "SE")
                    for m_ in range(NDC):
                        pslot = PS[4] if m_ < 4 else PS[5]
                        tr(pslot[:, (m_ % 4) * 128:(m_ % 4 + 1) * 128],
                           ffn_sb[:, m_ * 128:(m_ + 1) * 128],
                           idf[:])
                    fin("te", "TE", 1)
                    wt("ve", "TE")
                    for m_ in range(NDC):
                        pslot = PS[4] if m_ < 4 else PS[5]
                        col = slice(m_ * SESS_T + hoff + tg * 128,
                                    m_ * SESS_T + hoff + (tg + 1) * 128)
                        ve(lambda v, m_=m_, pslot=pslot, col=col:
                           v.tensor_add(
                               out_sb[:, m_ * 128:(m_ + 1) * 128],
                               pslot[:, (m_ % 4) * 128:(m_ % 4 + 1) * 128],
                               x1T[:, col]))
                    fin("ve", "VE", 1)
                    wt("te", "VE")
                    wt("sy", "VE")
                    tcol0 = s * SESS_T + hoff + tg * 128
                    dma(bass.AP(out_d, tcol0,
                                [[TOK_CORE, 128], [128 * TOK_CORE, NDC], [1, 128]]),
                        bass.AP(out_sb, 0,
                                [[D, 128], [128, NDC], [1, 128]]), "ST")


        def load_x(phase2, s2):
            xsrc = xT_d if phase2 == "M" else xTp_d
            wt("sy", "VE")
            wt("sy", "SE")
            wt("sy", "GP")
            wt("gp", "VE")
            wt("gp", "TE")
            ap = bass.AP(xsrc, s2 * SESS_T,
                         [[TOK_CORE, 128], [128 * TOK_CORE, NDC], [1, SESS_T]])
            dma(xT[:], ap)
            gdma(xbf[:], ap)

        def session(phase, s, nxt):
            main = phase == "M"
            tok0 = s * SESS_T

            wt("sy", "VE")
            wt("sy", "GP")
            dma(cbuf[:], bass.AP(ropec_d, (0 if main else TOK_CORE) * 64 + tok0 * 64,
                                 [[64, 128], [128 * 64, NCHUNK], [1, 64]]))
            dma(ssb[:], bass.AP(ropes_d, (0 if main else TOK_CORE) * 128 + tok0 * 128,
                                [[128, 128], [128 * 128, NCHUNK], [1, 128]]))
            xld = cnt["LD"]
            wt("se", "LD", xld)
            wt("ve", "LD", xld)
            wt("gp", "LD", xld)
            ve_app = ln_pass(xT, xnT, src_bf=xbf)
            if not main and nxt is not None:
                load_x(*nxt)
                nxt = None

            for d in range(NDC):
                blkc = slice(d * SESS_T, (d + 1) * SESS_T)
                wt("te", "VE", ve_app[d])
                mm(PS[2][0:H, :], wb[:, d * H:(d + 1) * H], xnT[:, blkc],
                   start=(d == 0), stop=(d == NDC - 1))
                mm(PS[3][0:H, :], wa[:, d * H:(d + 1) * H], xnT[:, blkc],
                   start=(d == 0), stop=(d == NDC - 1))
            fin("te", "TE", 1)
            for _w in range(2):
                mm(PS[4][:], ones_bf[:], xbf[:, 0:512])
            fin("te", "TE", 1)
            wt("se", "TE")
            se(lambda s_: s_.activation(beta[:], PS[2][0:H, :], AF.Sigmoid, bias=bb[:]))
            se(lambda s_: s_.activation(alpha[:], PS[3][0:H, :], AF.Sigmoid, bias=ba[:]))
            fin("se", "SE", 1)
            wt("ve", "SE")
            ve(lambda v: v.tensor_scalar(beta[:], beta[:], BETA_SPAN, BETA_MIN,
                                         ALU.mult, ALU.add))
            for c in range(NCHUNK):
                cbk = slice(c * CH, (c + 1) * CH)
                ve(lambda v, cbk=cbk: v.tensor_tensor_scan(
                    Pc[:, cbk], beta[:, cbk], zero8[:], 1.0, ALU.mult, ALU.add))
            for c in range(NCHUNK):
                cbk = slice(c * CH, (c + 1) * CH)
                ve(lambda v, c=c, cbk=cbk: v.tensor_scalar(
                    pendR[:, cbk], ones8[:],
                    Pc[:, c * CH + CH - 1:c * CH + CH], None, ALU.mult))
            fin("ve", "VE", 1)
            wt("te", "VE")
            for c in range(NCHUNK):
                cbk = slice(c * CH, (c + 1) * CH)
                tr(PS[4][:, c * 32 + 0:c * 32 + 8], Pc[0:H, cbk], idf[0:8, 0:8])
                tr(PS[4][:, c * 32 + 8:c * 32 + 16], alpha[0:H, cbk], idf[0:8, 0:8])
                tr(PS[4][:, c * 32 + 16:c * 32 + 24], pendR[0:H, cbk], idf[0:8, 0:8])
            fin("te", "TE", 1)
            wt("ve", "TE")
            ve(lambda v: v.tensor_copy(scaleT[:], PS[4][:, 0:128]))
            fin("ve", "VE", 1)
            ve_scaleT = cnt["VE"]
            ve(lambda v: v.reciprocal(
                rcpT[:], bass.AP(scaleT, 0, [[NCHUNK * 32, 128], [32, NCHUNK], [1, 8]])))
            for c in range(NCHUNK):
                b0 = c * 32
                ve(lambda v, c=c, b0=b0: v.tensor_mul(
                    aoPT[:, c * 8:(c + 1) * 8], scaleT[:, b0 + 8:b0 + 16],
                    rcpT[:, c * 8:(c + 1) * 8]))
            fin("ve", "VE", 1)
            ve_scaleT2 = cnt["VE"]
            if main and s == 0:
                if "scale" in tap_d:
                    wt("sy", "VE")
                    dma(tap_d["scale"].ap(), scaleT[:], "ST")
                if "xn" in tap_d:
                    wt("sy", "VE")
                    dma(tap_d["xn"].ap(), xnT[:], "ST")

            g = {"ve_scaleT": ve_scaleT, "ve_scaleT2": ve_scaleT2}
            for c in range(NCHUNK):
                g = chunk(phase, s, c, g)

            if main:
                wt("te", "SE", g["se_ps45"])
                ve_x1m = {}
                for m in range(NDC):
                    pwo = PS[0] if m % 2 == 0 else PS[1]
                    if m <= 1:
                        wt("te", "VE", g["ve_ps01"])
                    if m >= 2:
                        wt("te", "VE", ve_x1m[m - 2])
                    for hd in range(NDC):
                        mm(pwo[:], wo[:, hd * D + m * 128: hd * D + (m + 1) * 128],
                           OT[:, hd * SESS_T:(hd + 1) * SESS_T],
                           start=(hd == 0), stop=(hd == NDC - 1))
                    fin("te", "TE", 1)
                    wt("ve", "TE")
                    ve(lambda v, m=m, pwo=pwo: v.scalar_tensor_tensor(
                        x1T[:, m * SESS_T:(m + 1) * SESS_T], pwo[:],
                        bo[:, m:m + 1], xT[:, m * SESS_T:(m + 1) * SESS_T],
                        ALU.add, ALU.add))
                    fin("ve", "VE", 1)
                    ve_x1m[m] = cnt["VE"]
                if s == 0 and "x1" in tap_d:
                    wt("sy", "VE")
                    dma(tap_d["x1"].ap(), x1T[:], "ST")
                wt("te", "VE")
                ve_app2 = ln_pass(x1T, x1n)
                if nxt is not None:
                    load_x(*nxt)
                    nxt = None
                if s == 0 and "x1n" in tap_d:
                    wt("sy", "VE")
                    dma(tap_d["x1n"].ap(), x1n[:], "ST")
                if s == 0 and "ot" in tap_d:
                    wt("sy", "SE")
                    dma(tap_d["ot"].ap(), OT[:], "ST")
                ffn(s, ve_app2)
            return g

        seq = [("A", s) for s in range(n_sess_pa)] + \
              [("M", s) for s in range(n_sess_main)]
        if seq:
            load_x(*seq[0])
        snext = {i: (seq[i + 1] if i + 1 < len(seq) else None)
                 for i in range(len(seq))}
        for i, (ph, s) in enumerate(seq):
            if ph == "A":
                session("A", s, snext[i])
            if ph == "A" and (i + 1 == len(seq) or seq[i + 1][0] == "M"):
                if "sa" in tap_d:
                    wt("sy", "VE")
                    dma(tap_d["sa"].ap(), S_sb[:], "ST")
                    wt("ve", "ST")
                ve(lambda v: v.tensor_scalar(S_sb[:], S_sb[:], flag[:], None, ALU.mult))
                fin("ve", "VE", 1)
                wt("se", "VE")
                se(lambda s_: s_.copy(Sbf[:], S_sb[:]))
                fin("se", "SE", 1)
            if ph == "M":
                g = session("M", s, snext[i])
                if s == 0 and "s0" in tap_d:
                    wt("sy", "VE")
                    dma(tap_d["s0"].ap(), S_sb[:], "ST")

        wt("sy", "ST")
        return cnt, wmax

    results = {}
    for a in ("sy", "gp", "te", "ve", "se"):
        dec = {"sy": blk.sync, "gp": blk.gpsimd, "te": blk.tensor,
               "ve": blk.vector, "se": blk.scalar}[a]

        def f(_eng, a=a):
            results[a] = sched(a)
        dec(f)

    base = results["sy"][0]
    for a in ("gp", "te", "ve", "se"):
        assert results[a][0] == base, (a, results[a][0], base)
    for semn in base:
        assert results["sy"][1][semn] <= base[semn], (semn,)

    ctx.close()
    return nc, tap_d



def _rope_tables():
    import jax
    import jax.numpy as jnp
    half = 64
    with jax.default_device(jax.devices("cpu")[0]):
        inv = 1.0 / 10000.0 ** (jnp.arange(half, dtype=jnp.float32) / half)
        freqs = jnp.arange(L, dtype=jnp.float32)[:, None] * inv[None, :]
        cos = np.asarray(jax.device_get(jnp.cos(freqs)), dtype=np.float32)
        sin = np.asarray(jax.device_get(jnp.sin(freqs)), dtype=np.float32)
    ss = np.empty((L, 128), dtype=np.float32)
    ss[:, 0::2] = -sin
    ss[:, 1::2] = sin
    return cos, ss


def make_in_map(inputs, core):
    b, halfsel = core // 2, core % 2
    x = np.asarray(inputs["L0_attn_x"])
    own = np.ascontiguousarray(x[b, halfsel * TOK_CORE:(halfsel + 1) * TOK_CORE].T)
    par = np.ascontiguousarray(x[b, (1 - halfsel) * TOK_CORE:(2 - halfsel) * TOK_CORE].T)
    cos, ss = _rope_tables()
    own_off = halfsel * TOK_CORE
    par_off = (1 - halfsel) * TOK_CORE
    ropeC = np.ascontiguousarray(np.concatenate(
        [cos[own_off:own_off + TOK_CORE], cos[par_off:par_off + TOK_CORE]], axis=0))
    ropeS = np.ascontiguousarray(np.concatenate(
        [ss[own_off:own_off + TOK_CORE], ss[par_off:par_off + TOK_CORE]], axis=0))

    rm = np.asarray(inputs["L0_ffn_rail_maker"])
    rup = np.asarray(inputs["L0_ffn_rail_up_proj"])
    railA = np.zeros((128, 64 * 32), dtype=np.float32)
    railB = np.zeros((128, 64 * 128), dtype=np.float32)
    for h in range(4):
        for p in range(32):
            strip = 32 * (p % 4)
            for rl in range(2):
                r3 = h * 16 + (p // 4) * 2 + rl
                blkA = np.zeros((16, 32), np.float32)
                blkA[:, rl * 16:(rl + 1) * 16] = rm[h, 2 * p + rl].T
                blkB = np.zeros((16, 128), np.float32)
                blkB[:, rl * 64:(rl + 1) * 64] = rup[h, 2 * p + rl].T
                railA[strip:strip + 16, r3 * 32:(r3 + 1) * 32] = blkA
                railB[strip:strip + 16, r3 * 128:(r3 + 1) * 128] = blkB

    maskT = np.triu(np.ones((128, 128), dtype=np.float32))

    return {
        "xT": own, "xTp": par,
        "wqT": np.ascontiguousarray(inputs["L0_attn_Wq"].T),
        "wkT": np.ascontiguousarray(inputs["L0_attn_Wk"].T),
        "wvT": np.ascontiguousarray(inputs["L0_attn_Wv"].T),
        "woT": np.ascontiguousarray(inputs["L0_attn_Wo"].T),
        "wbT": np.ascontiguousarray(inputs["L0_attn_Wbeta"].T),
        "waT": np.ascontiguousarray(inputs["L0_attn_Walpha"].T),
        "bb": np.ascontiguousarray(np.asarray(inputs["L0_attn_bbeta"]).reshape(H, 1)),
        "ba": np.ascontiguousarray(np.asarray(inputs["L0_attn_balpha"]).reshape(H, 1)),
        "bo_cols": np.ascontiguousarray(
            np.asarray(inputs["L0_attn_bo"]).reshape(NDC, 128).T),
        "wcT_tiles": np.ascontiguousarray(
            np.asarray(inputs["L0_ffn_w_cproj"]).T.reshape(NKC, 128, D)),
        "ropeC": ropeC, "ropeS": ropeS,
        "railA": railA, "railB": railB,
        "maskT": maskT,
        "idf": np.eye(128, dtype=np.float32),
        "flag": np.full((128, 1), float(halfsel), dtype=np.float32),
        "epsc": np.full((128, 1), np.float32(EPS), dtype=np.float32),
    }


def kernel(**inputs):
    nc, _ = build_nc(8, 8, taps=())
    in_maps = [make_in_map(inputs, c) for c in range(8)]
    res = run_bass_kernel_spmd(nc, in_maps, core_ids=list(range(8)))
    out = np.empty((4, L, D), dtype=np.float32)
    for c in range(8):
        b, halfsel = c // 2, c % 2
        out[b, halfsel * TOK_CORE:(halfsel + 1) * TOK_CORE] = res.results[c]["outT"].T
    return out
